# version 06
# baseline (speedup 1.0000x reference)
import numpy as np
import ml_dtypes
import concourse.bass as bass
import concourse.bacc as bacc
import concourse.mybir as mybir
from concourse.bass_utils import run_bass_kernel_spmd

F32 = mybir.dt.float32
BF16 = mybir.dt.bfloat16
I32 = mybir.dt.int32
AF = mybir.ActivationFunctionType
ALU = mybir.AluOpType
AX = mybir.AxisListType
bfloat16 = ml_dtypes.bfloat16
ds = bass.ds

B, S, DM = 4, 4096, 1024
H, D, L, KSZ = 16, 64, 64, 65
H8, HD = 8, 512
NT = 32
TOK, NTO = 2048, 16
INNER = 4096
SEG = S // L
QSCALE = float(D) ** (-0.25)
MAGIC = float(np.float32(12582912.0))
EPS = 1e-5
NEGI = -3.0e38
ENGS = ("sync", "scalar", "tensor", "vector", "gpsimd")


def MM(e, *a, **kw):
    kw.setdefault("skip_group_check", True)
    return e.matmul(*a, **kw)


class Ctr:
    def __init__(self, sem):
        self.sem = sem
        self.n = 0

    def inc(self, k=1):
        self.n += k
        return self.n


class Prog:

    def __init__(self, eng_ctrs):
        self.steps = {e: [] for e in ENGS}
        self.eng_ctrs = eng_ctrs
        self.marks = {"scalar": None, "vector": None, "gpsimd": None}

    def _push(self, eng, fn, waits, ctr, amt, chain=True):
        waits = list(waits)
        if eng in self.marks and chain:
            if self.marks[eng] is not None:
                waits.append(self.marks[eng])
            if ctr is None:
                ctr, amt = self.eng_ctrs[eng], 1
        self.steps[eng].append((fn, waits, ctr, amt))
        v = ctr.inc(amt) if ctr is not None else None
        if eng in self.marks and chain and ctr is not None:
            self.marks[eng] = (ctr, v)
        return v

    def on(self, eng, fn, waits=()):
        self._push(eng, fn, waits, None, 0)

    def inc_on(self, eng, fn, waits, ctr, amt=1):
        return self._push(eng, fn, waits, ctr, amt)

    def nochain(self, eng, fn, waits, ctr, amt=1):
        return self._push(eng, fn, waits, ctr, amt, chain=False)

    def dma(self, eng, ctr, out, in_, waits=(), **kw):
        def fn(e, out=out, in_=in_, kw=kw):
            return e.dma_start(out, in_, **kw)
        return self._push(eng, fn, waits, ctr, 16, chain=False)

    def emit(self, nc):
        with nc.Block() as blk:
            for name in ENGS:
                steps = self.steps[name]
                if not steps:
                    continue

                def body(e, steps=steps):
                    for fn, waits, inc, amt in steps:
                        for (c, v) in waits:
                            if v > 0:
                                e.wait_ge(c.sem, v)
                        ins = fn(e)
                        if inc is not None:
                            ins.then_inc(inc.sem, amt)
                getattr(blk, name)(body)


def build_nc(debug=False, stop_after=99):
    nc = bacc.Bacc("TRN2")

    x_in = nc.dram_tensor("x_in", [S, DM], F32, kind="ExternalInput")
    x_own = nc.dram_tensor("x_own", [TOK, DM], F32, kind="ExternalInput")
    wq_in = nc.dram_tensor("wq_in", [DM, HD], F32, kind="ExternalInput")
    wk_in = nc.dram_tensor("wk_in", [DM, HD], F32, kind="ExternalInput")
    wv_in = nc.dram_tensor("wv_in", [DM, HD], F32, kind="ExternalInput")
    bq_in = nc.dram_tensor("bq_in", [128, 4], F32, kind="ExternalInput")
    bk_in = nc.dram_tensor("bk_in", [128, 4], F32, kind="ExternalInput")
    bv_in = nc.dram_tensor("bv_in", [1, HD], F32, kind="ExternalInput")
    convp_in = nc.dram_tensor("convp_in", [H8 * 3 * 128, 128], F32, kind="ExternalInput")
    wg_sl = nc.dram_tensor("wg_sl", [128, INNER], F32, kind="ExternalInput")
    wu_sl = nc.dram_tensor("wu_sl", [128, INNER], F32, kind="ExternalInput")
    wd_sl = nc.dram_tensor("wd_sl", [512, DM], F32, kind="ExternalInput")
    eye2_in = nc.dram_tensor("eye2_in", [128, 64], F32, kind="ExternalInput")
    offs_in = nc.dram_tensor("offs_in", [1, 8], I32, kind="ExternalInput")
    out_d = nc.dram_tensor("out_d", [TOK, DM], F32, kind="ExternalOutput")

    xln_d = nc.dram_tensor("xln_d", [S, DM], BF16, kind="Internal")
    x1_d = nc.dram_tensor("x1_d", [TOK, DM], F32, kind="Internal")
    xint_d = nc.dram_tensor("xint_d", [TOK, DM], BF16, kind="Internal")
    zint_d = nc.dram_tensor("zint_d", [TOK, INNER], BF16, kind="Internal")
    wsum_i = nc.dram_tensor("wsum_i", [1, 8], F32, kind="Internal")
    wsum_o = nc.dram_tensor("wsum_o", [1, 8], F32, kind="Internal")
    amax_i = nc.dram_tensor("amax_i", [1, 8], F32, kind="Internal")
    amax_o = nc.dram_tensor("amax_o", [1, 8], F32, kind="Internal")
    agg_i = nc.dram_tensor("agg_i", [128, INNER], BF16, kind="Internal")
    agg_o = nc.dram_tensor("agg_o", [DM, INNER], BF16, kind="Internal")
    agu_i = nc.dram_tensor("agu_i", [128, INNER], BF16, kind="Internal")
    agu_o = nc.dram_tensor("agu_o", [DM, INNER], BF16, kind="Internal")
    agd_i = nc.dram_tensor("agd_i", [512, DM], BF16, kind="Internal")
    agd_o = nc.dram_tensor("agd_o", [INNER, DM], BF16, kind="Internal")
    ctxme_d = nc.dram_tensor("ctxme_d", [S, HD], BF16, kind="Internal")
    ctxab_d = nc.dram_tensor("ctxab_d", [2 * S, HD], BF16, kind="Internal")

    dbg = {}
    if debug:
        for name, shape, dt_ in [
            ("dbg_qT", [128, 16384], BF16), ("dbg_kT", [128, 16384], BF16),
            ("dbg_v", [128, NT * 520], BF16), ("dbg_ql", [128, 256], BF16),
            ("dbg_kl", [128, 256], BF16), ("dbg_K2", [128, 256], F32),
            ("dbg_G", [128, 4 * 65], BF16), ("dbg_k3v", [128, 256], F32),
            ("dbg_cb", [128, 16], F32), ("dbg_srow", [1, 64], F32),
            ("dbg_stat", [128, 192], F32), ("dbg_ctxme", [S, HD], BF16),
            ("dbg_x1", [TOK, DM], F32), ("dbg_xint", [TOK, DM], BF16),
            ("dbg_zint", [TOK, INNER], BF16), ("dbg_wgT", [DM, INNER], BF16),
        ]:
            dbg[name] = nc.dram_tensor(name, shape, dt_, kind="ExternalOutput")

    XT = nc.alloc_sbuf_tensor("XT", [128, 32768], BF16)
    QT = nc.alloc_sbuf_tensor("QT", [128, 16384], BF16)
    KT = nc.alloc_sbuf_tensor("KT", [128, 16384], BF16)
    VT = nc.alloc_sbuf_tensor("VT", [128, 16640], BF16)
    WVB = nc.alloc_sbuf_tensor("WVB", [128, 4096], BF16)
    WQS = nc.alloc_sbuf_tensor("WQS", [128, 2048], BF16)
    WSL2 = nc.alloc_sbuf_tensor("WSL2", [128, 1024], BF16)
    XTI = nc.alloc_sbuf_tensor("XTI", [128, 4096], F32)
    EVAC = nc.alloc_sbuf_tensor("EVAC", [128, 1024], F32)
    AGPB = nc.alloc_sbuf_tensor("AGPB", [128, 512], BF16)
    qlT = nc.alloc_sbuf_tensor("qlT", [128, 256], BF16)
    klT = nc.alloc_sbuf_tensor("klT", [128, 256], BF16)
    K2sb = nc.alloc_sbuf_tensor("K2sb", [128, 256], F32)
    Zsb = nc.alloc_sbuf_tensor("Zsb", [64, 512], F32)
    LTsb = nc.alloc_sbuf_tensor("LTsb", [64, 512], F32)
    K3V = nc.alloc_sbuf_tensor("K3V", [64, 512], F32)
    Gsb = nc.alloc_sbuf_tensor("Gsb", [128, 4 * 65], BF16)
    EYE2 = nc.alloc_sbuf_tensor("EYE2", [128, 64], F32)
    ONEB = nc.alloc_sbuf_tensor("ONEB", [1, 128], BF16)
    ONEF = nc.alloc_sbuf_tensor("ONEF", [128, 1], F32)
    ONER = nc.alloc_sbuf_tensor("ONER", [1, 128], F32)
    BQS = nc.alloc_sbuf_tensor("BQS", [128, 4], F32)
    BKS = nc.alloc_sbuf_tensor("BKS", [128, 4], F32)
    BVB = nc.alloc_sbuf_tensor("BVB", [1, HD], BF16)
    CB = nc.alloc_sbuf_tensor("CB", [128, 16], F32)
    SROW = nc.alloc_sbuf_tensor("SROW", [1, 64], F32)
    STAT = nc.alloc_sbuf_tensor("STAT", [128, 192], F32)
    SMA = nc.alloc_sbuf_tensor("SMA", [128, 32], F32)
    SMB = nc.alloc_sbuf_tensor("SMB", [128, 64], F32)
    OFFS = nc.alloc_sbuf_tensor("OFFS", [1, 8], I32)
    CONSTS = nc.alloc_sbuf_tensor("CONSTS", [128, 2], F32)
    PB = [nc.alloc_psum_tensor(f"PB{i}", [128, 512], F32) for i in range(8)]

    XLB = [QT[:, 0:1024], QT[:, 1024:2048]]
    CONVP = XT[:, 16384:16384 + 3072]
    E1 = XT
    EYE128 = EVAC[:, 512:640]
    NCB = BVB

    sem_names = ["dX0", "dX1", "dX2", "dX3", "dS0", "dS1", "dM", "dT", "dW0", "dW1",
                 "dV", "dL", "dF0", "dF1", "dF2", "dE", "dP", "dWG", "dWU", "dSx",
                 "dSi", "dZ", "dT2", "dT3", "dV2", "dO", "dCX0", "dCX1", "dM2", "dDB",
                 "dN1", "dN2", "dN3", "dT3b", "dPb",
                 "tE", "vE", "aE", "gE", "cw", "cm", "cx"]
    guards = {n: nc.semaphore(n) for n in sem_names}
    sems = {n: Ctr(gd.__enter__()) for n, gd in guards.items()}
    dX = [sems["dX0"], sems["dX1"], sems["dX2"], sems["dX3"]]
    dS = [sems["dS0"], sems["dS1"]]
    dM, dT, dV, dL, dE, dP = (sems[k] for k in ("dM", "dT", "dV", "dL", "dE", "dP"))
    dW = [sems["dW0"], sems["dW1"]]
    dF3 = [sems["dF0"], sems["dF1"], sems["dF2"]]
    dWG, dWU, dSx, dSi, dZ = (sems[k] for k in ("dWG", "dWU", "dSx", "dSi", "dZ"))
    dT2, dT3, dV2, dO = (sems[k] for k in ("dT2", "dT3", "dV2", "dO"))
    dCX = [sems["dCX0"], sems["dCX1"]]
    dM2 = sems["dM2"]
    dDB = sems["dDB"]
    dN1, dN2, dN3 = sems["dN1"], sems["dN2"], sems["dN3"]
    dT3b = sems["dT3b"]
    dPb = sems["dPb"]
    dOs = None
    tE, vE, aE, gE = (sems[k] for k in ("tE", "vE", "aE", "gE"))
    cw, cm, cx = (sems[k] for k in ("cw", "cm", "cx"))

    (SS_SUM, SS_SSQ, SS_MAX, SS_MIN, SS_M, SS_RSTD, SS_AA, SS_BBM,
     SS_S2Q, SS_IG, SS_IU, SS_ID) = range(12)

    def st(jj, t=0, w=1):
        return STAT[:, jj * 16 + t: jj * 16 + t + w]

    p = Prog({"scalar": aE, "vector": vE, "gpsimd": gE})
    for dst, src in [(EYE2[:], eye2_in.ap()), (BQS[:], bq_in.ap()), (BKS[:], bk_in.ap()),
                     (BVB[:], bv_in.ap()), (OFFS[:], offs_in.ap())]:
        p.dma("gpsimd", dM, dst, src)
    dM_all = dM.n

    for _mfn in [lambda e: e.memset(ONEB[:], 1.0), lambda e: e.memset(ONEF[:], 1.0),
                 lambda e: e.memset(ONER[:], 1.0), lambda e: e.memset(Gsb[:], 1.0),
                 lambda e: e.memset(SMA[:], NEGI), lambda e: e.memset(CB[:], 0.0),
                 lambda e: e.memset(SROW[:], 0.0), lambda e: e.memset(STAT[:], 0.0),
                 lambda e: e.memset(CONSTS[:, 0:1], EPS),
                 lambda e: e.memset(CONSTS[:, 1:2], MAGIC)]:
        p.on("gpsimd", _mfn, [])
    ms_g = p.inc_on("gpsimd", lambda e: e.memset(SMB[:], 0.0), [], gE)

    nrm_done = [0] * NT
    store_done = [0] * NT
    for grp in range(8):
        xl = [0] * 4
        for t in range(4):
            ti = grp * 4 + t
            w = [(vE, nrm_done[ti - 4])] if grp > 0 else []
            xl[t] = p.dma("sync" if t % 2 == 0 else "scalar", dX[t],
                          XTI[:, t * 1024:(t + 1) * 1024],
                          x_in.ap()[ti * 128:(ti + 1) * 128, :], waits=w)
        for t in range(4):
            ti = grp * 4 + t
            wv = [(dX[t], xl[t])]
            if ti == 0:
                wv.append((gE, ms_g))
            p.inc_on("scalar", (lambda e, t=t: e.activation(
                EVAC[:], XTI[:, t * 1024:(t + 1) * 1024], AF.Square,
                accum_out=SMB[:, 8 + t:9 + t])), wv, aE)
            p.inc_on("vector", (lambda e, t=t: e.reduce_sum(
                SMB[:, 12 + t:13 + t], XTI[:, t * 1024:(t + 1) * 1024], axis=AX.X)), wv, vE)
        sq_a = aE.n
        p.inc_on("vector", lambda e: e.tensor_scalar(SMB[:, 16:20], SMB[:, 12:16], 1.0 / DM, None, ALU.mult),
                 [(aE, sq_a)], vE)
        p.inc_on("vector", lambda e: e.tensor_scalar(SMB[:, 20:24], SMB[:, 8:12], 1.0 / DM, None, ALU.mult),
                 [], vE)
        p.inc_on("vector", lambda e: e.scalar_tensor_tensor(SMB[:, 24:28], SMB[:, 16:20], 1.0,
                                                            SMB[:, 16:20], ALU.mult, ALU.mult), [], vE)
        var_v = p.inc_on("vector", lambda e: e.tensor_tensor(SMB[:, 24:28], SMB[:, 20:24], SMB[:, 24:28],
                                                             ALU.subtract), [], vE)
        sqrt_a = p.inc_on("scalar", lambda e: e.activation(SMB[:, 28:32], SMB[:, 24:28], AF.Sqrt, bias=CONSTS[:, 0:1]),
                          [(vE, var_v)], aE)
        p.inc_on("vector", lambda e: e.reciprocal(SMB[:, 24:28], SMB[:, 28:32]), [(aE, sqrt_a)], vE)
        for t in range(4):
            ti = grp * 4 + t
            buf = ti % 2
            w = [(dS[ti % 2], store_done[ti - 2])] if ti >= 2 else []
            nrm_done[ti] = p.inc_on("vector", (lambda e, t=t, buf=buf: e.tensor_scalar(
                XLB[buf], XTI[:, t * 1024:(t + 1) * 1024],
                SMB[:, 16 + t:17 + t], SMB[:, 24 + t:25 + t], ALU.subtract, ALU.mult)), w, vE)
            store_done[ti] = p.dma("gpsimd", dS[ti % 2], xln_d.ap()[ti * 128:(ti + 1) * 128, :],
                                   XLB[buf], waits=[(vE, nrm_done[ti])])
    p.emit(nc)
    if stop_after <= 0:
        for gd in guards.values():
            gd.__exit__(None, None, None)
        nc.finalize()
        return nc

    p = Prog({"scalar": aE, "vector": vE, "gpsimd": gE})
    ag_defer = []
    xst = [(dS[0], dS[0].n), (dS[1], dS[1].n)]
    for f in range(8):
        p.dma("sync", dT, XT[:, f * 4096:(f + 1) * 4096],
              xln_d.ap()[:, f * 128:(f + 1) * 128], transpose=True, waits=xst if f == 0 else [])
    xt_all = [(dT, dT.n)]

    wblocks = [("q", wq_in, m) for m in range(4)] + [("k", wk_in, m) for m in range(4)]
    wb_done = []
    for i, (_, win, m) in enumerate(wblocks):
        half = i % 2
        w = [(tE, (i - 1) * 8)] if i >= 2 else []
        v = p.dma("gpsimd", dW[half], WQS[:, half * 1024:(half + 1) * 1024].rearrange("p (a b) -> p a b", b=128),
                  win.ap()[:, m * 128:(m + 1) * 128].rearrange("(a p) b -> p a b", p=128), waits=w)
        wb_done.append((dW[half], v))
    wv_done = 0
    for kk in range(8):
        wv_done = p.dma("gpsimd", dV, WVB[:, kk * 512:(kk + 1) * 512],
                        wv_in.ap()[kk * 128:(kk + 1) * 128, :])

    p.inc_on("vector", lambda e: e.tensor_scalar(BQS[:], BQS[:], QSCALE, None, ALU.mult),
             [(dM, dM_all)], vE)
    bias_v = p.inc_on("vector", lambda e: e.tensor_scalar(BKS[:], BKS[:], QSCALE, None, ALU.mult), [], vE)

    qk_evac = []
    for i, (pj, _, m) in enumerate(wblocks):
        half = i % 2
        dstT = QT if pj == "q" else KT
        bias = BQS if pj == "q" else BKS
        for ntb in range(8):
            idx = i * 8 + ntb
            bank = PB[idx % 4]
            for kk in range(8):
                ww = []
                if kk == 0:
                    ww = [wb_done[i]] + xt_all
                    if idx >= 4:
                        ww.append((aE, qk_evac[idx - 4]))
                fn = (lambda e, bank=bank, half=half, kk=kk, ntb=ntb:
                      MM(e, bank[:], WQS[:, half * 1024 + kk * 128: half * 1024 + (kk + 1) * 128],
                               XT[:, kk * 4096 + ntb * 512: kk * 4096 + (ntb + 1) * 512],
                               start=(kk == 0), stop=(kk == 7)))
                if kk == 7:
                    mm_v = p.inc_on("tensor", fn, ww, tE)
                else:
                    p.on("tensor", fn, ww)
            ev = p.inc_on("scalar", (lambda e, bank=bank, dstT=dstT, bias=bias, m=m, ntb=ntb:
                                     e.activation(dstT[:, m * 4096 + ntb * 512: m * 4096 + (ntb + 1) * 512],
                                                  bank[:], AF.Identity, bias=bias[:, m:m + 1], scale=QSCALE)),
                          [(tE, mm_v), (vE, bias_v)], aE)
            qk_evac.append(ev)
    q_done_a = qk_evac[31]
    k_done_a = qk_evac[63]

    v_evac = []
    for nt in range(NT):
        bank = PB[4 + nt % 2]
        for kk in range(8):
            ww = []
            if kk == 0:
                ww = [(dV, wv_done)] + xt_all
                if nt >= 2:
                    ww.append((aE, v_evac[nt - 2]))
            p.on("tensor", (lambda e, bank=bank, kk=kk, nt=nt:
                            MM(e, bank[:], XT[:, kk * 4096 + nt * 128: kk * 4096 + nt * 128 + 128],
                                     WVB[:, kk * 512:(kk + 1) * 512],
                                     start=(kk == 0), stop=False)), ww)
        mm_v = p.inc_on("tensor", (lambda e, bank=bank:
                                   MM(e, bank[:], ONEB[:], BVB[:], start=False, stop=True)), [], tE)
        ev = p.inc_on("scalar", (lambda e, bank=bank, nt=nt: e.activation(
            VT[:, nt * 520:nt * 520 + 520].rearrange("p (a b) -> p a b", b=65)[:, :, 0:64],
            bank[:].rearrange("p (a b) -> p a b", b=64), AF.Copy)), [(tE, mm_v)], aE)
        v_evac.append(ev)
    vones_v = p.inc_on("vector", lambda e: e.memset(
        VT[:].rearrange("p (a b) -> p a b", b=65)[:, :, 64:65], 1.0), [(aE, v_evac[-1])], vE)

    pool_done_v = 0
    for nm, srcT, dstL, wgate in (("q", QT, qlT, q_done_a), ("k", KT, klT, k_done_a)):
        for chg in range(4):
            p.on("vector", (lambda e, srcT=srcT, chg=chg: e.tensor_reduce(
                SMB[:, 0:64], srcT[:, chg * 4096:(chg + 1) * 4096].rearrange("p (a b) -> p a b", b=64),
                axis=AX.X, op=ALU.add)), [(aE, wgate)] if chg == 0 else [])
            pool_done_v = p.inc_on("vector", (lambda e, dstL=dstL, chg=chg: e.tensor_scalar(
                dstL[:, chg * 64:(chg + 1) * 64], SMB[:, 0:64], 1.0 / SEG, None, ALU.mult)), [], vE)

    wpart_v = []
    gv_prev = 0
    for wi, wsl in ((0, wg_sl), (1, wu_sl), (2, wd_sl)):
        wld = [(gE, gv_prev)] if wi > 0 else []
        if wi < 2:
            v = p.dma("gpsimd", dL, XTI[:], wsl.ap(), waits=wld)
        else:
            v = p.dma("gpsimd", dL, XTI[:].rearrange("p (a b) -> p a b", b=1024),
                      wsl.ap().rearrange("(a p) b -> p a b", p=128), waits=wld)
        p.on("vector", (lambda e: e.tensor_reduce(SMB[:, 0:64], XTI[:].rearrange("p (a b) -> p a b", b=64),
                                        axis=AX.X, op=ALU.add, apply_absolute_value=True)),
             [(dL, v)])
        gv = p.inc_on("vector", (lambda e, wi=wi: e.tensor_reduce(
            SMA[:, 8 + wi:9 + wi], SMB[:, 0:64].rearrange("p (a b) -> p a b", b=64),
            axis=AX.X, op=ALU.add)), [], gE)
        gv_prev = gv
        tv = p.inc_on("tensor", (lambda e, wi=wi: MM(e, 
            PB[6][0:1, wi:wi + 1], ONEF[:], SMA[:, 8 + wi:9 + wi], start=True, stop=True)),
            [(gE, gv)], tE)
        wpart_v.append(p.inc_on("vector", (lambda e, wi=wi: e.tensor_copy(
            SROW[0:1, wi:wi + 1], PB[6][0:1, wi:wi + 1])), [(tE, tv)], vE))
    zero_v = p.inc_on("vector", lambda e: e.memset(SROW[0:1, 3:8], 0.0), [], vE)
    wsum_sent = p.dma("gpsimd", dE, wsum_i.ap(), SROW[0:1, 0:8],
                      waits=[(vE, wpart_v[-1]), (vE, zero_v)])
    ar_w = p.nochain("gpsimd", lambda e: e.collective_compute(
        "AllReduce", ALU.add, replica_groups=[list(range(8))],
        ins=[wsum_i.ap().opt()], outs=[wsum_o.ap().opt()]), [(dE, wsum_sent)], cw)
    rb_w = p.dma("gpsimd", dE, SROW[0:1, 8:16], wsum_o.ap(), waits=[(cw, ar_w)])

    p.on("vector", lambda e: e.tensor_scalar(SROW[0:1, 16:19], SROW[0:1, 8:11],
                                             1.0 / (INNER * DM), None, ALU.mult), [(dE, rb_w)])
    p.on("vector", lambda e: e.tensor_scalar(SROW[0:1, 16:19], SROW[0:1, 16:19],
                                             1e-5, None, ALU.max), [])
    p.on("vector", lambda e: e.memset(SROW[0:1, 19:20], 0.0), [])
    th_v = p.inc_on("vector", lambda e: e.tensor_scalar(SROW[0:1, 20:23], SROW[0:1, 16:19],
                                                        0.5, None, ALU.mult), [], vE)
    thb_t = p.inc_on("tensor", lambda e: MM(e, PB[6][:, 8:15], ONER[:], SROW[0:1, 16:23],
                                                  start=True, stop=True), [(vE, th_v)], tE)
    thb_v = p.inc_on("vector", lambda e: e.tensor_copy(SMA[:, 16:23], PB[6][:, 8:15]), [(tE, thb_t)], vE)
    thn_v = p.inc_on("vector", lambda e: e.tensor_scalar(SMA[:, 24:27], SMA[:, 20:23], -1.0,
                                                         None, ALU.mult), [(vE, thb_v)], vE)

    tv = 0
    prev_store = None
    for oi, (wi, wsl, agin, agout) in enumerate(
            [(2, wd_sl, agd_i, agd_o), (1, wu_sl, agu_i, agu_o), (0, wg_sl, agg_i, agg_o)]):
        if oi > 0:
            ww_ld = [(vE, tv)]
            if wi < 2:
                ld = p.dma("gpsimd", dL, XTI[:], wsl.ap(), waits=ww_ld)
            else:
                ld = p.dma("gpsimd", dL, XTI[:].rearrange("p (a b) -> p a b", b=1024),
                           wsl.ap().rearrange("(a p) b -> p a b", p=128), waits=ww_ld)
        else:
            ld = dL.n
        stv = 0
        for q4 in range(4):
            sl = XTI[:, q4 * 1024:(q4 + 1) * 1024]
            p.on("vector", (lambda e, sl=sl, wi=wi: e.tensor_scalar(
                WSL2[:], sl, SMA[:, 20 + wi:21 + wi], None, ALU.is_gt)),
                [(dL, ld), (vE, thn_v)] + ([(dF3[wi], stv)] if q4 > 0 else
                                           ([prev_store] if prev_store else [])))
            tv = p.inc_on("vector", (lambda e, sl=sl, wi=wi: e.scalar_tensor_tensor(
                WSL2[:], sl, SMA[:, 24 + wi:25 + wi], WSL2[:], ALU.is_lt, ALU.subtract)), [], vE)
            if wi < 2:
                stv = p.dma("gpsimd", dF3[wi], agin.ap()[:, q4 * 1024:(q4 + 1) * 1024], WSL2[:],
                            waits=[(vE, tv)])
            else:
                stv = p.dma("gpsimd", dF3[wi], agin.ap()[q4 * 128:(q4 + 1) * 128, :], WSL2[:],
                            waits=[(vE, tv)])
        prev_store = (dF3[wi], stv)
        ag_defer.append((agin, agout, dF3[wi], stv))
    p.emit(nc)
    if stop_after <= 1:
        for gd in guards.values():
            gd.__exit__(None, None, None)
        nc.finalize()
        return nc

    p = Prog({"scalar": aE, "vector": vE, "gpsimd": gE})

    def _fin():
        p.emit(nc)
        for gd in guards.values():
            gd.__exit__(None, None, None)
        nc.finalize()
        return nc
    convp_ld = p.dma("gpsimd", dM2, CONVP[:].rearrange("p (a b) -> p a b", b=128),
                     convp_in.ap().rearrange("(a p) b -> p a b", p=128))

    p.on("gpsimd", lambda e: e.memset(EVAC[:, 640:768], 1.0), [])
    eye_g = p.inc_on("gpsimd", lambda e: e.affine_select(EYE128, EVAC[:, 640:768], [[1, 128]],
                                                         ALU.is_equal, 0.0, base=0,
                                                         channel_multiplier=-1), [], gE)

    s2mm = []
    for h in range(H8):
        par, ch = h % 2, h // 2
        pr = slice(par * 64, par * 64 + 64)
        s2mm.append(p.inc_on("tensor", (lambda e, pr=pr, ch=ch, par=par: MM(e, 
            PB[0][pr, ch * 64:(ch + 1) * 64], qlT[pr, ch * 64:(ch + 1) * 64],
            klT[pr, ch * 64:(ch + 1) * 64], start=True, stop=True,
            tile_position=(par * 64, par * 64))), [], tE))
    k2_v = [0] * H8
    ex_l = [0] * H8
    for h in range(H8):
        par, ch = h % 2, h // 2
        pr = slice(par * 64, par * 64 + 64)
        p.on("vector", (lambda e, pr=pr, ch=ch, h=h: e.reduce_max(
            SMA[pr, h:h + 1], PB[0][pr, ch * 64:(ch + 1) * 64], axis=AX.X)),
            [(tE, s2mm[-1])] + ([(aE, ex_l[h - 2])] if h >= 2 else []))
        rm_v = p.inc_on("vector", (lambda e, pr=pr, h=h: e.tensor_scalar(
            SMB[pr, 32:33], SMA[pr, h:h + 1], -1.0, None, ALU.mult)), [], vE)
        ex_a = p.inc_on("scalar", (lambda e, pr=pr, ch=ch: e.activation(
            EVAC[pr, 0:64], PB[0][pr, ch * 64:(ch + 1) * 64], AF.Exp, bias=SMB[pr, 32:33],
            accum_out=SMB[pr, 33:34])),
            [(vE, rm_v)] + ([(vE, k2_v[h - 2])] if h >= 2 else []), aE)
        ex_l[h] = ex_a
        rc_v = p.inc_on("vector", (lambda e, pr=pr: e.reciprocal(SMB[pr, 34:35], SMB[pr, 33:34])),
                        [(aE, ex_a)], vE)
        k2_v[h] = p.inc_on("vector", (lambda e, pr=pr, ch=ch: e.tensor_scalar(
            K2sb[pr, ch * 64:(ch + 1) * 64], EVAC[pr, 0:64], SMB[pr, 34:35], None, ALU.mult)),
            [(vE, rc_v)], vE)
        p.inc_on("tensor", (lambda e, pr=pr, ch=ch, h=h, par=par: MM(e, 
            PB[1][0:1, h * 64:(h + 1) * 64], ONEF[pr, :], K2sb[pr, ch * 64:(ch + 1) * 64],
            start=True, stop=True, tile_position=(par * 64, 0))), [(vE, k2_v[h])], tE)
    k2_all_v = vE.n
    cs_all_t = tE.n
    if stop_after <= 2.1:
        return _fin()
    p.on("vector", lambda e: e.memset(SROW[0:1, 33:40], NEGI), [])
    am_v = p.inc_on("vector", lambda e: e.reduce_max(SROW[0:1, 32:33], PB[1][0:1, 0:512], axis=AX.X),
                    [(tE, cs_all_t)], vE)
    am_sent = p.dma("gpsimd", dE, amax_i.ap(), SROW[0:1, 32:40], waits=[(vE, am_v)])
    ar_m = p.nochain("gpsimd", lambda e: e.collective_compute(
        "AllReduce", ALU.max, replica_groups=[list(range(8))],
        ins=[amax_i.ap().opt()], outs=[amax_o.ap().opt()]), [(dE, am_sent)], cm)
    rb_m = p.dma("gpsimd", dE, SROW[0:1, 40:48], amax_o.ap(), waits=[(cm, ar_m)])
    for agin_, agout_, f3s_, stv_ in ag_defer:
        p.nochain("gpsimd", (lambda e, agin=agin_, agout=agout_: e.collective_compute(
            "AllGather", ALU.bypass, replica_groups=[list(range(8))],
            ins=[agin.ap().opt()], outs=[agout.ap().opt()])), [(f3s_, stv_)], cw)
    c_v = p.inc_on("vector", lambda e: e.reciprocal(SROW[0:1, 48:49], SROW[0:1, 40:41]),
                   [(dE, rb_m)], vE)
    cb_t = p.inc_on("tensor", lambda e: MM(e, PB[1][:, 12:13], ONER[:], SROW[0:1, 48:49],
                                                 start=True, stop=True), [(vE, c_v)], tE)
    cbc_v = p.inc_on("vector", lambda e: e.tensor_copy(CB[:, 12:13], PB[1][:, 12:13]),
                     [(tE, cb_t)], vE)
    if stop_after <= 2.2:
        return _fin()

    p.on("vector", lambda e: e.memset(EVAC[:, 768:832], NEGI), [(vE, k2_all_v)])
    cp_v = p.inc_on("vector", lambda e: e.tensor_copy(EVAC[:, 768:776], SMA[:, 0:8]), [], vE)
    ct_t = p.inc_on("tensor", lambda e: e.transpose(PB[2][0:64, 0:128], EVAC[:, 768:832], EYE128),
                    [(vE, cp_v), (gE, eye_g)], tE)
    cc_v = p.inc_on("vector", lambda e: e.reduce_max(SMB[0:64, 35:36], PB[2][0:64, 0:128], axis=AX.X),
                    [(tE, ct_t)], vE)

    p.on("vector", lambda e: e.memset(EVAC[0:64, 832:896], NEGI), [(vE, cc_v)])
    cp2_v = p.inc_on("vector", lambda e: e.tensor_copy(EVAC[0:64, 832:833], SMB[0:64, 35:36]), [], vE)
    ct2_t = p.inc_on("tensor", lambda e: e.transpose(PB[2][0:64, 128:192], EVAC[0:64, 832:896],
                                                     EYE2[0:64, :]), [(vE, cp2_v)], tE)
    ncrow_v = p.inc_on("vector", lambda e: e.tensor_scalar(SROW[0:1, 52:60], PB[2][0:1, 128:136],
                                                           -1.0, None, ALU.mult), [(tE, ct2_t)], vE)
    nb_t = p.inc_on("tensor", lambda e: MM(e, PB[2][:, 192:200], ONER[:], SROW[0:1, 52:60],
                                                 start=True, stop=True), [(vE, ncrow_v)], tE)
    nb_v = p.inc_on("vector", lambda e: e.tensor_copy(CB[:, 4:12], PB[2][:, 192:200]), [(tE, nb_t)], vE)

    for j in range(4):
        p.on("vector", (lambda e, j=j: e.tensor_copy(CB[0:64, j:j + 1],
                                                     CB[0:64, 4 + 2 * j:5 + 2 * j])),
             [(vE, nb_v)] if j == 0 else [])
        pairb_v = p.inc_on("vector", (lambda e, j=j: e.tensor_copy(
            CB[64:128, j:j + 1], CB[64:128, 5 + 2 * j:6 + 2 * j])), [], vE)

    for h in range(8):
        ncb_v = p.inc_on("vector", (lambda e, h=h: e.tensor_copy(
            NCB[0:1, h * 64:(h + 1) * 64], CB[0:1, 4 + h:5 + h].broadcast_to((1, 64)))),
            [(vE, nb_v)] if h == 0 else [], vE)

    if stop_after <= 2.3:
        return _fin()
    e1_evac = []
    for j in range(4):
        for ntb in range(8):
            idx = j * 8 + ntb
            bank = PB[0] if idx % 2 == 0 else PB[3]
            ww = [(vE, pairb_v)]
            if idx >= 2:
                ww.append((aE, e1_evac[idx - 2]))
            p.inc_on("tensor", (lambda e, bank=bank, j=j, ntb=ntb: MM(e, 
                bank[0:64, 0:512], klT[0:64, j * 64:(j + 1) * 64],
                QT[0:64, j * 4096 + ntb * 512: j * 4096 + (ntb + 1) * 512],
                start=True, stop=True, tile_position=(0, 0))), ww, tE)
            mm_v = p.inc_on("tensor", (lambda e, bank=bank, j=j, ntb=ntb: MM(e, 
                bank[64:128, 0:512], klT[64:128, j * 64:(j + 1) * 64],
                QT[64:128, j * 4096 + ntb * 512: j * 4096 + (ntb + 1) * 512],
                start=True, stop=True, tile_position=(64, 64))), [], tE)
            ev = p.inc_on("scalar", (lambda e, bank=bank, j=j, ntb=ntb: e.activation(
                E1[:, j * 4096 + ntb * 512: j * 4096 + (ntb + 1) * 512], bank[:, 0:512],
                AF.Exp, bias=CB[:, j:j + 1])), [(tE, mm_v)], aE)
            e1_evac.append(ev)
    e1_done_a = e1_evac[-1]

    if stop_after <= 2.4:
        return _fin()
    exp_done = [0, 0]
    k3_t = 0
    for nt in range(NT):
        sbank = PB[0] if nt % 2 == 0 else PB[3]
        ww0 = [(vE, ncb_v), (aE, e1_done_a)]
        if nt >= 2:
            ww0.append((aE, exp_done[nt % 2]))
        for h in range(H8):
            par, jj = h % 2, h // 2
            pr = slice(par * 64, par * 64 + 64)
            p.on("tensor", (lambda e, sbank=sbank, pr=pr, jj=jj, nt=nt, par=par, h=h: MM(e, 
                sbank[:, h * 64:(h + 1) * 64],
                KT[pr, jj * 4096 + nt * 128: jj * 4096 + nt * 128 + 128],
                qlT[pr, jj * 64:(jj + 1) * 64],
                start=True, stop=False, tile_position=(par * 64, 0))), ww0 if h == 0 else [])
            fn = (lambda e, sbank=sbank, h=h: MM(e, 
                sbank[:, h * 64:(h + 1) * 64], ONEB[:], NCB[0:1, h * 64:(h + 1) * 64],
                start=False, stop=True))
            smm = p.inc_on("tensor", fn, [], tE)
        ww_exp = [(tE, smm)]
        if nt >= 1:
            ww_exp.append((tE, k3_t))
        ev = p.inc_on("scalar", (lambda e, sbank=sbank: e.activation(
            AGPB[:], sbank[:, 0:512], AF.Exp)), ww_exp, aE)
        exp_done[nt % 2] = ev
        for h in range(H8):
            par, jj = h % 2, h // 2
            fn = (lambda e, h=h, jj=jj, nt=nt, par=par: MM(e, 
                PB[4 + jj][par * 64:par * 64 + 64, 0:65],
                AGPB[:, h * 64:(h + 1) * 64],
                VT[:, nt * 520 + 65 * h: nt * 520 + 65 * h + 65],
                start=(nt == 0), stop=(nt == NT - 1),
                tile_position=(0, par * 64)))
            ww = [(aE, ev)] if h == 0 else []
            if nt == 0 and h == 0:
                ww.append((vE, vones_v))
            k3_t = p.inc_on("tensor", fn, ww, tE)
    k3v_all_t = k3_t

    k3n_v = 0
    K3Vp = EVAC[:, 0:256]
    for jj in range(4):
        rc2 = p.inc_on("vector", (lambda e, jj=jj: e.reciprocal(SMB[:, 36 + jj:37 + jj],
                                                                PB[4 + jj][:, 64:65])),
                       [(tE, k3v_all_t)], vE)
        k3n_v = p.inc_on("vector", (lambda e, jj=jj: e.tensor_scalar(
            K3Vp[:, jj * 64:(jj + 1) * 64], PB[4 + jj][:, 0:64], SMB[:, 36 + jj:37 + jj],
            None, ALU.mult)), [(vE, rc2)], vE)
    k3mv = 0
    for h in range(H8):
        par, jj = h % 2, h // 2
        k3mv = p.dma("gpsimd", dN1, K3V[0:64, h * 64:(h + 1) * 64],
                     K3Vp[par * 64:par * 64 + 64, jj * 64:(jj + 1) * 64],
                     waits=[(vE, k3n_v)] if h == 0 else [])

    if stop_after <= 2.5:
        return _fin()
    K2all = EVAC[0:64, 256:768]
    K2Tall = EVAC[0:64, 256:768]
    NT1 = EVAC[0:64, 256:768]
    k2mv = 0
    for h in range(H8):
        par, ch = h % 2, h // 2
        k2mv = p.dma("gpsimd", dN2, K2all[:, h * 64:(h + 1) * 64],
                     K2sb[par * 64:par * 64 + 64, ch * 64:(ch + 1) * 64],
                     waits=[(vE, k2_all_v)] if h == 0 else [])
    z0_v = 0
    tK_l = []
    for h in range(H8):
        tk = p.inc_on("tensor", (lambda e, h=h: MM(e, PB[2][0:64, h * 64:(h + 1) * 64],
                                                   K2all[:, h * 64:(h + 1) * 64], EYE2[0:64, :],
                                                   start=True, stop=True)),
                      [(dN2, k2mv)] if h == 0 else [], tE)
        tK_l.append(tk)
    lt0_v = p.inc_on("vector", lambda e: e.tensor_scalar(LTsb[:], K2all[:], CB[0:64, 12:13],
                                                         None, ALU.mult),
                     [(vE, cbc_v), (dN2, k2mv)], vE)
    cpv = p.inc_on("vector", lambda e: e.tensor_copy(K2Tall[:], PB[2][0:64, 0:512]),
                   [(tE, tK_l[-1])], vE)
    zb_l = []
    for h in range(H8):
        zb = p.inc_on("tensor", (lambda e, h=h: MM(e, PB[1][0:64, h * 64:(h + 1) * 64],
                                                   K2Tall[:, h * 64:(h + 1) * 64],
                                                   K2Tall[:, h * 64:(h + 1) * 64],
                                                   start=True, stop=True)),
                      [(vE, cpv)] if h == 0 else [], tE)
        zb_l.append(zb)
    z0_v = p.inc_on("vector", lambda e: e.tensor_scalar(
        Zsb[:], PB[1][0:64, 0:512], CB[0:64, 12:13], None, ALU.mult),
        [(tE, zb_l[-1]), (vE, cbc_v)], vE)
    lt0_v = p.inc_on("vector", lambda e: e.tensor_scalar(LTsb[:], K2all[:], CB[0:64, 12:13],
                                                         None, ALU.mult),
                     [(vE, cbc_v), (dN2, k2mv)], vE)
    if stop_after <= 2.52:
        return _fin()

    def newton_sweep(CUR, first_wait, prev_read):
        for h in range(H8):
            ww = [first_wait] if h == 0 else []
            if h == 0 and prev_read:
                ww.append((vE, prev_read))
            u1t = p.inc_on("tensor", (lambda e, h=h, CUR=CUR: MM(e, 
                PB[0][0:64, h * 64:(h + 1) * 64], Zsb[:, h * 64:(h + 1) * 64],
                CUR[:, h * 64:(h + 1) * 64], start=True, stop=True)), ww, tE)
        t1v = p.inc_on("vector", (lambda e, CUR=CUR: e.scalar_tensor_tensor(
            NT1[:], CUR[:], 7.0, PB[0][0:64, 0:512], ALU.mult, ALU.subtract)),
            [(tE, u1t)], vE)
        for h in range(H8):
            u2t = p.inc_on("tensor", (lambda e, h=h: MM(e, 
                PB[3][0:64, h * 64:(h + 1) * 64], Zsb[:, h * 64:(h + 1) * 64],
                NT1[:, h * 64:(h + 1) * 64], start=True, stop=True)),
                [(vE, t1v)] if h == 0 else [], tE)
        t2v = p.inc_on("vector", (lambda e, CUR=CUR: e.scalar_tensor_tensor(
            NT1[:], CUR[:], 15.0, PB[3][0:64, 0:512], ALU.mult, ALU.subtract)),
            [(tE, u2t)], vE)
        for h in range(H8):
            u3t = p.inc_on("tensor", (lambda e, h=h: MM(e, 
                PB[0][0:64, h * 64:(h + 1) * 64], Zsb[:, h * 64:(h + 1) * 64],
                NT1[:, h * 64:(h + 1) * 64], start=True, stop=True)),
                [(vE, t2v)] if h == 0 else [], tE)
        rd = p.inc_on("vector", lambda e: e.tensor_scalar(NT1[:], PB[0][0:64, 0:512], 0.25,
                                                          None, ALU.mult), [(tE, u3t)], vE)
        last_v = p.inc_on("vector", (lambda e, CUR=CUR: e.scalar_tensor_tensor(
            CUR[:], CUR[:], 3.25, NT1[:], ALU.mult, ALU.subtract)), [], vE)
        return last_v, rd

    lt_v, z_v = lt0_v, z0_v
    prd = 0
    nits = 1 if stop_after <= 2.55 else 6
    for it in range(nits):
        lt_v, prd = newton_sweep(LTsb, (vE, max(lt_v, z_v)), prd)
        if it < 5:
            z_v, prd = newton_sweep(Zsb, (vE, lt_v), prd)
    if stop_after <= 2.55:
        return _fin()
    LT6 = LTsb

    if stop_after <= 2.6:
        return _fin()
    for h in range(H8):
        ww = [(vE, lt_v), (dN1, k3mv)] if h == 0 else []
        gt = p.inc_on("tensor", (lambda e, h=h: MM(e, 
            PB[1][0:64, h * 64:(h + 1) * 64], LT6[:, h * 64:(h + 1) * 64],
            K3V[:, h * 64:(h + 1) * 64], start=True, stop=True)), ww, tE)
    Gall = WSL2[0:64, 0:512]
    g_vv = p.inc_on("vector", lambda e: e.tensor_copy(Gall[:], PB[1][0:64, 0:512]),
                    [(tE, gt)], vE)
    g_v = 0
    for h in range(H8):
        par, j = h % 2, h // 2
        g_v = p.dma("gpsimd", dN3, Gsb[par * 64:par * 64 + 64, j * 65: j * 65 + 64],
                    Gall[:, h * 64:(h + 1) * 64], waits=[(vE, g_vv)] if h == 0 else [])
    g_v = (dN3, g_v)

    if debug:
        p.dma("gpsimd", dDB, dbg["dbg_qT"].ap(), QT[:], waits=[g_v])
        p.dma("gpsimd", dDB, dbg["dbg_kT"].ap(), KT[:])
        p.dma("gpsimd", dDB, dbg["dbg_v"].ap(), VT[:])
        p.dma("gpsimd", dDB, dbg["dbg_ql"].ap(), qlT[:])
        p.dma("gpsimd", dDB, dbg["dbg_kl"].ap(), klT[:])
        p.dma("gpsimd", dDB, dbg["dbg_K2"].ap(), K2sb[:])
        p.dma("gpsimd", dDB, dbg["dbg_G"].ap(), Gsb[:])
        p.dma("gpsimd", dDB, dbg["dbg_k3v"].ap(), K3V[:])
        p.dma("gpsimd", dDB, dbg["dbg_cb"].ap(), CB[:])
        p.dma("gpsimd", dDB, dbg["dbg_srow"].ap(), SROW[:])
    p.emit(nc)
    if stop_after <= 2:
        for gd in guards.values():
            gd.__exit__(None, None, None)
        nc.finalize()
        return nc

    g_last = g_v
    p = Prog({"scalar": aE, "vector": vE, "gpsimd": gE})
    ctx_done = []
    pay_st = [0] * NT
    rv_l = []
    AGPB2 = WSL2[:, 512:1024]
    for nt in range(NT):
        cbank = PB[(nt % 3) * 2]
        vbank = PB[(nt % 3) * 2 + 1]
        dbank = PB[6 + nt % 2]
        stg = AGPB if nt % 2 == 0 else AGPB2
        ww0 = [(dM2, convp_ld), g_last] if nt < 3 else [(vE, ctx_done[nt - 3])]
        first = True
        for h in range(H8):
            par, j = h % 2, h // 2
            cjs = [cj for cj in range(3)
                   if not ((nt == 0 and cj == 0) or (nt == NT - 1 and cj == 2))]
            for ci, cj in enumerate(cjs):
                stt = (ci == 0)
                lst = (ci == len(cjs) - 1)
                fnc = (lambda e, vbank=vbank, h=h, cj=cj, nt=nt, stt=stt, lst=lst: MM(e, 
                    vbank[:, h * 64:(h + 1) * 64],
                    CONVP[:, (h * 3 + cj) * 128:(h * 3 + cj + 1) * 128],
                    VT[:, (nt - 1 + cj) * 520 + 65 * h: (nt - 1 + cj) * 520 + 65 * h + 64],
                    start=stt, stop=lst))
                if lst:
                    p.inc_on("tensor", fnc, ww0 if first else [], tE)
                else:
                    p.on("tensor", fnc, ww0 if first else [])
                first = False
            p.inc_on("tensor", (lambda e, cbank=cbank, h=h, j=j, par=par, nt=nt: MM(e, 
                cbank[:, h * 64:(h + 1) * 64],
                E1[par * 64:par * 64 + 64, j * 4096 + nt * 128: j * 4096 + nt * 128 + 128],
                Gsb[par * 64:par * 64 + 64, j * 65: j * 65 + 64],
                start=True, stop=True, tile_position=(par * 64, 0))), [], tE)
            fn = (lambda e, dbank=dbank, h=h, j=j, par=par, nt=nt: MM(e, 
                dbank[:, h:h + 1],
                E1[par * 64:par * 64 + 64, j * 4096 + nt * 128: j * 4096 + nt * 128 + 128],
                Gsb[par * 64:par * 64 + 64, j * 65 + 64: j * 65 + 65],
                start=True, stop=True, tile_position=(par * 64, 0)))
            tmm = p.inc_on("tensor", fn,
                           [(vE, rv_l[nt - 2])] if (h == 0 and nt >= 2) else [], tE)
        rv = p.inc_on("vector", (lambda e, dbank=dbank: e.reciprocal(SMB[:, 40:48], dbank[:, 0:8])),
                      [(tE, tmm)], vE)
        rv_l.append(rv)
        cn = p.inc_on("vector", (lambda e, cbank=cbank: e.tensor_tensor(
            EVAC[:, 0:512].rearrange("p (a b) -> p a b", b=64),
            cbank[:, 0:512].rearrange("p (a b) -> p a b", b=64),
            SMB[:, 40:48].rearrange("p (a b) -> p a b", b=1).broadcast_to((128, 8, 64)),
            ALU.mult)), [(vE, rv)], vE)
        dPs = dP if nt % 2 == 0 else dPb
        fl = p.inc_on("vector", (lambda e, vbank=vbank, stg=stg: e.scalar_tensor_tensor(
            stg[:], vbank[:, 0:512], 1.0, EVAC[:, 0:512], ALU.mult, ALU.add)),
            [(vE, cn)] + ([(dPs, pay_st[nt - 2])] if nt >= 2 else []), vE)
        ctx_done.append(fl)
        pay_st[nt] = p.dma("gpsimd", dPs, ctxme_d.ap()[nt * 128:(nt + 1) * 128, :], stg[:],
                           waits=[(vE, fl)])
    ag_ctx = p.nochain("gpsimd", lambda e: e.collective_compute(
        "AllGather", ALU.bypass, replica_groups=[[0, 1], [2, 3], [4, 5], [6, 7]],
        ins=[ctxme_d.ap().opt()], outs=[ctxab_d.ap().opt()]),
        [(dP, pay_st[-2]), (dPb, pay_st[-1])], cx)
    p.emit(nc)
    if stop_after <= 3:
        for gd in guards.values():
            gd.__exit__(None, None, None)
        nc.finalize()
        return nc

    p = Prog({"scalar": aE, "vector": vE, "gpsimd": gE})
    dbg_sent = [(dDB, dDB.n)] if dDB.n else []
    wg_ld = 0
    for kk in range(8):
        wg_ld = p.dma("sync", dWG, XT[:, kk * 4096:(kk + 1) * 4096],
                      agg_o.ap()[kk * 128:(kk + 1) * 128, :],
                      waits=([(cw, 4)] + dbg_sent) if kk == 0 else [])
    wu_ld = 0
    for kk in range(8):
        dstT = QT if kk < 4 else KT
        wu_ld = p.dma("scalar", dWU, dstT[:, (kk % 4) * 4096:((kk % 4) + 1) * 4096],
                      agu_o.ap()[kk * 128:(kk + 1) * 128, :],
                      waits=([(cw, 4)] + dbg_sent) if kk == 0 else [])

    regs = {}

    def _mkregs(e):
        regs["rme"] = e.value_load(OFFS[0:1, 0:1])
        regs["rpa"] = e.value_load(OFFS[0:1, 1:2])
        regs["cme"] = e.value_load(OFFS[0:1, 2:3])
        regs["cot"] = e.value_load(OFFS[0:1, 3:4])
        return e.engine_nop()
    p.inc_on("gpsimd", _mkregs, [(cx, ag_ctx)], gE)

    x1_st = [0] * NTO
    asm_v = [0] * NTO
    sq_l = [0] * NTO
    for t in range(NTO):
        slot = t % 2
        xb = XTI[:, slot * 1024:(slot + 1) * 1024]
        cbuf = WVB[:, slot * 1024:(slot + 1) * 1024]
        w = [(vE, asm_v[t - 2]), (aE, sq_l[t - 2]), (dSx, x1_st[t - 2])] if t >= 2 else []
        xv = p.dma("sync", dX[slot], xb, x_own.ap()[t * 128:(t + 1) * 128, :], waits=w)
        def mkld(dst_cols_reg, src_dram, row_reg, t=t, cbuf=cbuf):
            def fn(e, dst_cols_reg=dst_cols_reg, src_dram=src_dram, row_reg=row_reg):
                return e.dma_start(cbuf[:, ds(regs[dst_cols_reg], 512)],
                                   src_dram.ap()[ds(regs[row_reg] + t * 128, 128), :])
            return fn
        wl = [(gE, gE.n)] if t < 2 else [(vE, asm_v[t - 2])]
        p.inc_on("gpsimd", mkld("cme", ctxme_d, "rme"), wl, dCX[slot], 16)
        l2 = p.inc_on("gpsimd", mkld("cot", ctxab_d, "rpa"), [], dCX[slot], 16)
        av = p.inc_on("vector", (lambda e, xb=xb, cbuf=cbuf, t=t: e.scalar_tensor_tensor(
            xb, cbuf, 1.0, xb, ALU.mult, ALU.add, accum_out=st(SS_SUM, t))),
            [(dX[slot], xv), (dCX[slot], l2)], vE)
        sq_l[t] = p.inc_on("scalar", (lambda e, xb=xb, t=t: e.activation(
            EVAC[:], xb, AF.Square, accum_out=st(SS_SSQ, t))), [(vE, av)], aE)
        p.on("vector", (lambda e, xb=xb, t=t: e.reduce_max(st(SS_MAX, t), xb, axis=AX.X)), [])
        asm_v[t] = p.inc_on("vector", (lambda e, xb=xb, t=t: e.tensor_reduce(
            st(SS_MIN, t), xb, axis=AX.X, op=ALU.min)), [], vE)
        x1_st[t] = p.dma("scalar", dSx, x1_d.ap()[t * 128:(t + 1) * 128, :], xb,
                         waits=[(vE, asm_v[t]), (aE, aE.n)] +
                               ([(dSx, x1_st[t - 1])] if t >= 1 else []))
    x1_all_a = aE.n

    p.on("vector", lambda e: e.tensor_scalar(st(SS_M, 0, 16), st(SS_SUM, 0, 16), 1.0 / DM,
                                             None, ALU.mult), [(vE, asm_v[-1]), (aE, x1_all_a)])
    p.on("vector", lambda e: e.tensor_scalar(st(SS_RSTD, 0, 16), st(SS_SSQ, 0, 16), 1.0 / DM,
                                             None, ALU.mult), [])
    p.on("vector", lambda e: e.scalar_tensor_tensor(st(SS_AA, 0, 16), st(SS_M, 0, 16), 1.0,
                                                    st(SS_M, 0, 16), ALU.mult, ALU.mult), [])
    sv = p.inc_on("vector", lambda e: e.tensor_tensor(st(SS_RSTD, 0, 16), st(SS_RSTD, 0, 16),
                                                      st(SS_AA, 0, 16), ALU.subtract), [], vE)
    sqa = p.inc_on("scalar", lambda e: e.activation(st(SS_AA, 0, 16), st(SS_RSTD, 0, 16),
                                                    AF.Sqrt, bias=CONSTS[:, 0:1]), [(vE, sv)], aE)

    _s2ops = [
        lambda e: e.reciprocal(st(SS_RSTD, 0, 16), st(SS_AA, 0, 16)),
        lambda e: e.tensor_tensor(st(SS_AA, 0, 16), st(SS_MAX, 0, 16), st(SS_M, 0, 16), ALU.subtract),
        lambda e: e.scalar_tensor_tensor(st(SS_BBM, 0, 16), st(SS_MIN, 0, 16), -1.0, st(SS_M, 0, 16),
                                         ALU.mult, ALU.add),
        lambda e: e.tensor_tensor(st(SS_AA, 0, 16), st(SS_AA, 0, 16), st(SS_BBM, 0, 16), ALU.max),
        lambda e: e.tensor_tensor(st(SS_AA, 0, 16), st(SS_AA, 0, 16), st(SS_RSTD, 0, 16), ALU.mult),
        lambda e: e.tensor_scalar(st(SS_AA, 0, 16), st(SS_AA, 0, 16), 1e-5, None, ALU.max),
        lambda e: e.reciprocal(st(SS_S2Q, 0, 16), st(SS_AA, 0, 16)),
        lambda e: e.tensor_scalar(st(SS_S2Q, 0, 16), st(SS_S2Q, 0, 16), 127.0, None, ALU.mult),
        lambda e: e.tensor_tensor(st(SS_AA, 0, 16), st(SS_RSTD, 0, 16), st(SS_S2Q, 0, 16), ALU.mult),
        lambda e: e.scalar_tensor_tensor(st(SS_BBM, 0, 16), st(SS_M, 0, 16), -1.0, st(SS_AA, 0, 16),
                                         ALU.mult, ALU.mult),
        lambda e: e.reciprocal(st(SS_IG, 0, 16), st(SS_S2Q, 0, 16)),
        lambda e: e.tensor_scalar(st(SS_IU, 0, 16), st(SS_IG, 0, 16), SMA[:, 17:18], -1.0, ALU.mult, ALU.mult),
    ]
    for ii, op_ in enumerate(_s2ops):
        p.on("vector", op_, [(aE, sqa)] if ii == 0 else [])
    st2 = p.inc_on("vector", lambda e: e.tensor_scalar(st(SS_IG, 0, 16), st(SS_IG, 0, 16),
                                                       SMA[:, 16:17], -1.0, ALU.mult, ALU.mult),
                   [], vE)

    ra_l = []
    rv_l = []
    xi_st = [0] * NTO
    x1all_sx = dSx.n
    for t in range(NTO):
        slot = t % 2
        xb = XTI[:, slot * 1024:(slot + 1) * 1024]
        zb = WVB[:, slot * 1024:(slot + 1) * 1024]
        w = [(vE, st2), (dSx, x1all_sx)]
        if t >= 2:
            w.append((aE, ra_l[t - 2]))
        xv = p.dma("sync", dX[slot], xb, x1_d.ap()[t * 128:(t + 1) * 128, :], waits=w)
        ra = p.inc_on("scalar", (lambda e, xb=xb, t=t: e.activation(
            EVAC[:], xb, AF.Identity, bias=st(SS_BBM, t), scale=st(SS_AA, t))),
            [(dX[slot], xv)] + ([(vE, rv_l[t - 1])] if t >= 1 else []), aE)
        ra_l.append(ra)
        rv = p.inc_on("vector", (lambda e, zb=zb: e.tensor_scalar(
            zb, EVAC[:], MAGIC, -MAGIC, ALU.add, ALU.add)),
            [(aE, ra)] + ([(dSi, xi_st[t - 2])] if t >= 2 else []), vE)
        rv_l.append(rv)
        xi_st[t] = p.dma("gpsimd", dSi, xint_d.ap()[t * 128:(t + 1) * 128, :], zb,
                         waits=[(vE, rv)] + ([(dSi, xi_st[t - 1])] if t >= 1 else []))
    for f in range(8):
        p.dma("sync", dT2, VT[:, f * 2048:(f + 1) * 2048],
              xint_d.ap()[:, f * 128:(f + 1) * 128], transpose=True,
              waits=[(dSi, xi_st[-1])] if f == 0 else [])
    xiT_all = [(dT2, dT2.n)]

    zi_st = [0] * NTO
    s2q_v = [0] * NTO
    gaL, uaL, zvL = [], [], []
    for t in range(NTO):
        gu_done = []
        for n in range(8):
            bka = PB[0 + (n % 2)]
            bkb = PB[2 + (n % 2)]
            gsl = WQS[:, (n % 2) * 1024:(n % 2) * 1024 + 512]
            usl = WQS[:, (n % 2) * 1024 + 512:(n % 2) * 1024 + 1024]
            gi = t * 8 + n
            for kk in range(8):
                ww = []
                if kk == 0:
                    if gi == 0:
                        ww = list(xiT_all) + [(dWG, wg_ld), (dWU, wu_ld), (dSi, xi_st[-1])]
                    if gi >= 2:
                        ww.append((aE, gaL[gi - 2]))
                fng = (lambda e, bka=bka, kk=kk, t=t, n=n: MM(e, 
                    bka[:], VT[:, kk * 2048 + t * 128: kk * 2048 + t * 128 + 128],
                    XT[:, kk * 4096 + n * 512: kk * 4096 + (n + 1) * 512],
                    start=(kk == 0), stop=(kk == 7)))
                if kk == 7:
                    p.inc_on("tensor", fng, ww, tE)
                else:
                    p.on("tensor", fng, ww)
            for kk in range(8):
                srcW = QT if kk < 4 else KT
                ww = []
                if kk == 0 and gi >= 2:
                    ww.append((aE, uaL[gi - 2]))
                fn = (lambda e, bkb=bkb, kk=kk, t=t, n=n, srcW=srcW: MM(e, 
                    bkb[:], VT[:, kk * 2048 + t * 128: kk * 2048 + t * 128 + 128],
                    srcW[:, (kk % 4) * 4096 + n * 512: (kk % 4) * 4096 + (n + 1) * 512],
                    start=(kk == 0), stop=(kk == 7)))
                if kk == 7:
                    up_t = p.inc_on("tensor", fn, ww, tE)
                else:
                    p.on("tensor", fn, ww)
            ga = p.inc_on("scalar", (lambda e, bka=bka, gsl=gsl, t=t: e.activation(
                gsl, bka[:], AF.Silu, scale=st(SS_IG, t))),
                [(tE, up_t)] + ([(vE, zvL[gi - 2])] if gi >= 2 else []), aE)
            ua = p.inc_on("scalar", (lambda e, bkb=bkb, usl=usl, t=t: e.activation(
                usl, bkb[:], AF.Copy, scale=st(SS_IU, t))), [], aE)
            zv = p.inc_on("vector", (lambda e, gsl=gsl, usl=usl, n=n: e.tensor_tensor(
                WVB[:, n * 512:(n + 1) * 512], gsl, usl, ALU.mult)),
                [(aE, ua)] + ([(dZ, zi_st[t - 1])] if (t >= 1 and n == 0) else []), vE)
            gu_done.append(zv)
            zvL.append(zv)
            gaL.append(ga)
            uaL.append(ua)
        p.on("vector", (lambda e: e.tensor_reduce(
            SMB[:, 0:64], WVB[:].rearrange("p (a b) -> p a b", b=64), axis=AX.X,
            op=ALU.max, apply_absolute_value=True)), [(vE, gu_done[-1])])
        p.on("vector", lambda e: e.reduce_max(SMB[:, 32:33], SMB[:, 0:64], axis=AX.X,
                                              apply_absolute_value=True), [])
        p.on("vector", lambda e: e.tensor_scalar(SMB[:, 32:33], SMB[:, 32:33], 1e-5, None, ALU.max), [])
        p.on("vector", lambda e: e.reciprocal(SMB[:, 33:34], SMB[:, 32:33]), [])
        s2q_v[t] = p.inc_on("vector", (lambda e, t=t: e.tensor_scalar(
            st(SS_S2Q, t), SMB[:, 33:34], 127.0, None, ALU.mult)), [], vE)
        qrv = 0
        for q4 in range(4):
            ra = p.inc_on("scalar", (lambda e, q4=q4, t=t: e.activation(
                EVAC[:], WVB[:, q4 * 1024:(q4 + 1) * 1024], AF.Identity, bias=CONSTS[:, 1:2],
                scale=st(SS_S2Q, t))),
                [(vE, s2q_v[t])] + ([(vE, qrv)] if q4 > 0 else []), aE)
            qrv = p.inc_on("vector", (lambda e, q4=q4: e.tensor_scalar(
                WVB[:, q4 * 1024:(q4 + 1) * 1024], EVAC[:], -MAGIC, None, ALU.add)),
                [(aE, ra)], vE)
        zq_v = vE.n
        zi_st[t] = p.dma("gpsimd", dZ, zint_d.ap()[t * 128:(t + 1) * 128, :], WVB[:],
                         waits=[(vE, zq_v)] + ([(dZ, zi_st[t - 1])] if t >= 1 else []))
    p.on("vector", lambda e: e.reciprocal(st(SS_IG, 0, 16), st(SS_S2Q, 0, 16)), [(vE, s2q_v[-1])])
    id_v = p.inc_on("vector", lambda e: e.tensor_scalar(st(SS_ID, 0, 16), st(SS_IG, 0, 16),
                                                        SMA[:, 18:19], -1.0, ALU.mult, ALU.mult),
                    [], vE)
    if debug:
        p.dma("gpsimd", dDB, dbg["dbg_stat"].ap(), STAT[:], waits=[(vE, id_v)])
    p.emit(nc)
    if stop_after <= 4:
        for gd in guards.values():
            gd.__exit__(None, None, None)
        nc.finalize()
        return nc

    p = Prog({"scalar": aE, "vector": vE, "gpsimd": gE})
    for f in range(32):
        if f < 16:
            dst = XT[:, f * 2048:(f + 1) * 2048]
        elif f < 24:
            dst = QT[:, (f - 16) * 2048:(f - 15) * 2048]
        else:
            dst = KT[:, (f - 24) * 2048:(f - 23) * 2048]
        p.dma("sync", dT3 if f < 16 else dT3b, dst,
              zint_d.ap()[:, f * 128:(f + 1) * 128], transpose=True,
              waits=[(dZ, dZ.n)] if f == 0 else [])
    zT_half1 = [(dT3, dT3.n)]
    zT_half2 = [(dT3b, dT3b.n)]

    def zT_chunk(kk):
        if kk < 16:
            return XT[:, kk * 2048:(kk + 1) * 2048]
        if kk < 24:
            return QT[:, (kk - 16) * 2048:(kk - 15) * 2048]
        return KT[:, (kk - 24) * 2048:(kk - 23) * 2048]

    dOs = [sems["dO"], sems["dM"]]
    ost_prev = [0, 0]
    ev_prev = [0] * 4
    x1all_sx2 = dSx.n
    for half in range(2):
        wd_ld = 0
        for kk in range(32):
            wd_ld = p.dma("gpsimd", dV2, VT[:, kk * 512:(kk + 1) * 512],
                          agd_o.ap()[kk * 128:(kk + 1) * 128, half * 512:(half + 1) * 512],
                          waits=([(tE, tE.n)] if (half == 1 and kk == 0) else
                                 ([(cw, 4)] if (half == 0 and kk == 0) else [])))
        for t in range(NTO):
            bank = PB[t % 4]
            oslot = t % 2
            xb = XTI[:, (t % 4) * 1024:(t % 4) * 1024 + 512]
            xv = p.dma("sync" if t % 2 == 0 else "scalar", dX[t % 4], xb,
                       x1_d.ap()[t * 128:(t + 1) * 128, half * 512:(half + 1) * 512],
                       waits=([(vE, ev_prev[t % 4])] if ev_prev[t % 4] else []) +
                             ([(dSx, x1all_sx2)] if (half == 0 and t < 4) else []))
            for kk in range(32):
                ww = []
                if kk == 0:
                    ww = list(zT_half1) + [(dV2, wd_ld)]
                    if t >= 4:
                        ww.append((vE, ev_prev[t % 4]))
                elif kk == 16:
                    ww = list(zT_half2)
                fn = (lambda e, bank=bank, kk=kk, t=t: MM(e, 
                    bank[:], zT_chunk(kk)[:, t * 128:(t + 1) * 128],
                    VT[:, kk * 512:(kk + 1) * 512],
                    start=(kk == 0), stop=(kk == 31)))
                if kk == 31:
                    mm_v = p.inc_on("tensor", fn, ww, tE)
                else:
                    p.on("tensor", fn, ww)
            ev = p.inc_on("vector", (lambda e, bank=bank, xb=xb, t=t, oslot=oslot: e.scalar_tensor_tensor(
                EVAC[:, oslot * 512:(oslot + 1) * 512], bank[:], st(SS_ID, t), xb,
                ALU.mult, ALU.add)),
                [(tE, mm_v), (dX[t % 4], xv)] + ([(dOs[oslot], ost_prev[oslot])] if ost_prev[oslot] else []), vE)
            ev_prev[t % 4] = ev
            ost_prev[oslot] = p.dma("gpsimd", dOs[oslot],
                                    out_d.ap()[t * 128:(t + 1) * 128, half * 512:(half + 1) * 512],
                                    EVAC[:, oslot * 512:(oslot + 1) * 512], waits=[(vE, ev)])
    f0, f1 = dOs[0].n, dOs[1].n
    p.on("sync", lambda e: (e.wait_ge(dOs[0].sem, f0), e.wait_ge(dOs[1].sem, f1))[-1], [])
    if debug:
        p.dma("gpsimd", dDB, dbg["dbg_ctxme"].ap(), ctxme_d.ap(), waits=[(dOs[0], f0), (dOs[1], f1)])
        p.dma("gpsimd", dDB, dbg["dbg_x1"].ap(), x1_d.ap())
        p.dma("gpsimd", dDB, dbg["dbg_xint"].ap(), xint_d.ap())
        p.dma("gpsimd", dDB, dbg["dbg_zint"].ap(), zint_d.ap())
        p.dma("gpsimd", dDB, dbg["dbg_wgT"].ap(), agg_o.ap())
        fdb = dDB.n
        p.on("gpsimd", lambda e: e.wait_ge(dDB.sem, fdb), [])
    p.emit(nc)

    for gd in guards.values():
        gd.__exit__(None, None, None)
    nc.finalize()
    return nc


_NC_CACHE = {}


def _get_nc(debug=False):
    key = bool(debug)
    if key not in _NC_CACHE:
        _NC_CACHE[key] = build_nc(debug=debug)
    return _NC_CACHE[key]


def _build_conv_patterns(conv_w, s):
    w = np.asarray(conv_w, np.float32).reshape(H, KSZ)
    out = np.zeros((H8 * 3 * 128, 128), np.float32)
    i_idx = np.arange(128)[:, None]
    t_idx = np.arange(128)[None, :]
    for hh in range(H8):
        hcol = s * H8 + hh
        for cj in range(3):
            kidx = 128 * (cj - 1) + i_idx - t_idx + 32
            valid = (kidx >= 0) & (kidx < KSZ)
            blk = np.zeros((128, 128), np.float32)
            blk[valid] = w[hcol][kidx[valid]]
            out[(hh * 3 + cj) * 128:(hh * 3 + cj + 1) * 128, :] = blk
    return out


def kernel(L0_attn_hidden_states, L0_attn_Wq, L0_attn_bq, L0_attn_Wk, L0_attn_bk,
           L0_attn_Wv, L0_attn_bv, L0_attn_conv_w, L0_ffn_w_gate, L0_ffn_w_up,
           L0_ffn_w_down, _debug=False):
    x = np.ascontiguousarray(np.asarray(L0_attn_hidden_states, np.float32))
    Wq = np.asarray(L0_attn_Wq, np.float32)
    Wk = np.asarray(L0_attn_Wk, np.float32)
    Wv = np.asarray(L0_attn_Wv, np.float32)
    bq = np.asarray(L0_attn_bq, np.float32)
    bk = np.asarray(L0_attn_bk, np.float32)
    bv = np.asarray(L0_attn_bv, np.float32)
    wg = np.asarray(L0_ffn_w_gate, np.float32)
    wu = np.asarray(L0_ffn_w_up, np.float32)
    wd = np.asarray(L0_ffn_w_down, np.float32)
    eye2 = np.vstack([np.eye(64, dtype=np.float32)] * 2)

    in_maps = []
    for c in range(8):
        b, s = c // 2, c % 2
        cols = slice(s * HD, (s + 1) * HD)
        offs = np.zeros((1, 8), np.int32)
        offs[0, 0] = s * TOK
        offs[0, 1] = (1 - s) * S + s * TOK
        offs[0, 2] = s * 512
        offs[0, 3] = (1 - s) * 512
        in_maps.append({
            "x_in": x[b],
            "x_own": np.ascontiguousarray(x[b][s * TOK:(s + 1) * TOK]),
            "wq_in": np.ascontiguousarray(Wq[:, cols]),
            "wk_in": np.ascontiguousarray(Wk[:, cols]),
            "wv_in": np.ascontiguousarray(Wv[:, cols]),
            "bq_in": np.ascontiguousarray(bq[cols].reshape(4, 128).T),
            "bk_in": np.ascontiguousarray(bk[cols].reshape(4, 128).T),
            "bv_in": np.ascontiguousarray(bv[cols].reshape(1, HD)),
            "convp_in": _build_conv_patterns(L0_attn_conv_w, s),
            "wg_sl": np.ascontiguousarray(wg[:, c * 128:(c + 1) * 128].T),
            "wu_sl": np.ascontiguousarray(wu[:, c * 128:(c + 1) * 128].T),
            "wd_sl": np.ascontiguousarray(wd[:, c * 512:(c + 1) * 512].T),
            "eye2_in": eye2,
            "offs_in": offs,
        })

    nc = _get_nc(debug=_debug)
    res = run_bass_kernel_spmd(nc, in_maps, core_ids=list(range(8)))

    out = np.empty((B, S, DM), np.float32)
    for c in range(8):
        b, s = c // 2, c % 2
        out[b, s * TOK:(s + 1) * TOK, :] = res.results[c]["out_d"]
    if _debug:
        kernel._dbg = res.results
    return out


# version 08
# speedup vs baseline: 1.0240x; 1.0240x over previous
import numpy as np
import ml_dtypes
import concourse.bass as bass
import concourse.bacc as bacc
import concourse.mybir as mybir
from concourse.bass_utils import run_bass_kernel_spmd

F32 = mybir.dt.float32
BF16 = mybir.dt.bfloat16
I32 = mybir.dt.int32
AF = mybir.ActivationFunctionType
ALU = mybir.AluOpType
AX = mybir.AxisListType
bfloat16 = ml_dtypes.bfloat16
ds = bass.ds

B, S, DM = 4, 4096, 1024
H, D, L, KSZ = 16, 64, 64, 65
H8, HD = 8, 512
NT = 32
TOK, NTO = 2048, 16
INNER = 4096
SEG = S // L
QSCALE = float(D) ** (-0.25)
MAGIC = float(np.float32(12582912.0))
EPS = 1e-5
NEGI = -3.0e38
ENGS = ("sync", "scalar", "tensor", "vector", "gpsimd")


def MM(e, *a, **kw):
    kw.setdefault("skip_group_check", True)
    return e.matmul(*a, **kw)


class Ctr:
    def __init__(self, sem):
        self.sem = sem
        self.n = 0

    def inc(self, k=1):
        self.n += k
        return self.n


class Prog:

    def __init__(self, eng_ctrs):
        self.steps = {e: [] for e in ENGS}
        self.eng_ctrs = eng_ctrs
        self.marks = {"scalar": None, "vector": None, "gpsimd": None}

    def _push(self, eng, fn, waits, ctr, amt, chain=True):
        waits = list(waits)
        if eng in self.marks and chain:
            if self.marks[eng] is not None:
                waits.append(self.marks[eng])
            if ctr is None:
                ctr, amt = self.eng_ctrs[eng], 1
        self.steps[eng].append((fn, waits, ctr, amt))
        v = ctr.inc(amt) if ctr is not None else None
        if eng in self.marks and chain and ctr is not None:
            self.marks[eng] = (ctr, v)
        return v

    def on(self, eng, fn, waits=()):
        self._push(eng, fn, waits, None, 0)

    def inc_on(self, eng, fn, waits, ctr, amt=1):
        return self._push(eng, fn, waits, ctr, amt)

    def nochain(self, eng, fn, waits, ctr, amt=1):
        return self._push(eng, fn, waits, ctr, amt, chain=False)

    def dma(self, eng, ctr, out, in_, waits=(), **kw):
        def fn(e, out=out, in_=in_, kw=kw):
            return e.dma_start(out, in_, **kw)
        return self._push(eng, fn, waits, ctr, 16, chain=False)

    def emit(self, nc):
        with nc.Block() as blk:
            for name in ENGS:
                steps = self.steps[name]
                if not steps:
                    continue

                def body(e, steps=steps):
                    for fn, waits, inc, amt in steps:
                        for (c, v) in waits:
                            if v > 0:
                                e.wait_ge(c.sem, v)
                        ins = fn(e)
                        if inc is not None:
                            ins.then_inc(inc.sem, amt)
                getattr(blk, name)(body)


def build_nc(debug=False, stop_after=99):
    nc = bacc.Bacc("TRN2")

    x_in = nc.dram_tensor("x_in", [S, DM], F32, kind="ExternalInput")
    x_own = nc.dram_tensor("x_own", [TOK, DM], F32, kind="ExternalInput")
    wq_in = nc.dram_tensor("wq_in", [DM, HD], F32, kind="ExternalInput")
    wk_in = nc.dram_tensor("wk_in", [DM, HD], F32, kind="ExternalInput")
    wv_in = nc.dram_tensor("wv_in", [DM, HD], F32, kind="ExternalInput")
    bq_in = nc.dram_tensor("bq_in", [128, 4], F32, kind="ExternalInput")
    bk_in = nc.dram_tensor("bk_in", [128, 4], F32, kind="ExternalInput")
    bv_in = nc.dram_tensor("bv_in", [1, HD], F32, kind="ExternalInput")
    convp_in = nc.dram_tensor("convp_in", [H8 * 3 * 128, 128], F32, kind="ExternalInput")
    wg_sl = nc.dram_tensor("wg_sl", [128, INNER], F32, kind="ExternalInput")
    wu_sl = nc.dram_tensor("wu_sl", [128, INNER], F32, kind="ExternalInput")
    wd_sl = nc.dram_tensor("wd_sl", [512, DM], F32, kind="ExternalInput")
    eye2_in = nc.dram_tensor("eye2_in", [128, 64], F32, kind="ExternalInput")
    offs_in = nc.dram_tensor("offs_in", [1, 8], I32, kind="ExternalInput")
    out_d = nc.dram_tensor("out_d", [TOK, DM], F32, kind="ExternalOutput")

    xln_d = nc.dram_tensor("xln_d", [S, DM], BF16, kind="Internal")
    x1_d = nc.dram_tensor("x1_d", [TOK, DM], F32, kind="Internal")
    xint_d = nc.dram_tensor("xint_d", [TOK, DM], BF16, kind="Internal")
    zint_d = nc.dram_tensor("zint_d", [TOK, INNER], BF16, kind="Internal")
    wsum_i = nc.dram_tensor("wsum_i", [1, 8], F32, kind="Internal")
    wsum_o = nc.dram_tensor("wsum_o", [1, 8], F32, kind="Internal")
    amax_i = nc.dram_tensor("amax_i", [1, 8], F32, kind="Internal")
    amax_o = nc.dram_tensor("amax_o", [1, 8], F32, kind="Internal")
    agg_i = nc.dram_tensor("agg_i", [128, INNER], BF16, kind="Internal")
    agg_o = nc.dram_tensor("agg_o", [DM, INNER], BF16, kind="Internal")
    agu_i = nc.dram_tensor("agu_i", [128, INNER], BF16, kind="Internal")
    agu_o = nc.dram_tensor("agu_o", [DM, INNER], BF16, kind="Internal")
    agd_i = nc.dram_tensor("agd_i", [512, DM], BF16, kind="Internal")
    agd_o = nc.dram_tensor("agd_o", [INNER, DM], BF16, kind="Internal")
    ctxme_d = nc.dram_tensor("ctxme_d", [S, HD], BF16, kind="Internal")
    ctxab_d = nc.dram_tensor("ctxab_d", [2 * S, HD], BF16, kind="Internal")

    dbg = {}
    if debug:
        for name, shape, dt_ in [
            ("dbg_qT", [128, 16384], BF16), ("dbg_kT", [128, 16384], BF16),
            ("dbg_v", [128, NT * 520], BF16), ("dbg_ql", [128, 256], BF16),
            ("dbg_kl", [128, 256], BF16), ("dbg_K2", [128, 256], F32),
            ("dbg_G", [128, 4 * 65], BF16), ("dbg_k3v", [128, 256], F32),
            ("dbg_cb", [128, 16], F32), ("dbg_srow", [1, 64], F32),
            ("dbg_stat", [128, 192], F32), ("dbg_ctxme", [S, HD], BF16),
            ("dbg_x1", [TOK, DM], F32), ("dbg_xint", [TOK, DM], BF16),
            ("dbg_zint", [TOK, INNER], BF16), ("dbg_wgT", [DM, INNER], BF16),
        ]:
            dbg[name] = nc.dram_tensor(name, shape, dt_, kind="ExternalOutput")

    XT = nc.alloc_sbuf_tensor("XT", [128, 32768], BF16)
    QT = nc.alloc_sbuf_tensor("QT", [128, 16384], BF16)
    KT = nc.alloc_sbuf_tensor("KT", [128, 16384], BF16)
    VT = nc.alloc_sbuf_tensor("VT", [128, 16640], BF16)
    WVB = nc.alloc_sbuf_tensor("WVB", [128, 4096], BF16)
    WQS = nc.alloc_sbuf_tensor("WQS", [128, 2048], BF16)
    WSL2 = nc.alloc_sbuf_tensor("WSL2", [128, 1024], BF16)
    XTI = nc.alloc_sbuf_tensor("XTI", [128, 4096], F32)
    EVAC = nc.alloc_sbuf_tensor("EVAC", [128, 1024], F32)
    AGPB = nc.alloc_sbuf_tensor("AGPB", [128, 512], BF16)
    qlT = nc.alloc_sbuf_tensor("qlT", [128, 256], BF16)
    klT = nc.alloc_sbuf_tensor("klT", [128, 256], BF16)
    K2sb = nc.alloc_sbuf_tensor("K2sb", [128, 256], F32)
    Zsb = nc.alloc_sbuf_tensor("Zsb", [64, 512], F32)
    LTsb = nc.alloc_sbuf_tensor("LTsb", [64, 512], F32)
    K3V = nc.alloc_sbuf_tensor("K3V", [64, 512], F32)
    Gsb = nc.alloc_sbuf_tensor("Gsb", [128, 4 * 65], BF16)
    EYE2 = nc.alloc_sbuf_tensor("EYE2", [128, 64], F32)
    ONEB = nc.alloc_sbuf_tensor("ONEB", [1, 128], BF16)
    ONEF = nc.alloc_sbuf_tensor("ONEF", [128, 1], F32)
    ONER = nc.alloc_sbuf_tensor("ONER", [1, 128], F32)
    BQS = nc.alloc_sbuf_tensor("BQS", [128, 4], F32)
    BKS = nc.alloc_sbuf_tensor("BKS", [128, 4], F32)
    BVB = nc.alloc_sbuf_tensor("BVB", [1, HD], BF16)
    CB = nc.alloc_sbuf_tensor("CB", [128, 16], F32)
    SROW = nc.alloc_sbuf_tensor("SROW", [1, 64], F32)
    STAT = nc.alloc_sbuf_tensor("STAT", [128, 192], F32)
    SMA = nc.alloc_sbuf_tensor("SMA", [128, 32], F32)
    SMB = nc.alloc_sbuf_tensor("SMB", [128, 64], F32)
    OFFS = nc.alloc_sbuf_tensor("OFFS", [1, 8], I32)
    CONSTS = nc.alloc_sbuf_tensor("CONSTS", [128, 2], F32)
    PB = [nc.alloc_psum_tensor(f"PB{i}", [128, 512], F32) for i in range(8)]

    XLB = [QT[:, 0:1024], QT[:, 1024:2048]]
    CONVP = XT[:, 16384:16384 + 3072]
    E1 = XT
    EYE128 = EVAC[:, 512:640]
    NCB = BVB

    sem_names = ["dX0", "dX1", "dX2", "dX3", "dS0", "dS1", "dM", "dT", "dW0", "dW1",
                 "dV", "dL", "dF0", "dF1", "dF2", "dE", "dP", "dWG", "dWU", "dSx",
                 "dSi", "dZ", "dT2", "dT3", "dV2", "dO", "dCX0", "dCX1", "dM2", "dDB",
                 "dN1", "dN2", "dN3", "dT3b", "dPb",
                 "tE", "vE", "aE", "gE", "cw", "cm", "cx"]
    guards = {n: nc.semaphore(n) for n in sem_names}
    sems = {n: Ctr(gd.__enter__()) for n, gd in guards.items()}
    dX = [sems["dX0"], sems["dX1"], sems["dX2"], sems["dX3"]]
    dS = [sems["dS0"], sems["dS1"]]
    dM, dT, dV, dL, dE, dP = (sems[k] for k in ("dM", "dT", "dV", "dL", "dE", "dP"))
    dW = [sems["dW0"], sems["dW1"]]
    dF3 = [sems["dF0"], sems["dF1"], sems["dF2"]]
    dWG, dWU, dSx, dSi, dZ = (sems[k] for k in ("dWG", "dWU", "dSx", "dSi", "dZ"))
    dT2, dT3, dV2, dO = (sems[k] for k in ("dT2", "dT3", "dV2", "dO"))
    dCX = [sems["dCX0"], sems["dCX1"]]
    dM2 = sems["dM2"]
    dDB = sems["dDB"]
    dN1, dN2, dN3 = sems["dN1"], sems["dN2"], sems["dN3"]
    dT3b = sems["dT3b"]
    dPb = sems["dPb"]
    dOs = None
    tE, vE, aE, gE = (sems[k] for k in ("tE", "vE", "aE", "gE"))
    cw, cm, cx = (sems[k] for k in ("cw", "cm", "cx"))

    (SS_SUM, SS_SSQ, SS_MAX, SS_MIN, SS_M, SS_RSTD, SS_AA, SS_BBM,
     SS_S2Q, SS_IG, SS_IU, SS_ID) = range(12)

    def st(jj, t=0, w=1):
        return STAT[:, jj * 16 + t: jj * 16 + t + w]

    p = Prog({"scalar": aE, "vector": vE, "gpsimd": gE})
    for dst, src in [(EYE2[:], eye2_in.ap()), (BQS[:], bq_in.ap()), (BKS[:], bk_in.ap()),
                     (BVB[:], bv_in.ap()), (OFFS[:], offs_in.ap())]:
        p.dma("gpsimd", dM, dst, src)
    dM_all = dM.n

    for _mfn in [lambda e: e.memset(ONEB[:], 1.0), lambda e: e.memset(ONEF[:], 1.0),
                 lambda e: e.memset(ONER[:], 1.0), lambda e: e.memset(Gsb[:], 1.0),
                 lambda e: e.memset(SMA[:], NEGI), lambda e: e.memset(CB[:], 0.0),
                 lambda e: e.memset(SROW[:], 0.0), lambda e: e.memset(STAT[:], 0.0),
                 lambda e: e.memset(CONSTS[:, 0:1], EPS),
                 lambda e: e.memset(CONSTS[:, 1:2], MAGIC)]:
        p.on("gpsimd", _mfn, [])
    ms_g = p.inc_on("gpsimd", lambda e: e.memset(SMB[:], 0.0), [], gE)

    nrm_done = [0] * NT
    store_done = [0] * NT
    for ti in range(NT):
        slot, buf = ti % 4, ti % 2
        xs = XTI[:, slot * 1024:(slot + 1) * 1024]
        w = [(vE, nrm_done[ti - 4])] if ti >= 4 else []
        xl = p.dma("sync" if ti % 2 == 0 else "scalar", dX[slot], xs,
                   x_in.ap()[ti * 128:(ti + 1) * 128, :], waits=w)
        wv = [(dX[slot], xl)]
        if ti == 0:
            wv.append((gE, ms_g))
        p.on("vector", (lambda e, xs=xs, slot=slot: e.bn_stats(
            SMB[:, 8 + slot * 12:14 + slot * 12], xs[:, 0:512])), wv)
        p.on("vector", (lambda e, xs=xs, slot=slot: e.bn_stats(
            SMB[:, 14 + slot * 12:20 + slot * 12], xs[:, 512:1024])), [])
        ag_v = p.inc_on("vector", (lambda e, slot=slot: e.bn_aggr(
            SMB[:, 56 + slot * 2:58 + slot * 2], SMB[:, 8 + slot * 12:20 + slot * 12])), [], vE)
        sq_a = p.inc_on("scalar", (lambda e, slot=slot: e.activation(
            SMB[:, slot:slot + 1], SMB[:, 57 + slot * 2:58 + slot * 2],
            AF.Sqrt, bias=CONSTS[:, 0:1])), [(vE, ag_v)], aE)
        p.on("vector", (lambda e, slot=slot: e.reciprocal(
            SMB[:, 4 + slot:5 + slot], SMB[:, slot:slot + 1])), [(aE, sq_a)])
        w2 = [(dS[buf], store_done[ti - 2])] if ti >= 2 else []
        nrm_done[ti] = p.inc_on("vector", (lambda e, xs=xs, buf=buf, slot=slot: e.tensor_scalar(
            XLB[buf], xs, SMB[:, 56 + slot * 2:57 + slot * 2],
            SMB[:, 4 + slot:5 + slot], ALU.subtract, ALU.mult)), w2, vE)
        store_done[ti] = p.dma("gpsimd", dS[buf], xln_d.ap()[ti * 128:(ti + 1) * 128, :],
                               XLB[buf], waits=[(vE, nrm_done[ti])])
    p.emit(nc)
    if stop_after <= 0:
        for gd in guards.values():
            gd.__exit__(None, None, None)
        nc.finalize()
        return nc

    p = Prog({"scalar": aE, "vector": vE, "gpsimd": gE})
    ag_defer = []
    xst = [(dS[0], dS[0].n), (dS[1], dS[1].n)]
    for f in range(8):
        p.dma("sync", dT, XT[:, f * 4096:(f + 1) * 4096],
              xln_d.ap()[:, f * 128:(f + 1) * 128], transpose=True, waits=xst if f == 0 else [])
    xt_all = [(dT, dT.n)]

    wblocks = [("q", wq_in, m) for m in range(4)] + [("k", wk_in, m) for m in range(4)]
    wb_done = []
    for i, (_, win, m) in enumerate(wblocks):
        half = i % 2
        w = [(tE, (i - 1) * 8)] if i >= 2 else []
        v = p.dma("gpsimd", dW[half], WQS[:, half * 1024:(half + 1) * 1024].rearrange("p (a b) -> p a b", b=128),
                  win.ap()[:, m * 128:(m + 1) * 128].rearrange("(a p) b -> p a b", p=128), waits=w)
        wb_done.append((dW[half], v))
    wv_done = 0
    for kk in range(8):
        wv_done = p.dma("gpsimd", dV, WVB[:, kk * 512:(kk + 1) * 512],
                        wv_in.ap()[kk * 128:(kk + 1) * 128, :])

    p.inc_on("vector", lambda e: e.tensor_scalar(BQS[:], BQS[:], QSCALE, None, ALU.mult),
             [(dM, dM_all)], vE)
    bias_v = p.inc_on("vector", lambda e: e.tensor_scalar(BKS[:], BKS[:], QSCALE, None, ALU.mult), [], vE)

    qk_evac = []
    for i, (pj, _, m) in enumerate(wblocks):
        half = i % 2
        dstT = QT if pj == "q" else KT
        bias = BQS if pj == "q" else BKS
        for ntb in range(8):
            idx = i * 8 + ntb
            bank = PB[idx % 4]
            for kk in range(8):
                ww = []
                if kk == 0:
                    ww = [wb_done[i]] + xt_all
                    if idx >= 4:
                        ww.append((aE, qk_evac[idx - 4]))
                fn = (lambda e, bank=bank, half=half, kk=kk, ntb=ntb:
                      MM(e, bank[:], WQS[:, half * 1024 + kk * 128: half * 1024 + (kk + 1) * 128],
                               XT[:, kk * 4096 + ntb * 512: kk * 4096 + (ntb + 1) * 512],
                               start=(kk == 0), stop=(kk == 7)))
                if kk == 7:
                    mm_v = p.inc_on("tensor", fn, ww, tE)
                else:
                    p.on("tensor", fn, ww)
            ev = p.inc_on("scalar", (lambda e, bank=bank, dstT=dstT, bias=bias, m=m, ntb=ntb:
                                     e.activation(dstT[:, m * 4096 + ntb * 512: m * 4096 + (ntb + 1) * 512],
                                                  bank[:], AF.Identity, bias=bias[:, m:m + 1], scale=QSCALE)),
                          [(tE, mm_v), (vE, bias_v)], aE)
            qk_evac.append(ev)
    q_done_a = qk_evac[31]
    k_done_a = qk_evac[63]

    v_evac = []
    for nt in range(NT):
        bank = PB[4 + nt % 2]
        for kk in range(8):
            ww = []
            if kk == 0:
                ww = [(dV, wv_done)] + xt_all
                if nt >= 2:
                    ww.append((aE, v_evac[nt - 2]))
            p.on("tensor", (lambda e, bank=bank, kk=kk, nt=nt:
                            MM(e, bank[:], XT[:, kk * 4096 + nt * 128: kk * 4096 + nt * 128 + 128],
                                     WVB[:, kk * 512:(kk + 1) * 512],
                                     start=(kk == 0), stop=False)), ww)
        mm_v = p.inc_on("tensor", (lambda e, bank=bank:
                                   MM(e, bank[:], ONEB[:], BVB[:], start=False, stop=True)), [], tE)
        ev = p.inc_on("scalar", (lambda e, bank=bank, nt=nt: e.activation(
            VT[:, nt * 520:nt * 520 + 520].rearrange("p (a b) -> p a b", b=65)[:, :, 0:64],
            bank[:].rearrange("p (a b) -> p a b", b=64), AF.Copy)), [(tE, mm_v)], aE)
        v_evac.append(ev)
    vones_v = p.inc_on("vector", lambda e: e.memset(
        VT[:].rearrange("p (a b) -> p a b", b=65)[:, :, 64:65], 1.0), [(aE, v_evac[-1])], vE)

    pool_done_v = 0
    for nm, srcT, dstL, wgate in (("q", QT, qlT, q_done_a), ("k", KT, klT, k_done_a)):
        for chg in range(4):
            p.on("vector", (lambda e, srcT=srcT, chg=chg: e.tensor_reduce(
                SMB[:, 0:64], srcT[:, chg * 4096:(chg + 1) * 4096].rearrange("p (a b) -> p a b", b=64),
                axis=AX.X, op=ALU.add)), [(aE, wgate)] if chg == 0 else [])
            pool_done_v = p.inc_on("vector", (lambda e, dstL=dstL, chg=chg: e.tensor_scalar(
                dstL[:, chg * 64:(chg + 1) * 64], SMB[:, 0:64], 1.0 / SEG, None, ALU.mult)), [], vE)

    wpart_v = []
    gv_prev = 0
    for wi, wsl in ((0, wg_sl), (1, wu_sl), (2, wd_sl)):
        wld = [(gE, gv_prev)] if wi > 0 else []
        if wi < 2:
            v = p.dma("gpsimd", dL, XTI[:], wsl.ap(), waits=wld)
        else:
            v = p.dma("gpsimd", dL, XTI[:].rearrange("p (a b) -> p a b", b=1024),
                      wsl.ap().rearrange("(a p) b -> p a b", p=128), waits=wld)
        p.on("vector", (lambda e: e.tensor_reduce(SMB[:, 0:64], XTI[:].rearrange("p (a b) -> p a b", b=64),
                                        axis=AX.X, op=ALU.add, apply_absolute_value=True)),
             [(dL, v)])
        gv = p.inc_on("vector", (lambda e, wi=wi: e.tensor_reduce(
            SMA[:, 8 + wi:9 + wi], SMB[:, 0:64].rearrange("p (a b) -> p a b", b=64),
            axis=AX.X, op=ALU.add)), [], gE)
        gv_prev = gv
        tv = p.inc_on("tensor", (lambda e, wi=wi: MM(e, 
            PB[6][0:1, wi:wi + 1], ONEF[:], SMA[:, 8 + wi:9 + wi], start=True, stop=True)),
            [(gE, gv)], tE)
        wpart_v.append(p.inc_on("vector", (lambda e, wi=wi: e.tensor_copy(
            SROW[0:1, wi:wi + 1], PB[6][0:1, wi:wi + 1])), [(tE, tv)], vE))
    zero_v = p.inc_on("vector", lambda e: e.memset(SROW[0:1, 3:8], 0.0), [], vE)
    wsum_sent = p.dma("gpsimd", dE, wsum_i.ap(), SROW[0:1, 0:8],
                      waits=[(vE, wpart_v[-1]), (vE, zero_v)])
    ar_w = p.nochain("gpsimd", lambda e: e.collective_compute(
        "AllReduce", ALU.add, replica_groups=[list(range(8))],
        ins=[wsum_i.ap().opt()], outs=[wsum_o.ap().opt()]), [(dE, wsum_sent)], cw)
    rb_w = p.dma("gpsimd", dE, SROW[0:1, 8:16], wsum_o.ap(), waits=[(cw, ar_w)])

    p.on("vector", lambda e: e.tensor_scalar(SROW[0:1, 16:19], SROW[0:1, 8:11],
                                             1.0 / (INNER * DM), None, ALU.mult), [(dE, rb_w)])
    p.on("vector", lambda e: e.tensor_scalar(SROW[0:1, 16:19], SROW[0:1, 16:19],
                                             1e-5, None, ALU.max), [])
    p.on("vector", lambda e: e.memset(SROW[0:1, 19:20], 0.0), [])
    th_v = p.inc_on("vector", lambda e: e.tensor_scalar(SROW[0:1, 20:23], SROW[0:1, 16:19],
                                                        0.5, None, ALU.mult), [], vE)
    thb_t = p.inc_on("tensor", lambda e: MM(e, PB[6][:, 8:15], ONER[:], SROW[0:1, 16:23],
                                                  start=True, stop=True), [(vE, th_v)], tE)
    thb_v = p.inc_on("vector", lambda e: e.tensor_copy(SMA[:, 16:23], PB[6][:, 8:15]), [(tE, thb_t)], vE)
    thn_v = p.inc_on("vector", lambda e: e.tensor_scalar(SMA[:, 24:27], SMA[:, 20:23], -1.0,
                                                         None, ALU.mult), [(vE, thb_v)], vE)

    tv = 0
    prev_store = None
    for oi, (wi, wsl, agin, agout) in enumerate(
            [(2, wd_sl, agd_i, agd_o), (1, wu_sl, agu_i, agu_o), (0, wg_sl, agg_i, agg_o)]):
        if oi > 0:
            ww_ld = [(vE, tv)]
            if wi < 2:
                ld = p.dma("gpsimd", dL, XTI[:], wsl.ap(), waits=ww_ld)
            else:
                ld = p.dma("gpsimd", dL, XTI[:].rearrange("p (a b) -> p a b", b=1024),
                           wsl.ap().rearrange("(a p) b -> p a b", p=128), waits=ww_ld)
        else:
            ld = dL.n
        stv = 0
        for q4 in range(4):
            sl = XTI[:, q4 * 1024:(q4 + 1) * 1024]
            p.on("vector", (lambda e, sl=sl, wi=wi: e.tensor_scalar(
                WSL2[:], sl, SMA[:, 20 + wi:21 + wi], None, ALU.is_gt)),
                [(dL, ld), (vE, thn_v)] + ([(dF3[wi], stv)] if q4 > 0 else
                                           ([prev_store] if prev_store else [])))
            tv = p.inc_on("vector", (lambda e, sl=sl, wi=wi: e.scalar_tensor_tensor(
                WSL2[:], sl, SMA[:, 24 + wi:25 + wi], WSL2[:], ALU.is_lt, ALU.subtract)), [], vE)
            if wi < 2:
                stv = p.dma("gpsimd", dF3[wi], agin.ap()[:, q4 * 1024:(q4 + 1) * 1024], WSL2[:],
                            waits=[(vE, tv)])
            else:
                stv = p.dma("gpsimd", dF3[wi], agin.ap()[q4 * 128:(q4 + 1) * 128, :], WSL2[:],
                            waits=[(vE, tv)])
        prev_store = (dF3[wi], stv)
        ag_defer.append((agin, agout, dF3[wi], stv))
    p.emit(nc)
    if stop_after <= 1:
        for gd in guards.values():
            gd.__exit__(None, None, None)
        nc.finalize()
        return nc

    p = Prog({"scalar": aE, "vector": vE, "gpsimd": gE})

    def _fin():
        p.emit(nc)
        for gd in guards.values():
            gd.__exit__(None, None, None)
        nc.finalize()
        return nc
    convp_ld = p.dma("gpsimd", dM2, CONVP[:].rearrange("p (a b) -> p a b", b=128),
                     convp_in.ap().rearrange("(a p) b -> p a b", p=128))

    p.on("gpsimd", lambda e: e.memset(EVAC[:, 640:768], 1.0), [])
    eye_g = p.inc_on("gpsimd", lambda e: e.affine_select(EYE128, EVAC[:, 640:768], [[1, 128]],
                                                         ALU.is_equal, 0.0, base=0,
                                                         channel_multiplier=-1), [], gE)

    s2mm = []
    for h in range(H8):
        par, ch = h % 2, h // 2
        pr = slice(par * 64, par * 64 + 64)
        s2mm.append(p.inc_on("tensor", (lambda e, pr=pr, ch=ch, par=par: MM(e, 
            PB[0][pr, ch * 64:(ch + 1) * 64], qlT[pr, ch * 64:(ch + 1) * 64],
            klT[pr, ch * 64:(ch + 1) * 64], start=True, stop=True,
            tile_position=(par * 64, par * 64))), [], tE))
    k2_v = [0] * H8
    ex_l = [0] * H8
    for h in range(H8):
        par, ch = h % 2, h // 2
        pr = slice(par * 64, par * 64 + 64)
        p.on("vector", (lambda e, pr=pr, ch=ch, h=h: e.reduce_max(
            SMA[pr, h:h + 1], PB[0][pr, ch * 64:(ch + 1) * 64], axis=AX.X)),
            [(tE, s2mm[-1])] + ([(aE, ex_l[h - 2])] if h >= 2 else []))
        rm_v = p.inc_on("vector", (lambda e, pr=pr, h=h: e.tensor_scalar(
            SMB[pr, 32:33], SMA[pr, h:h + 1], -1.0, None, ALU.mult)), [], vE)
        ex_a = p.inc_on("scalar", (lambda e, pr=pr, ch=ch: e.activation(
            EVAC[pr, 0:64], PB[0][pr, ch * 64:(ch + 1) * 64], AF.Exp, bias=SMB[pr, 32:33],
            accum_out=SMB[pr, 33:34])),
            [(vE, rm_v)] + ([(vE, k2_v[h - 2])] if h >= 2 else []), aE)
        ex_l[h] = ex_a
        rc_v = p.inc_on("vector", (lambda e, pr=pr: e.reciprocal(SMB[pr, 34:35], SMB[pr, 33:34])),
                        [(aE, ex_a)], vE)
        k2_v[h] = p.inc_on("vector", (lambda e, pr=pr, ch=ch: e.tensor_scalar(
            K2sb[pr, ch * 64:(ch + 1) * 64], EVAC[pr, 0:64], SMB[pr, 34:35], None, ALU.mult)),
            [(vE, rc_v)], vE)
        p.inc_on("tensor", (lambda e, pr=pr, ch=ch, h=h, par=par: MM(e, 
            PB[1][0:1, h * 64:(h + 1) * 64], ONEF[pr, :], K2sb[pr, ch * 64:(ch + 1) * 64],
            start=True, stop=True, tile_position=(par * 64, 0))), [(vE, k2_v[h])], tE)
    k2_all_v = vE.n
    cs_all_t = tE.n
    if stop_after <= 2.1:
        return _fin()
    p.on("vector", lambda e: e.memset(SROW[0:1, 33:40], NEGI), [])
    am_v = p.inc_on("vector", lambda e: e.reduce_max(SROW[0:1, 32:33], PB[1][0:1, 0:512], axis=AX.X),
                    [(tE, cs_all_t)], vE)
    am_sent = p.dma("gpsimd", dE, amax_i.ap(), SROW[0:1, 32:40], waits=[(vE, am_v)])
    ar_m = p.nochain("gpsimd", lambda e: e.collective_compute(
        "AllReduce", ALU.max, replica_groups=[list(range(8))],
        ins=[amax_i.ap().opt()], outs=[amax_o.ap().opt()]), [(dE, am_sent)], cm)
    rb_m = p.dma("gpsimd", dE, SROW[0:1, 40:48], amax_o.ap(), waits=[(cm, ar_m)])
    for agin_, agout_, f3s_, stv_ in ag_defer:
        p.nochain("gpsimd", (lambda e, agin=agin_, agout=agout_: e.collective_compute(
            "AllGather", ALU.bypass, replica_groups=[list(range(8))],
            ins=[agin.ap().opt()], outs=[agout.ap().opt()])), [(f3s_, stv_)], cw)
    c_v = p.inc_on("vector", lambda e: e.reciprocal(SROW[0:1, 48:49], SROW[0:1, 40:41]),
                   [(dE, rb_m)], vE)
    cb_t = p.inc_on("tensor", lambda e: MM(e, PB[1][:, 12:13], ONER[:], SROW[0:1, 48:49],
                                                 start=True, stop=True), [(vE, c_v)], tE)
    cbc_v = p.inc_on("vector", lambda e: e.tensor_copy(CB[:, 12:13], PB[1][:, 12:13]),
                     [(tE, cb_t)], vE)
    if stop_after <= 2.2:
        return _fin()

    p.on("vector", lambda e: e.memset(EVAC[:, 768:832], NEGI), [(vE, k2_all_v)])
    cp_v = p.inc_on("vector", lambda e: e.tensor_copy(EVAC[:, 768:776], SMA[:, 0:8]), [], vE)
    ct_t = p.inc_on("tensor", lambda e: e.transpose(PB[2][0:64, 0:128], EVAC[:, 768:832], EYE128),
                    [(vE, cp_v), (gE, eye_g)], tE)
    cc_v = p.inc_on("vector", lambda e: e.reduce_max(SMB[0:64, 35:36], PB[2][0:64, 0:128], axis=AX.X),
                    [(tE, ct_t)], vE)

    p.on("vector", lambda e: e.memset(EVAC[0:64, 832:896], NEGI), [(vE, cc_v)])
    cp2_v = p.inc_on("vector", lambda e: e.tensor_copy(EVAC[0:64, 832:833], SMB[0:64, 35:36]), [], vE)
    ct2_t = p.inc_on("tensor", lambda e: e.transpose(PB[2][0:64, 128:192], EVAC[0:64, 832:896],
                                                     EYE2[0:64, :]), [(vE, cp2_v)], tE)
    ncrow_v = p.inc_on("vector", lambda e: e.tensor_scalar(SROW[0:1, 52:60], PB[2][0:1, 128:136],
                                                           -1.0, None, ALU.mult), [(tE, ct2_t)], vE)
    nb_t = p.inc_on("tensor", lambda e: MM(e, PB[2][:, 192:200], ONER[:], SROW[0:1, 52:60],
                                                 start=True, stop=True), [(vE, ncrow_v)], tE)
    nb_v = p.inc_on("vector", lambda e: e.tensor_copy(CB[:, 4:12], PB[2][:, 192:200]), [(tE, nb_t)], vE)

    for j in range(4):
        p.on("vector", (lambda e, j=j: e.tensor_copy(CB[0:64, j:j + 1],
                                                     CB[0:64, 4 + 2 * j:5 + 2 * j])),
             [(vE, nb_v)] if j == 0 else [])
        pairb_v = p.inc_on("vector", (lambda e, j=j: e.tensor_copy(
            CB[64:128, j:j + 1], CB[64:128, 5 + 2 * j:6 + 2 * j])), [], vE)

    for h in range(8):
        ncb_v = p.inc_on("vector", (lambda e, h=h: e.tensor_copy(
            NCB[0:1, h * 64:(h + 1) * 64], CB[0:1, 4 + h:5 + h].broadcast_to((1, 64)))),
            [(vE, nb_v)] if h == 0 else [], vE)

    if stop_after <= 2.3:
        return _fin()
    e1_evac = []
    for j in range(4):
        for ntb in range(8):
            idx = j * 8 + ntb
            bank = PB[0] if idx % 2 == 0 else PB[3]
            ww = [(vE, pairb_v)]
            if idx >= 2:
                ww.append((aE, e1_evac[idx - 2]))
            p.inc_on("tensor", (lambda e, bank=bank, j=j, ntb=ntb: MM(e, 
                bank[0:64, 0:512], klT[0:64, j * 64:(j + 1) * 64],
                QT[0:64, j * 4096 + ntb * 512: j * 4096 + (ntb + 1) * 512],
                start=True, stop=True, tile_position=(0, 0))), ww, tE)
            mm_v = p.inc_on("tensor", (lambda e, bank=bank, j=j, ntb=ntb: MM(e, 
                bank[64:128, 0:512], klT[64:128, j * 64:(j + 1) * 64],
                QT[64:128, j * 4096 + ntb * 512: j * 4096 + (ntb + 1) * 512],
                start=True, stop=True, tile_position=(64, 64))), [], tE)
            ev = p.inc_on("scalar", (lambda e, bank=bank, j=j, ntb=ntb: e.activation(
                E1[:, j * 4096 + ntb * 512: j * 4096 + (ntb + 1) * 512], bank[:, 0:512],
                AF.Exp, bias=CB[:, j:j + 1])), [(tE, mm_v)], aE)
            e1_evac.append(ev)
    e1_done_a = e1_evac[-1]

    if stop_after <= 2.4:
        return _fin()
    exp_done = [0, 0]
    k3_t = 0
    for nt in range(NT):
        sbank = PB[0] if nt % 2 == 0 else PB[3]
        ww0 = [(vE, ncb_v), (aE, e1_done_a)]
        if nt >= 2:
            ww0.append((aE, exp_done[nt % 2]))
        for h in range(H8):
            par, jj = h % 2, h // 2
            pr = slice(par * 64, par * 64 + 64)
            p.on("tensor", (lambda e, sbank=sbank, pr=pr, jj=jj, nt=nt, par=par, h=h: MM(e, 
                sbank[:, h * 64:(h + 1) * 64],
                KT[pr, jj * 4096 + nt * 128: jj * 4096 + nt * 128 + 128],
                qlT[pr, jj * 64:(jj + 1) * 64],
                start=True, stop=False, tile_position=(par * 64, 0))), ww0 if h == 0 else [])
            fn = (lambda e, sbank=sbank, h=h: MM(e, 
                sbank[:, h * 64:(h + 1) * 64], ONEB[:], NCB[0:1, h * 64:(h + 1) * 64],
                start=False, stop=True))
            smm = p.inc_on("tensor", fn, [], tE)
        ww_exp = [(tE, smm)]
        if nt >= 1:
            ww_exp.append((tE, k3_t))
        ev = p.inc_on("scalar", (lambda e, sbank=sbank: e.activation(
            AGPB[:], sbank[:, 0:512], AF.Exp)), ww_exp, aE)
        exp_done[nt % 2] = ev
        for h in range(H8):
            par, jj = h % 2, h // 2
            fn = (lambda e, h=h, jj=jj, nt=nt, par=par: MM(e, 
                PB[4 + jj][par * 64:par * 64 + 64, 0:65],
                AGPB[:, h * 64:(h + 1) * 64],
                VT[:, nt * 520 + 65 * h: nt * 520 + 65 * h + 65],
                start=(nt == 0), stop=(nt == NT - 1),
                tile_position=(0, par * 64)))
            ww = [(aE, ev)] if h == 0 else []
            if nt == 0 and h == 0:
                ww.append((vE, vones_v))
            k3_t = p.inc_on("tensor", fn, ww, tE)
    k3v_all_t = k3_t

    k3n_v = 0
    K3Vp = EVAC[:, 0:256]
    for jj in range(4):
        rc2 = p.inc_on("vector", (lambda e, jj=jj: e.reciprocal(SMB[:, 36 + jj:37 + jj],
                                                                PB[4 + jj][:, 64:65])),
                       [(tE, k3v_all_t)], vE)
        k3n_v = p.inc_on("vector", (lambda e, jj=jj: e.tensor_scalar(
            K3Vp[:, jj * 64:(jj + 1) * 64], PB[4 + jj][:, 0:64], SMB[:, 36 + jj:37 + jj],
            None, ALU.mult)), [(vE, rc2)], vE)
    k3mv = 0
    for h in range(H8):
        par, jj = h % 2, h // 2
        k3mv = p.dma("gpsimd", dN1, K3V[0:64, h * 64:(h + 1) * 64],
                     K3Vp[par * 64:par * 64 + 64, jj * 64:(jj + 1) * 64],
                     waits=[(vE, k3n_v)] if h == 0 else [])

    if stop_after <= 2.5:
        return _fin()
    K2all = EVAC[0:64, 256:768]
    K2Tall = EVAC[0:64, 256:768]
    NT1 = EVAC[0:64, 256:768]
    k2mv = 0
    for h in range(H8):
        par, ch = h % 2, h // 2
        k2mv = p.dma("gpsimd", dN2, K2all[:, h * 64:(h + 1) * 64],
                     K2sb[par * 64:par * 64 + 64, ch * 64:(ch + 1) * 64],
                     waits=[(vE, k2_all_v)] if h == 0 else [])
    z0_v = 0
    tK_l = []
    for h in range(H8):
        tk = p.inc_on("tensor", (lambda e, h=h: MM(e, PB[2][0:64, h * 64:(h + 1) * 64],
                                                   K2all[:, h * 64:(h + 1) * 64], EYE2[0:64, :],
                                                   start=True, stop=True)),
                      [(dN2, k2mv)] if h == 0 else [], tE)
        tK_l.append(tk)
    lt0_v = p.inc_on("vector", lambda e: e.tensor_scalar(LTsb[:], K2all[:], CB[0:64, 12:13],
                                                         None, ALU.mult),
                     [(vE, cbc_v), (dN2, k2mv)], vE)
    cpv = p.inc_on("vector", lambda e: e.tensor_copy(K2Tall[:], PB[2][0:64, 0:512]),
                   [(tE, tK_l[-1])], vE)
    zb_l = []
    for h in range(H8):
        zb = p.inc_on("tensor", (lambda e, h=h: MM(e, PB[1][0:64, h * 64:(h + 1) * 64],
                                                   K2Tall[:, h * 64:(h + 1) * 64],
                                                   K2Tall[:, h * 64:(h + 1) * 64],
                                                   start=True, stop=True)),
                      [(vE, cpv)] if h == 0 else [], tE)
        zb_l.append(zb)
    z0_v = p.inc_on("vector", lambda e: e.tensor_scalar(
        Zsb[:], PB[1][0:64, 0:512], CB[0:64, 12:13], None, ALU.mult),
        [(tE, zb_l[-1]), (vE, cbc_v)], vE)
    lt0_v = p.inc_on("vector", lambda e: e.tensor_scalar(LTsb[:], K2all[:], CB[0:64, 12:13],
                                                         None, ALU.mult),
                     [(vE, cbc_v), (dN2, k2mv)], vE)
    if stop_after <= 2.52:
        return _fin()

    def newton_sweep(CUR, first_wait, prev_read):
        for h in range(H8):
            ww = [first_wait] if h == 0 else []
            if h == 0 and prev_read:
                ww.append((vE, prev_read))
            u1t = p.inc_on("tensor", (lambda e, h=h, CUR=CUR: MM(e, 
                PB[0][0:64, h * 64:(h + 1) * 64], Zsb[:, h * 64:(h + 1) * 64],
                CUR[:, h * 64:(h + 1) * 64], start=True, stop=True)), ww, tE)
        t1v = p.inc_on("vector", (lambda e, CUR=CUR: e.scalar_tensor_tensor(
            NT1[:], CUR[:], 7.0, PB[0][0:64, 0:512], ALU.mult, ALU.subtract)),
            [(tE, u1t)], vE)
        for h in range(H8):
            u2t = p.inc_on("tensor", (lambda e, h=h: MM(e, 
                PB[3][0:64, h * 64:(h + 1) * 64], Zsb[:, h * 64:(h + 1) * 64],
                NT1[:, h * 64:(h + 1) * 64], start=True, stop=True)),
                [(vE, t1v)] if h == 0 else [], tE)
        t2v = p.inc_on("vector", (lambda e, CUR=CUR: e.scalar_tensor_tensor(
            NT1[:], CUR[:], 15.0, PB[3][0:64, 0:512], ALU.mult, ALU.subtract)),
            [(tE, u2t)], vE)
        for h in range(H8):
            u3t = p.inc_on("tensor", (lambda e, h=h: MM(e, 
                PB[0][0:64, h * 64:(h + 1) * 64], Zsb[:, h * 64:(h + 1) * 64],
                NT1[:, h * 64:(h + 1) * 64], start=True, stop=True)),
                [(vE, t2v)] if h == 0 else [], tE)
        rd = p.inc_on("vector", lambda e: e.tensor_scalar(NT1[:], PB[0][0:64, 0:512], 0.25,
                                                          None, ALU.mult), [(tE, u3t)], vE)
        last_v = p.inc_on("vector", (lambda e, CUR=CUR: e.scalar_tensor_tensor(
            CUR[:], CUR[:], 3.25, NT1[:], ALU.mult, ALU.subtract)), [], vE)
        return last_v, rd

    lt_v, z_v = lt0_v, z0_v
    prd = 0
    nits = 1 if stop_after <= 2.55 else 6
    for it in range(nits):
        lt_v, prd = newton_sweep(LTsb, (vE, max(lt_v, z_v)), prd)
        if it < 5:
            z_v, prd = newton_sweep(Zsb, (vE, lt_v), prd)
    if stop_after <= 2.55:
        return _fin()
    LT6 = LTsb

    if stop_after <= 2.6:
        return _fin()
    for h in range(H8):
        ww = [(vE, lt_v), (dN1, k3mv)] if h == 0 else []
        gt = p.inc_on("tensor", (lambda e, h=h: MM(e, 
            PB[1][0:64, h * 64:(h + 1) * 64], LT6[:, h * 64:(h + 1) * 64],
            K3V[:, h * 64:(h + 1) * 64], start=True, stop=True)), ww, tE)
    Gall = WSL2[0:64, 0:512]
    g_vv = p.inc_on("vector", lambda e: e.tensor_copy(Gall[:], PB[1][0:64, 0:512]),
                    [(tE, gt)], vE)
    g_v = 0
    for h in range(H8):
        par, j = h % 2, h // 2
        g_v = p.dma("gpsimd", dN3, Gsb[par * 64:par * 64 + 64, j * 65: j * 65 + 64],
                    Gall[:, h * 64:(h + 1) * 64], waits=[(vE, g_vv)] if h == 0 else [])
    g_v = (dN3, g_v)

    if debug:
        p.dma("gpsimd", dDB, dbg["dbg_qT"].ap(), QT[:], waits=[g_v])
        p.dma("gpsimd", dDB, dbg["dbg_kT"].ap(), KT[:])
        p.dma("gpsimd", dDB, dbg["dbg_v"].ap(), VT[:])
        p.dma("gpsimd", dDB, dbg["dbg_ql"].ap(), qlT[:])
        p.dma("gpsimd", dDB, dbg["dbg_kl"].ap(), klT[:])
        p.dma("gpsimd", dDB, dbg["dbg_K2"].ap(), K2sb[:])
        p.dma("gpsimd", dDB, dbg["dbg_G"].ap(), Gsb[:])
        p.dma("gpsimd", dDB, dbg["dbg_k3v"].ap(), K3V[:])
        p.dma("gpsimd", dDB, dbg["dbg_cb"].ap(), CB[:])
        p.dma("gpsimd", dDB, dbg["dbg_srow"].ap(), SROW[:])
    p.emit(nc)
    if stop_after <= 2:
        for gd in guards.values():
            gd.__exit__(None, None, None)
        nc.finalize()
        return nc

    g_last = g_v
    p = Prog({"scalar": aE, "vector": vE, "gpsimd": gE})
    ctx_done = []
    pay_st = [0] * NT
    rv_l = []
    AGPB2 = WSL2[:, 512:1024]
    for nt in range(NT):
        cbank = PB[(nt % 3) * 2]
        vbank = PB[(nt % 3) * 2 + 1]
        dbank = PB[6 + nt % 2]
        stg = AGPB if nt % 2 == 0 else AGPB2
        ww0 = [(dM2, convp_ld), g_last] if nt < 3 else [(vE, ctx_done[nt - 3])]
        first = True
        for h in range(H8):
            par, j = h % 2, h // 2
            cjs = [cj for cj in range(3)
                   if not ((nt == 0 and cj == 0) or (nt == NT - 1 and cj == 2))]
            for ci, cj in enumerate(cjs):
                stt = (ci == 0)
                lst = (ci == len(cjs) - 1)
                fnc = (lambda e, vbank=vbank, h=h, cj=cj, nt=nt, stt=stt, lst=lst: MM(e, 
                    vbank[:, h * 64:(h + 1) * 64],
                    CONVP[:, (h * 3 + cj) * 128:(h * 3 + cj + 1) * 128],
                    VT[:, (nt - 1 + cj) * 520 + 65 * h: (nt - 1 + cj) * 520 + 65 * h + 64],
                    start=stt, stop=lst))
                if lst:
                    p.inc_on("tensor", fnc, ww0 if first else [], tE)
                else:
                    p.on("tensor", fnc, ww0 if first else [])
                first = False
            p.inc_on("tensor", (lambda e, cbank=cbank, h=h, j=j, par=par, nt=nt: MM(e, 
                cbank[:, h * 64:(h + 1) * 64],
                E1[par * 64:par * 64 + 64, j * 4096 + nt * 128: j * 4096 + nt * 128 + 128],
                Gsb[par * 64:par * 64 + 64, j * 65: j * 65 + 64],
                start=True, stop=True, tile_position=(par * 64, 0))), [], tE)
            fn = (lambda e, dbank=dbank, h=h, j=j, par=par, nt=nt: MM(e, 
                dbank[:, h:h + 1],
                E1[par * 64:par * 64 + 64, j * 4096 + nt * 128: j * 4096 + nt * 128 + 128],
                Gsb[par * 64:par * 64 + 64, j * 65 + 64: j * 65 + 65],
                start=True, stop=True, tile_position=(par * 64, 0)))
            tmm = p.inc_on("tensor", fn,
                           [(vE, rv_l[nt - 2])] if (h == 0 and nt >= 2) else [], tE)
        rv = p.inc_on("vector", (lambda e, dbank=dbank: e.reciprocal(SMB[:, 40:48], dbank[:, 0:8])),
                      [(tE, tmm)], vE)
        rv_l.append(rv)
        cn = p.inc_on("vector", (lambda e, cbank=cbank: e.tensor_tensor(
            EVAC[:, 0:512].rearrange("p (a b) -> p a b", b=64),
            cbank[:, 0:512].rearrange("p (a b) -> p a b", b=64),
            SMB[:, 40:48].rearrange("p (a b) -> p a b", b=1).broadcast_to((128, 8, 64)),
            ALU.mult)), [(vE, rv)], vE)
        dPs = dP if nt % 2 == 0 else dPb
        fl = p.inc_on("vector", (lambda e, vbank=vbank, stg=stg: e.scalar_tensor_tensor(
            stg[:], vbank[:, 0:512], 1.0, EVAC[:, 0:512], ALU.mult, ALU.add)),
            [(vE, cn)] + ([(dPs, pay_st[nt - 2])] if nt >= 2 else []), vE)
        ctx_done.append(fl)
        pay_st[nt] = p.dma("gpsimd", dPs, ctxme_d.ap()[nt * 128:(nt + 1) * 128, :], stg[:],
                           waits=[(vE, fl)])
    ag_ctx = p.nochain("gpsimd", lambda e: e.collective_compute(
        "AllGather", ALU.bypass, replica_groups=[[0, 1], [2, 3], [4, 5], [6, 7]],
        ins=[ctxme_d.ap().opt()], outs=[ctxab_d.ap().opt()]),
        [(dP, pay_st[-2]), (dPb, pay_st[-1])], cx)
    p.emit(nc)
    if stop_after <= 3:
        for gd in guards.values():
            gd.__exit__(None, None, None)
        nc.finalize()
        return nc

    p = Prog({"scalar": aE, "vector": vE, "gpsimd": gE})
    dbg_sent = [(dDB, dDB.n)] if dDB.n else []
    wg_ld = 0
    for kk in range(8):
        wg_ld = p.dma("sync", dWG, XT[:, kk * 4096:(kk + 1) * 4096],
                      agg_o.ap()[kk * 128:(kk + 1) * 128, :],
                      waits=([(cw, 4)] + dbg_sent) if kk == 0 else [])
    wu_ld = 0
    for kk in range(8):
        dstT = QT if kk < 4 else KT
        wu_ld = p.dma("scalar", dWU, dstT[:, (kk % 4) * 4096:((kk % 4) + 1) * 4096],
                      agu_o.ap()[kk * 128:(kk + 1) * 128, :],
                      waits=([(cw, 4)] + dbg_sent) if kk == 0 else [])

    regs = {}

    def _mkregs(e):
        regs["rme"] = e.value_load(OFFS[0:1, 0:1])
        regs["rpa"] = e.value_load(OFFS[0:1, 1:2])
        regs["cme"] = e.value_load(OFFS[0:1, 2:3])
        regs["cot"] = e.value_load(OFFS[0:1, 3:4])
        return e.engine_nop()
    p.inc_on("gpsimd", _mkregs, [(cx, ag_ctx)], gE)

    x1_st = [0] * NTO
    asm_v = [0] * NTO
    sq_l = [0] * NTO
    for t in range(NTO):
        slot = t % 2
        xb = XTI[:, slot * 1024:(slot + 1) * 1024]
        cbuf = WVB[:, slot * 1024:(slot + 1) * 1024]
        w = [(vE, asm_v[t - 2]), (aE, sq_l[t - 2]), (dSx, x1_st[t - 2])] if t >= 2 else []
        xv = p.dma("sync", dX[slot], xb, x_own.ap()[t * 128:(t + 1) * 128, :], waits=w)
        def mkld(dst_cols_reg, src_dram, row_reg, t=t, cbuf=cbuf):
            def fn(e, dst_cols_reg=dst_cols_reg, src_dram=src_dram, row_reg=row_reg):
                return e.dma_start(cbuf[:, ds(regs[dst_cols_reg], 512)],
                                   src_dram.ap()[ds(regs[row_reg] + t * 128, 128), :])
            return fn
        wl = [(gE, gE.n)] if t < 2 else [(vE, asm_v[t - 2])]
        p.inc_on("gpsimd", mkld("cme", ctxme_d, "rme"), wl, dCX[slot], 16)
        l2 = p.inc_on("gpsimd", mkld("cot", ctxab_d, "rpa"), [], dCX[slot], 16)
        av = p.inc_on("vector", (lambda e, xb=xb, cbuf=cbuf, t=t: e.scalar_tensor_tensor(
            xb, cbuf, 1.0, xb, ALU.mult, ALU.add, accum_out=st(SS_SUM, t))),
            [(dX[slot], xv), (dCX[slot], l2)], vE)
        sq_l[t] = p.inc_on("scalar", (lambda e, xb=xb, t=t: e.activation(
            EVAC[:], xb, AF.Square, accum_out=st(SS_SSQ, t))), [(vE, av)], aE)
        p.on("vector", (lambda e, xb=xb, t=t: e.reduce_max(st(SS_MAX, t), xb, axis=AX.X)), [])
        asm_v[t] = p.inc_on("vector", (lambda e, xb=xb, t=t: e.tensor_reduce(
            st(SS_MIN, t), xb, axis=AX.X, op=ALU.min)), [], vE)
        x1_st[t] = p.dma("scalar", dSx, x1_d.ap()[t * 128:(t + 1) * 128, :], xb,
                         waits=[(vE, asm_v[t]), (aE, aE.n)] +
                               ([(dSx, x1_st[t - 1])] if t >= 1 else []))
    x1_all_a = aE.n

    p.on("vector", lambda e: e.tensor_scalar(st(SS_M, 0, 16), st(SS_SUM, 0, 16), 1.0 / DM,
                                             None, ALU.mult), [(vE, asm_v[-1]), (aE, x1_all_a)])
    p.on("vector", lambda e: e.tensor_scalar(st(SS_RSTD, 0, 16), st(SS_SSQ, 0, 16), 1.0 / DM,
                                             None, ALU.mult), [])
    p.on("vector", lambda e: e.scalar_tensor_tensor(st(SS_AA, 0, 16), st(SS_M, 0, 16), 1.0,
                                                    st(SS_M, 0, 16), ALU.mult, ALU.mult), [])
    sv = p.inc_on("vector", lambda e: e.tensor_tensor(st(SS_RSTD, 0, 16), st(SS_RSTD, 0, 16),
                                                      st(SS_AA, 0, 16), ALU.subtract), [], vE)
    sqa = p.inc_on("scalar", lambda e: e.activation(st(SS_AA, 0, 16), st(SS_RSTD, 0, 16),
                                                    AF.Sqrt, bias=CONSTS[:, 0:1]), [(vE, sv)], aE)

    _s2ops = [
        lambda e: e.reciprocal(st(SS_RSTD, 0, 16), st(SS_AA, 0, 16)),
        lambda e: e.tensor_tensor(st(SS_AA, 0, 16), st(SS_MAX, 0, 16), st(SS_M, 0, 16), ALU.subtract),
        lambda e: e.scalar_tensor_tensor(st(SS_BBM, 0, 16), st(SS_MIN, 0, 16), -1.0, st(SS_M, 0, 16),
                                         ALU.mult, ALU.add),
        lambda e: e.tensor_tensor(st(SS_AA, 0, 16), st(SS_AA, 0, 16), st(SS_BBM, 0, 16), ALU.max),
        lambda e: e.tensor_tensor(st(SS_AA, 0, 16), st(SS_AA, 0, 16), st(SS_RSTD, 0, 16), ALU.mult),
        lambda e: e.tensor_scalar(st(SS_AA, 0, 16), st(SS_AA, 0, 16), 1e-5, None, ALU.max),
        lambda e: e.reciprocal(st(SS_S2Q, 0, 16), st(SS_AA, 0, 16)),
        lambda e: e.tensor_scalar(st(SS_S2Q, 0, 16), st(SS_S2Q, 0, 16), 127.0, None, ALU.mult),
        lambda e: e.tensor_tensor(st(SS_AA, 0, 16), st(SS_RSTD, 0, 16), st(SS_S2Q, 0, 16), ALU.mult),
        lambda e: e.scalar_tensor_tensor(st(SS_BBM, 0, 16), st(SS_M, 0, 16), -1.0, st(SS_AA, 0, 16),
                                         ALU.mult, ALU.mult),
        lambda e: e.reciprocal(st(SS_IG, 0, 16), st(SS_S2Q, 0, 16)),
        lambda e: e.tensor_scalar(st(SS_IU, 0, 16), st(SS_IG, 0, 16), SMA[:, 17:18], -1.0, ALU.mult, ALU.mult),
    ]
    for ii, op_ in enumerate(_s2ops):
        p.on("vector", op_, [(aE, sqa)] if ii == 0 else [])
    st2 = p.inc_on("vector", lambda e: e.tensor_scalar(st(SS_IG, 0, 16), st(SS_IG, 0, 16),
                                                       SMA[:, 16:17], -1.0, ALU.mult, ALU.mult),
                   [], vE)

    ra_l = []
    rv_l = []
    xi_st = [0] * NTO
    x1all_sx = dSx.n
    for t in range(NTO):
        slot = t % 2
        xb = XTI[:, slot * 1024:(slot + 1) * 1024]
        zb = WVB[:, slot * 1024:(slot + 1) * 1024]
        w = [(vE, st2), (dSx, x1all_sx)]
        if t >= 2:
            w.append((aE, ra_l[t - 2]))
        xv = p.dma("sync", dX[slot], xb, x1_d.ap()[t * 128:(t + 1) * 128, :], waits=w)
        ra = p.inc_on("scalar", (lambda e, xb=xb, t=t: e.activation(
            EVAC[:], xb, AF.Identity, bias=st(SS_BBM, t), scale=st(SS_AA, t))),
            [(dX[slot], xv)] + ([(vE, rv_l[t - 1])] if t >= 1 else []), aE)
        ra_l.append(ra)
        rv = p.inc_on("vector", (lambda e, zb=zb: e.tensor_scalar(
            zb, EVAC[:], MAGIC, -MAGIC, ALU.add, ALU.add)),
            [(aE, ra)] + ([(dSi, xi_st[t - 2])] if t >= 2 else []), vE)
        rv_l.append(rv)
        xi_st[t] = p.dma("gpsimd", dSi, xint_d.ap()[t * 128:(t + 1) * 128, :], zb,
                         waits=[(vE, rv)] + ([(dSi, xi_st[t - 1])] if t >= 1 else []))
    for f in range(8):
        p.dma("sync", dT2, VT[:, f * 2048:(f + 1) * 2048],
              xint_d.ap()[:, f * 128:(f + 1) * 128], transpose=True,
              waits=[(dSi, xi_st[-1])] if f == 0 else [])
    xiT_all = [(dT2, dT2.n)]

    zi_st = [0] * NTO
    s2q_v = [0] * NTO
    gaL, uaL, zvL = [], [], []
    for t in range(NTO):
        gu_done = []
        for n in range(8):
            bka = PB[0 + (n % 2)]
            bkb = PB[2 + (n % 2)]
            gsl = WQS[:, (n % 2) * 1024:(n % 2) * 1024 + 512]
            usl = WQS[:, (n % 2) * 1024 + 512:(n % 2) * 1024 + 1024]
            gi = t * 8 + n
            for kk in range(8):
                ww = []
                if kk == 0:
                    if gi == 0:
                        ww = list(xiT_all) + [(dWG, wg_ld), (dWU, wu_ld), (dSi, xi_st[-1])]
                    if gi >= 2:
                        ww.append((aE, gaL[gi - 2]))
                fng = (lambda e, bka=bka, kk=kk, t=t, n=n: MM(e, 
                    bka[:], VT[:, kk * 2048 + t * 128: kk * 2048 + t * 128 + 128],
                    XT[:, kk * 4096 + n * 512: kk * 4096 + (n + 1) * 512],
                    start=(kk == 0), stop=(kk == 7)))
                if kk == 7:
                    p.inc_on("tensor", fng, ww, tE)
                else:
                    p.on("tensor", fng, ww)
            for kk in range(8):
                srcW = QT if kk < 4 else KT
                ww = []
                if kk == 0 and gi >= 2:
                    ww.append((aE, uaL[gi - 2]))
                fn = (lambda e, bkb=bkb, kk=kk, t=t, n=n, srcW=srcW: MM(e, 
                    bkb[:], VT[:, kk * 2048 + t * 128: kk * 2048 + t * 128 + 128],
                    srcW[:, (kk % 4) * 4096 + n * 512: (kk % 4) * 4096 + (n + 1) * 512],
                    start=(kk == 0), stop=(kk == 7)))
                if kk == 7:
                    up_t = p.inc_on("tensor", fn, ww, tE)
                else:
                    p.on("tensor", fn, ww)
            ga = p.inc_on("scalar", (lambda e, bka=bka, gsl=gsl, t=t: e.activation(
                gsl, bka[:], AF.Silu, scale=st(SS_IG, t))),
                [(tE, up_t)] + ([(vE, zvL[gi - 2])] if gi >= 2 else []), aE)
            ua = p.inc_on("scalar", (lambda e, bkb=bkb, usl=usl, t=t: e.activation(
                usl, bkb[:], AF.Copy, scale=st(SS_IU, t))), [], aE)
            zv = p.inc_on("vector", (lambda e, gsl=gsl, usl=usl, n=n: e.tensor_tensor(
                WVB[:, n * 512:(n + 1) * 512], gsl, usl, ALU.mult)),
                [(aE, ua)] + ([(dZ, zi_st[t - 1])] if (t >= 1 and n == 0) else []), vE)
            gu_done.append(zv)
            zvL.append(zv)
            gaL.append(ga)
            uaL.append(ua)
        p.on("vector", (lambda e: e.tensor_reduce(
            SMB[:, 0:64], WVB[:].rearrange("p (a b) -> p a b", b=64), axis=AX.X,
            op=ALU.max, apply_absolute_value=True)), [(vE, gu_done[-1])])
        p.on("vector", lambda e: e.reduce_max(SMB[:, 32:33], SMB[:, 0:64], axis=AX.X,
                                              apply_absolute_value=True), [])
        p.on("vector", lambda e: e.tensor_scalar(SMB[:, 32:33], SMB[:, 32:33], 1e-5, None, ALU.max), [])
        p.on("vector", lambda e: e.reciprocal(SMB[:, 33:34], SMB[:, 32:33]), [])
        s2q_v[t] = p.inc_on("vector", (lambda e, t=t: e.tensor_scalar(
            st(SS_S2Q, t), SMB[:, 33:34], 127.0, None, ALU.mult)), [], vE)
        qrv = 0
        for q4 in range(4):
            ra = p.inc_on("scalar", (lambda e, q4=q4, t=t: e.activation(
                EVAC[:], WVB[:, q4 * 1024:(q4 + 1) * 1024], AF.Identity, bias=CONSTS[:, 1:2],
                scale=st(SS_S2Q, t))),
                [(vE, s2q_v[t])] + ([(vE, qrv)] if q4 > 0 else []), aE)
            qrv = p.inc_on("vector", (lambda e, q4=q4: e.tensor_scalar(
                WVB[:, q4 * 1024:(q4 + 1) * 1024], EVAC[:], -MAGIC, None, ALU.add)),
                [(aE, ra)], vE)
        zq_v = vE.n
        zi_st[t] = p.dma("gpsimd", dZ, zint_d.ap()[t * 128:(t + 1) * 128, :], WVB[:],
                         waits=[(vE, zq_v)] + ([(dZ, zi_st[t - 1])] if t >= 1 else []))
    p.on("vector", lambda e: e.reciprocal(st(SS_IG, 0, 16), st(SS_S2Q, 0, 16)), [(vE, s2q_v[-1])])
    id_v = p.inc_on("vector", lambda e: e.tensor_scalar(st(SS_ID, 0, 16), st(SS_IG, 0, 16),
                                                        SMA[:, 18:19], -1.0, ALU.mult, ALU.mult),
                    [], vE)
    if debug:
        p.dma("gpsimd", dDB, dbg["dbg_stat"].ap(), STAT[:], waits=[(vE, id_v)])
    p.emit(nc)
    if stop_after <= 4:
        for gd in guards.values():
            gd.__exit__(None, None, None)
        nc.finalize()
        return nc

    p = Prog({"scalar": aE, "vector": vE, "gpsimd": gE})
    for f in range(32):
        if f < 16:
            dst = XT[:, f * 2048:(f + 1) * 2048]
        elif f < 24:
            dst = QT[:, (f - 16) * 2048:(f - 15) * 2048]
        else:
            dst = KT[:, (f - 24) * 2048:(f - 23) * 2048]
        p.dma("sync", dT3 if f < 16 else dT3b, dst,
              zint_d.ap()[:, f * 128:(f + 1) * 128], transpose=True,
              waits=[(dZ, dZ.n)] if f == 0 else [])
    zT_half1 = [(dT3, dT3.n)]
    zT_half2 = [(dT3b, dT3b.n)]

    def zT_chunk(kk):
        if kk < 16:
            return XT[:, kk * 2048:(kk + 1) * 2048]
        if kk < 24:
            return QT[:, (kk - 16) * 2048:(kk - 15) * 2048]
        return KT[:, (kk - 24) * 2048:(kk - 23) * 2048]

    dOs = [sems["dO"], sems["dM"]]
    ost_prev = [0, 0]
    ev_prev = [0] * 4
    x1all_sx2 = dSx.n
    for half in range(2):
        wd_ld = 0
        for kk in range(32):
            wd_ld = p.dma("gpsimd", dV2, VT[:, kk * 512:(kk + 1) * 512],
                          agd_o.ap()[kk * 128:(kk + 1) * 128, half * 512:(half + 1) * 512],
                          waits=([(tE, tE.n)] if (half == 1 and kk == 0) else
                                 ([(cw, 4)] if (half == 0 and kk == 0) else [])))
        for t in range(NTO):
            bank = PB[t % 4]
            oslot = t % 2
            xb = XTI[:, (t % 4) * 1024:(t % 4) * 1024 + 512]
            xv = p.dma("sync" if t % 2 == 0 else "scalar", dX[t % 4], xb,
                       x1_d.ap()[t * 128:(t + 1) * 128, half * 512:(half + 1) * 512],
                       waits=([(vE, ev_prev[t % 4])] if ev_prev[t % 4] else []) +
                             ([(dSx, x1all_sx2)] if (half == 0 and t < 4) else []))
            for kk in range(32):
                ww = []
                if kk == 0:
                    ww = list(zT_half1) + [(dV2, wd_ld)]
                    if t >= 4:
                        ww.append((vE, ev_prev[t % 4]))
                elif kk == 16:
                    ww = list(zT_half2)
                fn = (lambda e, bank=bank, kk=kk, t=t: MM(e, 
                    bank[:], zT_chunk(kk)[:, t * 128:(t + 1) * 128],
                    VT[:, kk * 512:(kk + 1) * 512],
                    start=(kk == 0), stop=(kk == 31)))
                if kk == 31:
                    mm_v = p.inc_on("tensor", fn, ww, tE)
                else:
                    p.on("tensor", fn, ww)
            ev = p.inc_on("vector", (lambda e, bank=bank, xb=xb, t=t, oslot=oslot: e.scalar_tensor_tensor(
                EVAC[:, oslot * 512:(oslot + 1) * 512], bank[:], st(SS_ID, t), xb,
                ALU.mult, ALU.add)),
                [(tE, mm_v), (dX[t % 4], xv)] + ([(dOs[oslot], ost_prev[oslot])] if ost_prev[oslot] else []), vE)
            ev_prev[t % 4] = ev
            ost_prev[oslot] = p.dma("gpsimd", dOs[oslot],
                                    out_d.ap()[t * 128:(t + 1) * 128, half * 512:(half + 1) * 512],
                                    EVAC[:, oslot * 512:(oslot + 1) * 512], waits=[(vE, ev)])
    f0, f1 = dOs[0].n, dOs[1].n
    p.on("sync", lambda e: (e.wait_ge(dOs[0].sem, f0), e.wait_ge(dOs[1].sem, f1))[-1], [])
    if debug:
        p.dma("gpsimd", dDB, dbg["dbg_ctxme"].ap(), ctxme_d.ap(), waits=[(dOs[0], f0), (dOs[1], f1)])
        p.dma("gpsimd", dDB, dbg["dbg_x1"].ap(), x1_d.ap())
        p.dma("gpsimd", dDB, dbg["dbg_xint"].ap(), xint_d.ap())
        p.dma("gpsimd", dDB, dbg["dbg_zint"].ap(), zint_d.ap())
        p.dma("gpsimd", dDB, dbg["dbg_wgT"].ap(), agg_o.ap())
        fdb = dDB.n
        p.on("gpsimd", lambda e: e.wait_ge(dDB.sem, fdb), [])
    p.emit(nc)

    for gd in guards.values():
        gd.__exit__(None, None, None)
    nc.finalize()
    return nc


_NC_CACHE = {}


def _get_nc(debug=False):
    key = bool(debug)
    if key not in _NC_CACHE:
        _NC_CACHE[key] = build_nc(debug=debug)
    return _NC_CACHE[key]


def _build_conv_patterns(conv_w, s):
    w = np.asarray(conv_w, np.float32).reshape(H, KSZ)
    out = np.zeros((H8 * 3 * 128, 128), np.float32)
    i_idx = np.arange(128)[:, None]
    t_idx = np.arange(128)[None, :]
    for hh in range(H8):
        hcol = s * H8 + hh
        for cj in range(3):
            kidx = 128 * (cj - 1) + i_idx - t_idx + 32
            valid = (kidx >= 0) & (kidx < KSZ)
            blk = np.zeros((128, 128), np.float32)
            blk[valid] = w[hcol][kidx[valid]]
            out[(hh * 3 + cj) * 128:(hh * 3 + cj + 1) * 128, :] = blk
    return out


def kernel(L0_attn_hidden_states, L0_attn_Wq, L0_attn_bq, L0_attn_Wk, L0_attn_bk,
           L0_attn_Wv, L0_attn_bv, L0_attn_conv_w, L0_ffn_w_gate, L0_ffn_w_up,
           L0_ffn_w_down, _debug=False):
    x = np.ascontiguousarray(np.asarray(L0_attn_hidden_states, np.float32))
    Wq = np.asarray(L0_attn_Wq, np.float32)
    Wk = np.asarray(L0_attn_Wk, np.float32)
    Wv = np.asarray(L0_attn_Wv, np.float32)
    bq = np.asarray(L0_attn_bq, np.float32)
    bk = np.asarray(L0_attn_bk, np.float32)
    bv = np.asarray(L0_attn_bv, np.float32)
    wg = np.asarray(L0_ffn_w_gate, np.float32)
    wu = np.asarray(L0_ffn_w_up, np.float32)
    wd = np.asarray(L0_ffn_w_down, np.float32)
    eye2 = np.vstack([np.eye(64, dtype=np.float32)] * 2)

    in_maps = []
    for c in range(8):
        b, s = c // 2, c % 2
        cols = slice(s * HD, (s + 1) * HD)
        offs = np.zeros((1, 8), np.int32)
        offs[0, 0] = s * TOK
        offs[0, 1] = (1 - s) * S + s * TOK
        offs[0, 2] = s * 512
        offs[0, 3] = (1 - s) * 512
        in_maps.append({
            "x_in": x[b],
            "x_own": np.ascontiguousarray(x[b][s * TOK:(s + 1) * TOK]),
            "wq_in": np.ascontiguousarray(Wq[:, cols]),
            "wk_in": np.ascontiguousarray(Wk[:, cols]),
            "wv_in": np.ascontiguousarray(Wv[:, cols]),
            "bq_in": np.ascontiguousarray(bq[cols].reshape(4, 128).T),
            "bk_in": np.ascontiguousarray(bk[cols].reshape(4, 128).T),
            "bv_in": np.ascontiguousarray(bv[cols].reshape(1, HD)),
            "convp_in": _build_conv_patterns(L0_attn_conv_w, s),
            "wg_sl": np.ascontiguousarray(wg[:, c * 128:(c + 1) * 128].T),
            "wu_sl": np.ascontiguousarray(wu[:, c * 128:(c + 1) * 128].T),
            "wd_sl": np.ascontiguousarray(wd[:, c * 512:(c + 1) * 512].T),
            "eye2_in": eye2,
            "offs_in": offs,
        })

    nc = _get_nc(debug=_debug)
    res = run_bass_kernel_spmd(nc, in_maps, core_ids=list(range(8)))

    out = np.empty((B, S, DM), np.float32)
    for c in range(8):
        b, s = c // 2, c % 2
        out[b, s * TOK:(s + 1) * TOK, :] = res.results[c]["out_d"]
    if _debug:
        kernel._dbg = res.results
    return out


# version 09
# speedup vs baseline: 1.0267x; 1.0027x over previous
import numpy as np
import ml_dtypes
import concourse.bass as bass
import concourse.bacc as bacc
import concourse.mybir as mybir
from concourse.bass_utils import run_bass_kernel_spmd

F32 = mybir.dt.float32
BF16 = mybir.dt.bfloat16
I32 = mybir.dt.int32
AF = mybir.ActivationFunctionType
ALU = mybir.AluOpType
AX = mybir.AxisListType
bfloat16 = ml_dtypes.bfloat16
ds = bass.ds

B, S, DM = 4, 4096, 1024
H, D, L, KSZ = 16, 64, 64, 65
H8, HD = 8, 512
NT = 32
TOK, NTO = 2048, 16
INNER = 4096
SEG = S // L
QSCALE = float(D) ** (-0.25)
MAGIC = float(np.float32(12582912.0))
EPS = 1e-5
NEGI = -3.0e38
ENGS = ("sync", "scalar", "tensor", "vector", "gpsimd")


def MM(e, *a, **kw):
    kw.setdefault("skip_group_check", True)
    return e.matmul(*a, **kw)


class Ctr:
    def __init__(self, sem):
        self.sem = sem
        self.n = 0

    def inc(self, k=1):
        self.n += k
        return self.n


class Prog:

    def __init__(self, eng_ctrs):
        self.steps = {e: [] for e in ENGS}
        self.eng_ctrs = eng_ctrs
        self.marks = {"scalar": None, "vector": None, "gpsimd": None}

    def _push(self, eng, fn, waits, ctr, amt, chain=True):
        waits = list(waits)
        if eng in self.marks and chain:
            if self.marks[eng] is not None:
                waits.append(self.marks[eng])
            if ctr is None:
                ctr, amt = self.eng_ctrs[eng], 1
        self.steps[eng].append((fn, waits, ctr, amt))
        v = ctr.inc(amt) if ctr is not None else None
        if eng in self.marks and chain and ctr is not None:
            self.marks[eng] = (ctr, v)
        return v

    def on(self, eng, fn, waits=()):
        self._push(eng, fn, waits, None, 0)

    def inc_on(self, eng, fn, waits, ctr, amt=1):
        return self._push(eng, fn, waits, ctr, amt)

    def nochain(self, eng, fn, waits, ctr, amt=1):
        return self._push(eng, fn, waits, ctr, amt, chain=False)

    def dma(self, eng, ctr, out, in_, waits=(), **kw):
        def fn(e, out=out, in_=in_, kw=kw):
            return e.dma_start(out, in_, **kw)
        return self._push(eng, fn, waits, ctr, 16, chain=False)

    def emit(self, nc):
        with nc.Block() as blk:
            for name in ENGS:
                steps = self.steps[name]
                if not steps:
                    continue

                def body(e, steps=steps):
                    for fn, waits, inc, amt in steps:
                        for (c, v) in waits:
                            if v > 0:
                                e.wait_ge(c.sem, v)
                        ins = fn(e)
                        if inc is not None:
                            ins.then_inc(inc.sem, amt)
                getattr(blk, name)(body)


def build_nc(debug=False, stop_after=99):
    nc = bacc.Bacc("TRN2")

    x_in = nc.dram_tensor("x_in", [S, DM], F32, kind="ExternalInput")
    x_own = nc.dram_tensor("x_own", [TOK, DM], F32, kind="ExternalInput")
    wq_in = nc.dram_tensor("wq_in", [DM, HD], F32, kind="ExternalInput")
    wk_in = nc.dram_tensor("wk_in", [DM, HD], F32, kind="ExternalInput")
    wv_in = nc.dram_tensor("wv_in", [DM, HD], F32, kind="ExternalInput")
    bq_in = nc.dram_tensor("bq_in", [128, 4], F32, kind="ExternalInput")
    bk_in = nc.dram_tensor("bk_in", [128, 4], F32, kind="ExternalInput")
    bv_in = nc.dram_tensor("bv_in", [1, HD], F32, kind="ExternalInput")
    convp_in = nc.dram_tensor("convp_in", [H8 * 3 * 128, 128], F32, kind="ExternalInput")
    wg_sl = nc.dram_tensor("wg_sl", [128, INNER], F32, kind="ExternalInput")
    wu_sl = nc.dram_tensor("wu_sl", [128, INNER], F32, kind="ExternalInput")
    wd_sl = nc.dram_tensor("wd_sl", [512, DM], F32, kind="ExternalInput")
    eye2_in = nc.dram_tensor("eye2_in", [128, 64], F32, kind="ExternalInput")
    offs_in = nc.dram_tensor("offs_in", [1, 8], I32, kind="ExternalInput")
    out_d = nc.dram_tensor("out_d", [TOK, DM], F32, kind="ExternalOutput")

    xln_d = nc.dram_tensor("xln_d", [S, DM], BF16, kind="Internal")
    x1_d = nc.dram_tensor("x1_d", [TOK, DM], F32, kind="Internal")
    xint_d = nc.dram_tensor("xint_d", [TOK, DM], BF16, kind="Internal")
    zint_d = nc.dram_tensor("zint_d", [TOK, INNER], BF16, kind="Internal")
    wsum_i = nc.dram_tensor("wsum_i", [1, 8], F32, kind="Internal")
    wsum_o = nc.dram_tensor("wsum_o", [1, 8], F32, kind="Internal")
    amax_i = nc.dram_tensor("amax_i", [1, 8], F32, kind="Internal")
    amax_o = nc.dram_tensor("amax_o", [1, 8], F32, kind="Internal")
    agg_i = nc.dram_tensor("agg_i", [128, INNER], BF16, kind="Internal")
    agg_o = nc.dram_tensor("agg_o", [DM, INNER], BF16, kind="Internal")
    agu_i = nc.dram_tensor("agu_i", [128, INNER], BF16, kind="Internal")
    agu_o = nc.dram_tensor("agu_o", [DM, INNER], BF16, kind="Internal")
    agd_i = nc.dram_tensor("agd_i", [512, DM], BF16, kind="Internal")
    agd_o = nc.dram_tensor("agd_o", [INNER, DM], BF16, kind="Internal")
    ctxme_d = nc.dram_tensor("ctxme_d", [S, HD], BF16, kind="Internal")
    ctxab_d = nc.dram_tensor("ctxab_d", [2 * S, HD], BF16, kind="Internal")

    dbg = {}
    if debug:
        for name, shape, dt_ in [
            ("dbg_qT", [128, 16384], BF16), ("dbg_kT", [128, 16384], BF16),
            ("dbg_v", [128, NT * 520], BF16), ("dbg_ql", [128, 256], BF16),
            ("dbg_kl", [128, 256], BF16), ("dbg_K2", [128, 256], F32),
            ("dbg_G", [128, 4 * 65], BF16), ("dbg_k3v", [128, 256], F32),
            ("dbg_cb", [128, 16], F32), ("dbg_srow", [1, 64], F32),
            ("dbg_stat", [128, 192], F32), ("dbg_ctxme", [S, HD], BF16),
            ("dbg_x1", [TOK, DM], F32), ("dbg_xint", [TOK, DM], BF16),
            ("dbg_zint", [TOK, INNER], BF16), ("dbg_wgT", [DM, INNER], BF16),
        ]:
            dbg[name] = nc.dram_tensor(name, shape, dt_, kind="ExternalOutput")

    XT = nc.alloc_sbuf_tensor("XT", [128, 32768], BF16)
    QT = nc.alloc_sbuf_tensor("QT", [128, 16384], BF16)
    KT = nc.alloc_sbuf_tensor("KT", [128, 16384], BF16)
    VT = nc.alloc_sbuf_tensor("VT", [128, 16640], BF16)
    WVB = nc.alloc_sbuf_tensor("WVB", [128, 4096], BF16)
    WQS = nc.alloc_sbuf_tensor("WQS", [128, 2048], BF16)
    WSL2 = nc.alloc_sbuf_tensor("WSL2", [128, 1024], BF16)
    XTI = nc.alloc_sbuf_tensor("XTI", [128, 4096], F32)
    EVAC = nc.alloc_sbuf_tensor("EVAC", [128, 1024], F32)
    AGPB = nc.alloc_sbuf_tensor("AGPB", [128, 512], BF16)
    qlT = nc.alloc_sbuf_tensor("qlT", [128, 256], BF16)
    klT = nc.alloc_sbuf_tensor("klT", [128, 256], BF16)
    K2sb = nc.alloc_sbuf_tensor("K2sb", [128, 256], F32)
    Zsb = nc.alloc_sbuf_tensor("Zsb", [64, 512], F32)
    LTsb = nc.alloc_sbuf_tensor("LTsb", [64, 512], F32)
    K3V = nc.alloc_sbuf_tensor("K3V", [64, 512], F32)
    Gsb = nc.alloc_sbuf_tensor("Gsb", [128, 4 * 65], BF16)
    EYE2 = nc.alloc_sbuf_tensor("EYE2", [128, 64], F32)
    ONEB = nc.alloc_sbuf_tensor("ONEB", [1, 128], BF16)
    ONEF = nc.alloc_sbuf_tensor("ONEF", [128, 1], F32)
    ONER = nc.alloc_sbuf_tensor("ONER", [1, 128], F32)
    BQS = nc.alloc_sbuf_tensor("BQS", [128, 4], F32)
    BKS = nc.alloc_sbuf_tensor("BKS", [128, 4], F32)
    BVB = nc.alloc_sbuf_tensor("BVB", [1, HD], BF16)
    CB = nc.alloc_sbuf_tensor("CB", [128, 16], F32)
    SROW = nc.alloc_sbuf_tensor("SROW", [1, 64], F32)
    STAT = nc.alloc_sbuf_tensor("STAT", [128, 192], F32)
    SMA = nc.alloc_sbuf_tensor("SMA", [128, 32], F32)
    SMB = nc.alloc_sbuf_tensor("SMB", [128, 64], F32)
    OFFS = nc.alloc_sbuf_tensor("OFFS", [1, 8], I32)
    CONSTS = nc.alloc_sbuf_tensor("CONSTS", [128, 2], F32)
    PB = [nc.alloc_psum_tensor(f"PB{i}", [128, 512], F32) for i in range(8)]

    XLB = [QT[:, 0:1024], QT[:, 1024:2048]]
    CONVP = XT[:, 16384:16384 + 3072]
    E1 = XT
    EYE128 = EVAC[:, 512:640]
    NCB = BVB

    sem_names = ["dX0", "dX1", "dX2", "dX3", "dS0", "dS1", "dM", "dT", "dW0", "dW1",
                 "dV", "dL", "dF0", "dF1", "dF2", "dE", "dP", "dWG", "dWU", "dSx",
                 "dSi", "dZ", "dT2", "dT3", "dV2", "dO", "dCX0", "dCX1", "dM2", "dDB",
                 "dN1", "dN2", "dN3", "dT3b", "dPb", "dCX2", "dCX3",
                 "tE", "vE", "aE", "gE", "cw", "cm", "cx"]
    guards = {n: nc.semaphore(n) for n in sem_names}
    sems = {n: Ctr(gd.__enter__()) for n, gd in guards.items()}
    dX = [sems["dX0"], sems["dX1"], sems["dX2"], sems["dX3"]]
    dS = [sems["dS0"], sems["dS1"]]
    dM, dT, dV, dL, dE, dP = (sems[k] for k in ("dM", "dT", "dV", "dL", "dE", "dP"))
    dW = [sems["dW0"], sems["dW1"]]
    dF3 = [sems["dF0"], sems["dF1"], sems["dF2"]]
    dWG, dWU, dSx, dSi, dZ = (sems[k] for k in ("dWG", "dWU", "dSx", "dSi", "dZ"))
    dT2, dT3, dV2, dO = (sems[k] for k in ("dT2", "dT3", "dV2", "dO"))
    dCX = [sems["dCX0"], sems["dCX1"], sems["dCX2"], sems["dCX3"]]
    dM2 = sems["dM2"]
    dDB = sems["dDB"]
    dN1, dN2, dN3 = sems["dN1"], sems["dN2"], sems["dN3"]
    dT3b = sems["dT3b"]
    dPb = sems["dPb"]
    dOs = None
    tE, vE, aE, gE = (sems[k] for k in ("tE", "vE", "aE", "gE"))
    cw, cm, cx = (sems[k] for k in ("cw", "cm", "cx"))

    (SS_SUM, SS_SSQ, SS_MAX, SS_MIN, SS_M, SS_RSTD, SS_AA, SS_BBM,
     SS_S2Q, SS_IG, SS_IU, SS_ID) = range(12)

    def st(jj, t=0, w=1):
        return STAT[:, jj * 16 + t: jj * 16 + t + w]

    p = Prog({"scalar": aE, "vector": vE, "gpsimd": gE})
    for dst, src in [(EYE2[:], eye2_in.ap()), (BQS[:], bq_in.ap()), (BKS[:], bk_in.ap()),
                     (BVB[:], bv_in.ap()), (OFFS[:], offs_in.ap())]:
        p.dma("gpsimd", dM, dst, src)
    dM_all = dM.n

    for _mfn in [lambda e: e.memset(ONEB[:], 1.0), lambda e: e.memset(ONEF[:], 1.0),
                 lambda e: e.memset(ONER[:], 1.0), lambda e: e.memset(Gsb[:], 1.0),
                 lambda e: e.memset(SMA[:], NEGI), lambda e: e.memset(CB[:], 0.0),
                 lambda e: e.memset(SROW[:], 0.0), lambda e: e.memset(STAT[:], 0.0),
                 lambda e: e.memset(CONSTS[:, 0:1], EPS),
                 lambda e: e.memset(CONSTS[:, 1:2], MAGIC)]:
        p.on("gpsimd", _mfn, [])
    ms_g = p.inc_on("gpsimd", lambda e: e.memset(SMB[:], 0.0), [], gE)

    nrm_done = [0] * NT
    store_done = [0] * NT
    for ti in range(NT):
        slot, buf = ti % 4, ti % 2
        xs = XTI[:, slot * 1024:(slot + 1) * 1024]
        w = [(vE, nrm_done[ti - 4])] if ti >= 4 else []
        xl = p.dma("sync" if ti % 2 == 0 else "scalar", dX[slot], xs,
                   x_in.ap()[ti * 128:(ti + 1) * 128, :], waits=w)
        wv = [(dX[slot], xl)]
        if ti == 0:
            wv.append((gE, ms_g))
        p.on("vector", (lambda e, xs=xs, slot=slot: e.bn_stats(
            SMB[:, 8 + slot * 12:14 + slot * 12], xs[:, 0:512])), wv)
        p.on("vector", (lambda e, xs=xs, slot=slot: e.bn_stats(
            SMB[:, 14 + slot * 12:20 + slot * 12], xs[:, 512:1024])), [])
        ag_v = p.inc_on("vector", (lambda e, slot=slot: e.bn_aggr(
            SMB[:, 56 + slot * 2:58 + slot * 2], SMB[:, 8 + slot * 12:20 + slot * 12])), [], vE)
        sq_a = p.inc_on("scalar", (lambda e, slot=slot: e.activation(
            SMB[:, slot:slot + 1], SMB[:, 57 + slot * 2:58 + slot * 2],
            AF.Sqrt, bias=CONSTS[:, 0:1])), [(vE, ag_v)], aE)
        p.on("vector", (lambda e, slot=slot: e.reciprocal(
            SMB[:, 4 + slot:5 + slot], SMB[:, slot:slot + 1])), [(aE, sq_a)])
        w2 = [(dS[buf], store_done[ti - 2])] if ti >= 2 else []
        nrm_done[ti] = p.inc_on("vector", (lambda e, xs=xs, buf=buf, slot=slot: e.tensor_scalar(
            XLB[buf], xs, SMB[:, 56 + slot * 2:57 + slot * 2],
            SMB[:, 4 + slot:5 + slot], ALU.subtract, ALU.mult)), w2, vE)
        store_done[ti] = p.dma("gpsimd", dS[buf], xln_d.ap()[ti * 128:(ti + 1) * 128, :],
                               XLB[buf], waits=[(vE, nrm_done[ti])])
    p.emit(nc)
    if stop_after <= 0:
        for gd in guards.values():
            gd.__exit__(None, None, None)
        nc.finalize()
        return nc

    p = Prog({"scalar": aE, "vector": vE, "gpsimd": gE})
    ag_defer = []
    xst = [(dS[0], dS[0].n), (dS[1], dS[1].n)]
    for f in range(8):
        p.dma("sync", dT, XT[:, f * 4096:(f + 1) * 4096],
              xln_d.ap()[:, f * 128:(f + 1) * 128], transpose=True, waits=xst if f == 0 else [])
    xt_all = [(dT, dT.n)]

    wblocks = [("q", wq_in, m) for m in range(4)] + [("k", wk_in, m) for m in range(4)]
    wb_done = []
    for i, (_, win, m) in enumerate(wblocks):
        half = i % 2
        w = [(tE, (i - 1) * 8)] if i >= 2 else []
        v = p.dma("gpsimd", dW[half], WQS[:, half * 1024:(half + 1) * 1024].rearrange("p (a b) -> p a b", b=128),
                  win.ap()[:, m * 128:(m + 1) * 128].rearrange("(a p) b -> p a b", p=128), waits=w)
        wb_done.append((dW[half], v))
    wv_done = 0
    for kk in range(8):
        wv_done = p.dma("gpsimd", dV, WVB[:, kk * 512:(kk + 1) * 512],
                        wv_in.ap()[kk * 128:(kk + 1) * 128, :])

    p.inc_on("vector", lambda e: e.tensor_scalar(BQS[:], BQS[:], QSCALE, None, ALU.mult),
             [(dM, dM_all)], vE)
    bias_v = p.inc_on("vector", lambda e: e.tensor_scalar(BKS[:], BKS[:], QSCALE, None, ALU.mult), [], vE)

    qk_evac = []
    for i, (pj, _, m) in enumerate(wblocks):
        half = i % 2
        dstT = QT if pj == "q" else KT
        bias = BQS if pj == "q" else BKS
        for ntb in range(8):
            idx = i * 8 + ntb
            bank = PB[idx % 4]
            for kk in range(8):
                ww = []
                if kk == 0:
                    ww = [wb_done[i]] + xt_all
                    if idx >= 4:
                        ww.append((aE, qk_evac[idx - 4]))
                fn = (lambda e, bank=bank, half=half, kk=kk, ntb=ntb:
                      MM(e, bank[:], WQS[:, half * 1024 + kk * 128: half * 1024 + (kk + 1) * 128],
                               XT[:, kk * 4096 + ntb * 512: kk * 4096 + (ntb + 1) * 512],
                               start=(kk == 0), stop=(kk == 7)))
                if kk == 7:
                    mm_v = p.inc_on("tensor", fn, ww, tE)
                else:
                    p.on("tensor", fn, ww)
            ev = p.inc_on("scalar", (lambda e, bank=bank, dstT=dstT, bias=bias, m=m, ntb=ntb:
                                     e.activation(dstT[:, m * 4096 + ntb * 512: m * 4096 + (ntb + 1) * 512],
                                                  bank[:], AF.Identity, bias=bias[:, m:m + 1], scale=QSCALE)),
                          [(tE, mm_v), (vE, bias_v)], aE)
            qk_evac.append(ev)
    q_done_a = qk_evac[31]
    k_done_a = qk_evac[63]

    v_evac = []
    for nt in range(NT):
        bank = PB[4 + nt % 2]
        for kk in range(8):
            ww = []
            if kk == 0:
                ww = [(dV, wv_done)] + xt_all
                if nt >= 2:
                    ww.append((aE, v_evac[nt - 2]))
            p.on("tensor", (lambda e, bank=bank, kk=kk, nt=nt:
                            MM(e, bank[:], XT[:, kk * 4096 + nt * 128: kk * 4096 + nt * 128 + 128],
                                     WVB[:, kk * 512:(kk + 1) * 512],
                                     start=(kk == 0), stop=False)), ww)
        mm_v = p.inc_on("tensor", (lambda e, bank=bank:
                                   MM(e, bank[:], ONEB[:], BVB[:], start=False, stop=True)), [], tE)
        ev = p.inc_on("scalar", (lambda e, bank=bank, nt=nt: e.activation(
            VT[:, nt * 520:nt * 520 + 520].rearrange("p (a b) -> p a b", b=65)[:, :, 0:64],
            bank[:].rearrange("p (a b) -> p a b", b=64), AF.Copy)), [(tE, mm_v)], aE)
        v_evac.append(ev)
    vones_v = p.inc_on("vector", lambda e: e.memset(
        VT[:].rearrange("p (a b) -> p a b", b=65)[:, :, 64:65], 1.0), [(aE, v_evac[-1])], vE)

    pool_done_v = 0
    for nm, srcT, dstL, wgate in (("q", QT, qlT, q_done_a), ("k", KT, klT, k_done_a)):
        for chg in range(4):
            p.on("vector", (lambda e, srcT=srcT, chg=chg: e.tensor_reduce(
                SMB[:, 0:64], srcT[:, chg * 4096:(chg + 1) * 4096].rearrange("p (a b) -> p a b", b=64),
                axis=AX.X, op=ALU.add)), [(aE, wgate)] if chg == 0 else [])
            pool_done_v = p.inc_on("vector", (lambda e, dstL=dstL, chg=chg: e.tensor_scalar(
                dstL[:, chg * 64:(chg + 1) * 64], SMB[:, 0:64], 1.0 / SEG, None, ALU.mult)), [], vE)

    wpart_v = []
    gv_prev = 0
    for wi, wsl in ((0, wg_sl), (1, wu_sl), (2, wd_sl)):
        wld = [(gE, gv_prev)] if wi > 0 else []
        if wi < 2:
            v = p.dma("gpsimd", dL, XTI[:], wsl.ap(), waits=wld)
        else:
            v = p.dma("gpsimd", dL, XTI[:].rearrange("p (a b) -> p a b", b=1024),
                      wsl.ap().rearrange("(a p) b -> p a b", p=128), waits=wld)
        p.on("vector", (lambda e: e.tensor_reduce(SMB[:, 0:64], XTI[:].rearrange("p (a b) -> p a b", b=64),
                                        axis=AX.X, op=ALU.add, apply_absolute_value=True)),
             [(dL, v)])
        gv = p.inc_on("vector", (lambda e, wi=wi: e.tensor_reduce(
            SMA[:, 8 + wi:9 + wi], SMB[:, 0:64].rearrange("p (a b) -> p a b", b=64),
            axis=AX.X, op=ALU.add)), [], gE)
        gv_prev = gv
        tv = p.inc_on("tensor", (lambda e, wi=wi: MM(e, 
            PB[6][0:1, wi:wi + 1], ONEF[:], SMA[:, 8 + wi:9 + wi], start=True, stop=True)),
            [(gE, gv)], tE)
        wpart_v.append(p.inc_on("vector", (lambda e, wi=wi: e.tensor_copy(
            SROW[0:1, wi:wi + 1], PB[6][0:1, wi:wi + 1])), [(tE, tv)], vE))
    zero_v = p.inc_on("vector", lambda e: e.memset(SROW[0:1, 3:8], 0.0), [], vE)
    wsum_sent = p.dma("gpsimd", dE, wsum_i.ap(), SROW[0:1, 0:8],
                      waits=[(vE, wpart_v[-1]), (vE, zero_v)])
    ar_w = p.nochain("gpsimd", lambda e: e.collective_compute(
        "AllReduce", ALU.add, replica_groups=[list(range(8))],
        ins=[wsum_i.ap().opt()], outs=[wsum_o.ap().opt()]), [(dE, wsum_sent)], cw)
    rb_w = p.dma("gpsimd", dE, SROW[0:1, 8:16], wsum_o.ap(), waits=[(cw, ar_w)])

    p.on("vector", lambda e: e.tensor_scalar(SROW[0:1, 16:19], SROW[0:1, 8:11],
                                             1.0 / (INNER * DM), None, ALU.mult), [(dE, rb_w)])
    p.on("vector", lambda e: e.tensor_scalar(SROW[0:1, 16:19], SROW[0:1, 16:19],
                                             1e-5, None, ALU.max), [])
    p.on("vector", lambda e: e.memset(SROW[0:1, 19:20], 0.0), [])
    th_v = p.inc_on("vector", lambda e: e.tensor_scalar(SROW[0:1, 20:23], SROW[0:1, 16:19],
                                                        0.5, None, ALU.mult), [], vE)
    thb_t = p.inc_on("tensor", lambda e: MM(e, PB[6][:, 8:15], ONER[:], SROW[0:1, 16:23],
                                                  start=True, stop=True), [(vE, th_v)], tE)
    thb_v = p.inc_on("vector", lambda e: e.tensor_copy(SMA[:, 16:23], PB[6][:, 8:15]), [(tE, thb_t)], vE)
    thn_v = p.inc_on("vector", lambda e: e.tensor_scalar(SMA[:, 24:27], SMA[:, 20:23], -1.0,
                                                         None, ALU.mult), [(vE, thb_v)], vE)

    tv = 0
    prev_store = None
    for oi, (wi, wsl, agin, agout) in enumerate(
            [(2, wd_sl, agd_i, agd_o), (1, wu_sl, agu_i, agu_o), (0, wg_sl, agg_i, agg_o)]):
        if oi > 0:
            ww_ld = [(vE, tv)]
            if wi < 2:
                ld = p.dma("gpsimd", dL, XTI[:], wsl.ap(), waits=ww_ld)
            else:
                ld = p.dma("gpsimd", dL, XTI[:].rearrange("p (a b) -> p a b", b=1024),
                           wsl.ap().rearrange("(a p) b -> p a b", p=128), waits=ww_ld)
        else:
            ld = dL.n
        stv = 0
        for q4 in range(4):
            sl = XTI[:, q4 * 1024:(q4 + 1) * 1024]
            p.on("vector", (lambda e, sl=sl, wi=wi: e.tensor_scalar(
                WSL2[:], sl, SMA[:, 20 + wi:21 + wi], None, ALU.is_gt)),
                [(dL, ld), (vE, thn_v)] + ([(dF3[wi], stv)] if q4 > 0 else
                                           ([prev_store] if prev_store else [])))
            tv = p.inc_on("vector", (lambda e, sl=sl, wi=wi: e.scalar_tensor_tensor(
                WSL2[:], sl, SMA[:, 24 + wi:25 + wi], WSL2[:], ALU.is_lt, ALU.subtract)), [], vE)
            if wi < 2:
                stv = p.dma("gpsimd", dF3[wi], agin.ap()[:, q4 * 1024:(q4 + 1) * 1024], WSL2[:],
                            waits=[(vE, tv)])
            else:
                stv = p.dma("gpsimd", dF3[wi], agin.ap()[q4 * 128:(q4 + 1) * 128, :], WSL2[:],
                            waits=[(vE, tv)])
        prev_store = (dF3[wi], stv)
        ag_defer.append((agin, agout, dF3[wi], stv))
    p.emit(nc)
    if stop_after <= 1:
        for gd in guards.values():
            gd.__exit__(None, None, None)
        nc.finalize()
        return nc

    p = Prog({"scalar": aE, "vector": vE, "gpsimd": gE})

    def _fin():
        p.emit(nc)
        for gd in guards.values():
            gd.__exit__(None, None, None)
        nc.finalize()
        return nc
    convp_ld = p.dma("gpsimd", dM2, CONVP[:].rearrange("p (a b) -> p a b", b=128),
                     convp_in.ap().rearrange("(a p) b -> p a b", p=128))

    p.on("gpsimd", lambda e: e.memset(EVAC[:, 640:768], 1.0), [])
    eye_g = p.inc_on("gpsimd", lambda e: e.affine_select(EYE128, EVAC[:, 640:768], [[1, 128]],
                                                         ALU.is_equal, 0.0, base=0,
                                                         channel_multiplier=-1), [], gE)

    s2mm = []
    for h in range(H8):
        par, ch = h % 2, h // 2
        pr = slice(par * 64, par * 64 + 64)
        s2mm.append(p.inc_on("tensor", (lambda e, pr=pr, ch=ch, par=par: MM(e, 
            PB[0][pr, ch * 64:(ch + 1) * 64], qlT[pr, ch * 64:(ch + 1) * 64],
            klT[pr, ch * 64:(ch + 1) * 64], start=True, stop=True,
            tile_position=(par * 64, par * 64))), [], tE))
    k2_v = [0] * H8
    ex_l = [0] * H8
    for h in range(H8):
        par, ch = h % 2, h // 2
        pr = slice(par * 64, par * 64 + 64)
        p.on("vector", (lambda e, pr=pr, ch=ch, h=h: e.reduce_max(
            SMA[pr, h:h + 1], PB[0][pr, ch * 64:(ch + 1) * 64], axis=AX.X)),
            [(tE, s2mm[-1])] + ([(aE, ex_l[h - 2])] if h >= 2 else []))
        rm_v = p.inc_on("vector", (lambda e, pr=pr, h=h: e.tensor_scalar(
            SMB[pr, 32:33], SMA[pr, h:h + 1], -1.0, None, ALU.mult)), [], vE)
        ex_a = p.inc_on("scalar", (lambda e, pr=pr, ch=ch: e.activation(
            EVAC[pr, 0:64], PB[0][pr, ch * 64:(ch + 1) * 64], AF.Exp, bias=SMB[pr, 32:33],
            accum_out=SMB[pr, 33:34])),
            [(vE, rm_v)] + ([(vE, k2_v[h - 2])] if h >= 2 else []), aE)
        ex_l[h] = ex_a
        rc_v = p.inc_on("vector", (lambda e, pr=pr: e.reciprocal(SMB[pr, 34:35], SMB[pr, 33:34])),
                        [(aE, ex_a)], vE)
        k2_v[h] = p.inc_on("vector", (lambda e, pr=pr, ch=ch: e.tensor_scalar(
            K2sb[pr, ch * 64:(ch + 1) * 64], EVAC[pr, 0:64], SMB[pr, 34:35], None, ALU.mult)),
            [(vE, rc_v)], vE)
        p.inc_on("tensor", (lambda e, pr=pr, ch=ch, h=h, par=par: MM(e, 
            PB[1][0:1, h * 64:(h + 1) * 64], ONEF[pr, :], K2sb[pr, ch * 64:(ch + 1) * 64],
            start=True, stop=True, tile_position=(par * 64, 0))), [(vE, k2_v[h])], tE)
    k2_all_v = vE.n
    cs_all_t = tE.n
    if stop_after <= 2.1:
        return _fin()
    p.on("vector", lambda e: e.memset(SROW[0:1, 33:40], NEGI), [])
    am_v = p.inc_on("vector", lambda e: e.reduce_max(SROW[0:1, 32:33], PB[1][0:1, 0:512], axis=AX.X),
                    [(tE, cs_all_t)], vE)
    am_sent = p.dma("gpsimd", dE, amax_i.ap(), SROW[0:1, 32:40], waits=[(vE, am_v)])
    ar_m = p.nochain("gpsimd", lambda e: e.collective_compute(
        "AllReduce", ALU.max, replica_groups=[list(range(8))],
        ins=[amax_i.ap().opt()], outs=[amax_o.ap().opt()]), [(dE, am_sent)], cm)
    rb_m = p.dma("gpsimd", dE, SROW[0:1, 40:48], amax_o.ap(), waits=[(cm, ar_m)])
    for agin_, agout_, f3s_, stv_ in ag_defer:
        p.nochain("gpsimd", (lambda e, agin=agin_, agout=agout_: e.collective_compute(
            "AllGather", ALU.bypass, replica_groups=[list(range(8))],
            ins=[agin.ap().opt()], outs=[agout.ap().opt()])), [(f3s_, stv_)], cw)
    c_v = p.inc_on("vector", lambda e: e.reciprocal(SROW[0:1, 48:49], SROW[0:1, 40:41]),
                   [(dE, rb_m)], vE)
    cb_t = p.inc_on("tensor", lambda e: MM(e, PB[1][:, 12:13], ONER[:], SROW[0:1, 48:49],
                                                 start=True, stop=True), [(vE, c_v)], tE)
    cbc_v = p.inc_on("vector", lambda e: e.tensor_copy(CB[:, 12:13], PB[1][:, 12:13]),
                     [(tE, cb_t)], vE)
    if stop_after <= 2.2:
        return _fin()

    p.on("vector", lambda e: e.memset(EVAC[:, 768:832], NEGI), [(vE, k2_all_v)])
    cp_v = p.inc_on("vector", lambda e: e.tensor_copy(EVAC[:, 768:776], SMA[:, 0:8]), [], vE)
    ct_t = p.inc_on("tensor", lambda e: e.transpose(PB[2][0:64, 0:128], EVAC[:, 768:832], EYE128),
                    [(vE, cp_v), (gE, eye_g)], tE)
    cc_v = p.inc_on("vector", lambda e: e.reduce_max(SMB[0:64, 35:36], PB[2][0:64, 0:128], axis=AX.X),
                    [(tE, ct_t)], vE)

    p.on("vector", lambda e: e.memset(EVAC[0:64, 832:896], NEGI), [(vE, cc_v)])
    cp2_v = p.inc_on("vector", lambda e: e.tensor_copy(EVAC[0:64, 832:833], SMB[0:64, 35:36]), [], vE)
    ct2_t = p.inc_on("tensor", lambda e: e.transpose(PB[2][0:64, 128:192], EVAC[0:64, 832:896],
                                                     EYE2[0:64, :]), [(vE, cp2_v)], tE)
    ncrow_v = p.inc_on("vector", lambda e: e.tensor_scalar(SROW[0:1, 52:60], PB[2][0:1, 128:136],
                                                           -1.0, None, ALU.mult), [(tE, ct2_t)], vE)
    nb_t = p.inc_on("tensor", lambda e: MM(e, PB[2][:, 192:200], ONER[:], SROW[0:1, 52:60],
                                                 start=True, stop=True), [(vE, ncrow_v)], tE)
    nb_v = p.inc_on("vector", lambda e: e.tensor_copy(CB[:, 4:12], PB[2][:, 192:200]), [(tE, nb_t)], vE)

    for j in range(4):
        p.on("vector", (lambda e, j=j: e.tensor_copy(CB[0:64, j:j + 1],
                                                     CB[0:64, 4 + 2 * j:5 + 2 * j])),
             [(vE, nb_v)] if j == 0 else [])
        pairb_v = p.inc_on("vector", (lambda e, j=j: e.tensor_copy(
            CB[64:128, j:j + 1], CB[64:128, 5 + 2 * j:6 + 2 * j])), [], vE)

    for h in range(8):
        ncb_v = p.inc_on("vector", (lambda e, h=h: e.tensor_copy(
            NCB[0:1, h * 64:(h + 1) * 64], CB[0:1, 4 + h:5 + h].broadcast_to((1, 64)))),
            [(vE, nb_v)] if h == 0 else [], vE)

    if stop_after <= 2.3:
        return _fin()
    e1_evac = []
    for j in range(4):
        for ntb in range(8):
            idx = j * 8 + ntb
            bank = PB[0] if idx % 2 == 0 else PB[3]
            ww = [(vE, pairb_v)]
            if idx >= 2:
                ww.append((aE, e1_evac[idx - 2]))
            p.inc_on("tensor", (lambda e, bank=bank, j=j, ntb=ntb: MM(e, 
                bank[0:64, 0:512], klT[0:64, j * 64:(j + 1) * 64],
                QT[0:64, j * 4096 + ntb * 512: j * 4096 + (ntb + 1) * 512],
                start=True, stop=True, tile_position=(0, 0))), ww, tE)
            mm_v = p.inc_on("tensor", (lambda e, bank=bank, j=j, ntb=ntb: MM(e, 
                bank[64:128, 0:512], klT[64:128, j * 64:(j + 1) * 64],
                QT[64:128, j * 4096 + ntb * 512: j * 4096 + (ntb + 1) * 512],
                start=True, stop=True, tile_position=(64, 64))), [], tE)
            ev = p.inc_on("scalar", (lambda e, bank=bank, j=j, ntb=ntb: e.activation(
                E1[:, j * 4096 + ntb * 512: j * 4096 + (ntb + 1) * 512], bank[:, 0:512],
                AF.Exp, bias=CB[:, j:j + 1])), [(tE, mm_v)], aE)
            e1_evac.append(ev)
    e1_done_a = e1_evac[-1]

    if stop_after <= 2.4:
        return _fin()
    exp_done = [0, 0]
    k3_t = 0
    for nt in range(NT):
        sbank = PB[0] if nt % 2 == 0 else PB[3]
        ww0 = [(vE, ncb_v), (aE, e1_done_a)]
        if nt >= 2:
            ww0.append((aE, exp_done[nt % 2]))
        for h in range(H8):
            par, jj = h % 2, h // 2
            pr = slice(par * 64, par * 64 + 64)
            p.on("tensor", (lambda e, sbank=sbank, pr=pr, jj=jj, nt=nt, par=par, h=h: MM(e, 
                sbank[:, h * 64:(h + 1) * 64],
                KT[pr, jj * 4096 + nt * 128: jj * 4096 + nt * 128 + 128],
                qlT[pr, jj * 64:(jj + 1) * 64],
                start=True, stop=False, tile_position=(par * 64, 0))), ww0 if h == 0 else [])
            fn = (lambda e, sbank=sbank, h=h: MM(e, 
                sbank[:, h * 64:(h + 1) * 64], ONEB[:], NCB[0:1, h * 64:(h + 1) * 64],
                start=False, stop=True))
            smm = p.inc_on("tensor", fn, [], tE)
        ww_exp = [(tE, smm)]
        if nt >= 1:
            ww_exp.append((tE, k3_t))
        ev = p.inc_on("scalar", (lambda e, sbank=sbank: e.activation(
            AGPB[:], sbank[:, 0:512], AF.Exp)), ww_exp, aE)
        exp_done[nt % 2] = ev
        for h in range(H8):
            par, jj = h % 2, h // 2
            fn = (lambda e, h=h, jj=jj, nt=nt, par=par: MM(e, 
                PB[4 + jj][par * 64:par * 64 + 64, 0:65],
                AGPB[:, h * 64:(h + 1) * 64],
                VT[:, nt * 520 + 65 * h: nt * 520 + 65 * h + 65],
                start=(nt == 0), stop=(nt == NT - 1),
                tile_position=(0, par * 64)))
            ww = [(aE, ev)] if h == 0 else []
            if nt == 0 and h == 0:
                ww.append((vE, vones_v))
            k3_t = p.inc_on("tensor", fn, ww, tE)
    k3v_all_t = k3_t

    k3n_v = 0
    K3Vp = EVAC[:, 0:256]
    for jj in range(4):
        rc2 = p.inc_on("vector", (lambda e, jj=jj: e.reciprocal(SMB[:, 36 + jj:37 + jj],
                                                                PB[4 + jj][:, 64:65])),
                       [(tE, k3v_all_t)], vE)
        k3n_v = p.inc_on("vector", (lambda e, jj=jj: e.tensor_scalar(
            K3Vp[:, jj * 64:(jj + 1) * 64], PB[4 + jj][:, 0:64], SMB[:, 36 + jj:37 + jj],
            None, ALU.mult)), [(vE, rc2)], vE)
    k3mv = 0
    for h in range(H8):
        par, jj = h % 2, h // 2
        k3mv = p.dma("gpsimd", dN1, K3V[0:64, h * 64:(h + 1) * 64],
                     K3Vp[par * 64:par * 64 + 64, jj * 64:(jj + 1) * 64],
                     waits=[(vE, k3n_v)] if h == 0 else [])

    if stop_after <= 2.5:
        return _fin()
    K2all = EVAC[0:64, 256:768]
    K2Tall = EVAC[0:64, 256:768]
    NT1 = EVAC[0:64, 256:768]
    k2mv = 0
    for h in range(H8):
        par, ch = h % 2, h // 2
        k2mv = p.dma("gpsimd", dN2, K2all[:, h * 64:(h + 1) * 64],
                     K2sb[par * 64:par * 64 + 64, ch * 64:(ch + 1) * 64],
                     waits=[(vE, k2_all_v)] if h == 0 else [])
    z0_v = 0
    tK_l = []
    for h in range(H8):
        tk = p.inc_on("tensor", (lambda e, h=h: MM(e, PB[2][0:64, h * 64:(h + 1) * 64],
                                                   K2all[:, h * 64:(h + 1) * 64], EYE2[0:64, :],
                                                   start=True, stop=True)),
                      [(dN2, k2mv)] if h == 0 else [], tE)
        tK_l.append(tk)
    lt0_v = p.inc_on("vector", lambda e: e.tensor_scalar(LTsb[:], K2all[:], CB[0:64, 12:13],
                                                         None, ALU.mult),
                     [(vE, cbc_v), (dN2, k2mv)], vE)
    cpv = p.inc_on("vector", lambda e: e.tensor_copy(K2Tall[:], PB[2][0:64, 0:512]),
                   [(tE, tK_l[-1])], vE)
    zb_l = []
    for h in range(H8):
        zb = p.inc_on("tensor", (lambda e, h=h: MM(e, PB[1][0:64, h * 64:(h + 1) * 64],
                                                   K2Tall[:, h * 64:(h + 1) * 64],
                                                   K2Tall[:, h * 64:(h + 1) * 64],
                                                   start=True, stop=True)),
                      [(vE, cpv)] if h == 0 else [], tE)
        zb_l.append(zb)
    z0_v = p.inc_on("vector", lambda e: e.tensor_scalar(
        Zsb[:], PB[1][0:64, 0:512], CB[0:64, 12:13], None, ALU.mult),
        [(tE, zb_l[-1]), (vE, cbc_v)], vE)
    lt0_v = p.inc_on("vector", lambda e: e.tensor_scalar(LTsb[:], K2all[:], CB[0:64, 12:13],
                                                         None, ALU.mult),
                     [(vE, cbc_v), (dN2, k2mv)], vE)
    if stop_after <= 2.52:
        return _fin()

    def newton_sweep(CUR, first_wait, prev_read):
        for h in range(H8):
            ww = [first_wait] if h == 0 else []
            if h == 0 and prev_read:
                ww.append((vE, prev_read))
            u1t = p.inc_on("tensor", (lambda e, h=h, CUR=CUR: MM(e, 
                PB[0][0:64, h * 64:(h + 1) * 64], Zsb[:, h * 64:(h + 1) * 64],
                CUR[:, h * 64:(h + 1) * 64], start=True, stop=True)), ww, tE)
        t1v = p.inc_on("vector", (lambda e, CUR=CUR: e.scalar_tensor_tensor(
            NT1[:], CUR[:], 7.0, PB[0][0:64, 0:512], ALU.mult, ALU.subtract)),
            [(tE, u1t)], vE)
        for h in range(H8):
            u2t = p.inc_on("tensor", (lambda e, h=h: MM(e, 
                PB[3][0:64, h * 64:(h + 1) * 64], Zsb[:, h * 64:(h + 1) * 64],
                NT1[:, h * 64:(h + 1) * 64], start=True, stop=True)),
                [(vE, t1v)] if h == 0 else [], tE)
        t2v = p.inc_on("vector", (lambda e, CUR=CUR: e.scalar_tensor_tensor(
            NT1[:], CUR[:], 15.0, PB[3][0:64, 0:512], ALU.mult, ALU.subtract)),
            [(tE, u2t)], vE)
        for h in range(H8):
            u3t = p.inc_on("tensor", (lambda e, h=h: MM(e, 
                PB[0][0:64, h * 64:(h + 1) * 64], Zsb[:, h * 64:(h + 1) * 64],
                NT1[:, h * 64:(h + 1) * 64], start=True, stop=True)),
                [(vE, t2v)] if h == 0 else [], tE)
        rd = p.inc_on("vector", lambda e: e.tensor_scalar(NT1[:], PB[0][0:64, 0:512], 0.25,
                                                          None, ALU.mult), [(tE, u3t)], vE)
        last_v = p.inc_on("vector", (lambda e, CUR=CUR: e.scalar_tensor_tensor(
            CUR[:], CUR[:], 3.25, NT1[:], ALU.mult, ALU.subtract)), [], vE)
        return last_v, rd

    lt_v, z_v = lt0_v, z0_v
    prd = 0
    nits = 1 if stop_after <= 2.55 else 6
    for it in range(nits):
        lt_v, prd = newton_sweep(LTsb, (vE, max(lt_v, z_v)), prd)
        if it < 5:
            z_v, prd = newton_sweep(Zsb, (vE, lt_v), prd)
    if stop_after <= 2.55:
        return _fin()
    LT6 = LTsb

    if stop_after <= 2.6:
        return _fin()
    for h in range(H8):
        ww = [(vE, lt_v), (dN1, k3mv)] if h == 0 else []
        gt = p.inc_on("tensor", (lambda e, h=h: MM(e, 
            PB[1][0:64, h * 64:(h + 1) * 64], LT6[:, h * 64:(h + 1) * 64],
            K3V[:, h * 64:(h + 1) * 64], start=True, stop=True)), ww, tE)
    Gall = WSL2[0:64, 0:512]
    g_vv = p.inc_on("vector", lambda e: e.tensor_copy(Gall[:], PB[1][0:64, 0:512]),
                    [(tE, gt)], vE)
    g_v = 0
    for h in range(H8):
        par, j = h % 2, h // 2
        g_v = p.dma("gpsimd", dN3, Gsb[par * 64:par * 64 + 64, j * 65: j * 65 + 64],
                    Gall[:, h * 64:(h + 1) * 64], waits=[(vE, g_vv)] if h == 0 else [])
    g_v = (dN3, g_v)

    if debug:
        p.dma("gpsimd", dDB, dbg["dbg_qT"].ap(), QT[:], waits=[g_v])
        p.dma("gpsimd", dDB, dbg["dbg_kT"].ap(), KT[:])
        p.dma("gpsimd", dDB, dbg["dbg_v"].ap(), VT[:])
        p.dma("gpsimd", dDB, dbg["dbg_ql"].ap(), qlT[:])
        p.dma("gpsimd", dDB, dbg["dbg_kl"].ap(), klT[:])
        p.dma("gpsimd", dDB, dbg["dbg_K2"].ap(), K2sb[:])
        p.dma("gpsimd", dDB, dbg["dbg_G"].ap(), Gsb[:])
        p.dma("gpsimd", dDB, dbg["dbg_k3v"].ap(), K3V[:])
        p.dma("gpsimd", dDB, dbg["dbg_cb"].ap(), CB[:])
        p.dma("gpsimd", dDB, dbg["dbg_srow"].ap(), SROW[:])
    p.emit(nc)
    if stop_after <= 2:
        for gd in guards.values():
            gd.__exit__(None, None, None)
        nc.finalize()
        return nc

    g_last = g_v
    p = Prog({"scalar": aE, "vector": vE, "gpsimd": gE})
    ctx_done = []
    pay_st = [0] * NT
    rv_l = []
    AGPB2 = WSL2[:, 512:1024]
    for nt in range(NT):
        cbank = PB[(nt % 3) * 2]
        vbank = PB[(nt % 3) * 2 + 1]
        dbank = PB[6 + nt % 2]
        stg = AGPB if nt % 2 == 0 else AGPB2
        ww0 = [(dM2, convp_ld), g_last] if nt < 3 else [(vE, ctx_done[nt - 3])]
        first = True
        for h in range(H8):
            par, j = h % 2, h // 2
            cjs = [cj for cj in range(3)
                   if not ((nt == 0 and cj == 0) or (nt == NT - 1 and cj == 2))]
            for ci, cj in enumerate(cjs):
                stt = (ci == 0)
                lst = (ci == len(cjs) - 1)
                fnc = (lambda e, vbank=vbank, h=h, cj=cj, nt=nt, stt=stt, lst=lst: MM(e, 
                    vbank[:, h * 64:(h + 1) * 64],
                    CONVP[:, (h * 3 + cj) * 128:(h * 3 + cj + 1) * 128],
                    VT[:, (nt - 1 + cj) * 520 + 65 * h: (nt - 1 + cj) * 520 + 65 * h + 64],
                    start=stt, stop=lst))
                if lst:
                    p.inc_on("tensor", fnc, ww0 if first else [], tE)
                else:
                    p.on("tensor", fnc, ww0 if first else [])
                first = False
            p.inc_on("tensor", (lambda e, cbank=cbank, h=h, j=j, par=par, nt=nt: MM(e, 
                cbank[:, h * 64:(h + 1) * 64],
                E1[par * 64:par * 64 + 64, j * 4096 + nt * 128: j * 4096 + nt * 128 + 128],
                Gsb[par * 64:par * 64 + 64, j * 65: j * 65 + 64],
                start=True, stop=True, tile_position=(par * 64, 0))), [], tE)
            fn = (lambda e, dbank=dbank, h=h, j=j, par=par, nt=nt: MM(e, 
                dbank[:, h:h + 1],
                E1[par * 64:par * 64 + 64, j * 4096 + nt * 128: j * 4096 + nt * 128 + 128],
                Gsb[par * 64:par * 64 + 64, j * 65 + 64: j * 65 + 65],
                start=True, stop=True, tile_position=(par * 64, 0)))
            tmm = p.inc_on("tensor", fn,
                           [(vE, rv_l[nt - 2])] if (h == 0 and nt >= 2) else [], tE)
        rv = p.inc_on("vector", (lambda e, dbank=dbank: e.reciprocal(SMB[:, 40:48], dbank[:, 0:8])),
                      [(tE, tmm)], vE)
        rv_l.append(rv)
        cn = p.inc_on("vector", (lambda e, cbank=cbank: e.tensor_tensor(
            EVAC[:, 0:512].rearrange("p (a b) -> p a b", b=64),
            cbank[:, 0:512].rearrange("p (a b) -> p a b", b=64),
            SMB[:, 40:48].rearrange("p (a b) -> p a b", b=1).broadcast_to((128, 8, 64)),
            ALU.mult)), [(vE, rv)], vE)
        dPs = dP if nt % 2 == 0 else dPb
        fl = p.inc_on("vector", (lambda e, vbank=vbank, stg=stg: e.scalar_tensor_tensor(
            stg[:], vbank[:, 0:512], 1.0, EVAC[:, 0:512], ALU.mult, ALU.add)),
            [(vE, cn)] + ([(dPs, pay_st[nt - 2])] if nt >= 2 else []), vE)
        ctx_done.append(fl)
        pay_st[nt] = p.dma("gpsimd", dPs, ctxme_d.ap()[nt * 128:(nt + 1) * 128, :], stg[:],
                           waits=[(vE, fl)])
    ag_ctx = p.nochain("gpsimd", lambda e: e.collective_compute(
        "AllGather", ALU.bypass, replica_groups=[[0, 1], [2, 3], [4, 5], [6, 7]],
        ins=[ctxme_d.ap().opt()], outs=[ctxab_d.ap().opt()]),
        [(dP, pay_st[-2]), (dPb, pay_st[-1])], cx)
    p.emit(nc)
    if stop_after <= 3:
        for gd in guards.values():
            gd.__exit__(None, None, None)
        nc.finalize()
        return nc

    p = Prog({"scalar": aE, "vector": vE, "gpsimd": gE})
    dbg_sent = [(dDB, dDB.n)] if dDB.n else []
    wg_ld = 0
    for kk in range(8):
        wg_ld = p.dma("sync", dWG, XT[:, kk * 4096:(kk + 1) * 4096],
                      agg_o.ap()[kk * 128:(kk + 1) * 128, :],
                      waits=([(cw, 4)] + dbg_sent) if kk == 0 else [])
    wu_ld = 0
    for kk in range(8):
        dstT = QT if kk < 4 else KT
        wu_ld = p.dma("scalar", dWU, dstT[:, (kk % 4) * 4096:((kk % 4) + 1) * 4096],
                      agu_o.ap()[kk * 128:(kk + 1) * 128, :],
                      waits=([(cw, 4)] + dbg_sent) if kk == 0 else [])

    regs = {}

    def _mkregs(e):
        regs["rme"] = e.value_load(OFFS[0:1, 0:1])
        regs["rpa"] = e.value_load(OFFS[0:1, 1:2])
        regs["cme"] = e.value_load(OFFS[0:1, 2:3])
        regs["cot"] = e.value_load(OFFS[0:1, 3:4])
        return e.engine_nop()
    p.inc_on("gpsimd", _mkregs, [(cx, ag_ctx)], gE)

    x1_st = [0] * NTO
    asm_v = [0] * NTO
    sq_l = [0] * NTO
    for t in range(NTO):
        slot = t % 4
        xb = XTI[:, slot * 1024:(slot + 1) * 1024]
        cbuf = WVB[:, slot * 1024:(slot + 1) * 1024]
        w = [(vE, asm_v[t - 4]), (aE, sq_l[t - 4]), (dSx, x1_st[t - 4])] if t >= 4 else []
        xv = p.dma("sync", dX[slot], xb, x_own.ap()[t * 128:(t + 1) * 128, :], waits=w)
        def mkld(dst_cols_reg, src_dram, row_reg, t=t, cbuf=cbuf):
            def fn(e, dst_cols_reg=dst_cols_reg, src_dram=src_dram, row_reg=row_reg):
                return e.dma_start(cbuf[:, ds(regs[dst_cols_reg], 512)],
                                   src_dram.ap()[ds(regs[row_reg] + t * 128, 128), :])
            return fn
        wl = [(gE, gE.n)] if t < 4 else [(vE, asm_v[t - 4])]
        p.inc_on("gpsimd", mkld("cme", ctxme_d, "rme"), wl, dCX[slot], 16)
        l2 = p.inc_on("gpsimd", mkld("cot", ctxab_d, "rpa"), [], dCX[slot], 16)
        av = p.inc_on("vector", (lambda e, xb=xb, cbuf=cbuf, t=t: e.scalar_tensor_tensor(
            xb, cbuf, 1.0, xb, ALU.mult, ALU.add, accum_out=st(SS_SUM, t))),
            [(dX[slot], xv), (dCX[slot], l2)], vE)
        sq_l[t] = p.inc_on("scalar", (lambda e, xb=xb, t=t: e.activation(
            EVAC[:], xb, AF.Square, accum_out=st(SS_SSQ, t))), [(vE, av)], aE)
        p.on("vector", (lambda e, xb=xb, t=t: e.reduce_max(st(SS_MAX, t), xb, axis=AX.X)), [])
        asm_v[t] = p.inc_on("vector", (lambda e, xb=xb, t=t: e.tensor_reduce(
            st(SS_MIN, t), xb, axis=AX.X, op=ALU.min)), [], vE)
        x1_st[t] = p.dma("scalar", dSx, x1_d.ap()[t * 128:(t + 1) * 128, :], xb,
                         waits=[(vE, asm_v[t]), (aE, aE.n)] +
                               ([(dSx, x1_st[t - 1])] if t >= 1 else []))
    x1_all_a = aE.n

    p.on("vector", lambda e: e.tensor_scalar(st(SS_M, 0, 16), st(SS_SUM, 0, 16), 1.0 / DM,
                                             None, ALU.mult), [(vE, asm_v[-1]), (aE, x1_all_a)])
    p.on("vector", lambda e: e.tensor_scalar(st(SS_RSTD, 0, 16), st(SS_SSQ, 0, 16), 1.0 / DM,
                                             None, ALU.mult), [])
    p.on("vector", lambda e: e.scalar_tensor_tensor(st(SS_AA, 0, 16), st(SS_M, 0, 16), 1.0,
                                                    st(SS_M, 0, 16), ALU.mult, ALU.mult), [])
    sv = p.inc_on("vector", lambda e: e.tensor_tensor(st(SS_RSTD, 0, 16), st(SS_RSTD, 0, 16),
                                                      st(SS_AA, 0, 16), ALU.subtract), [], vE)
    sqa = p.inc_on("scalar", lambda e: e.activation(st(SS_AA, 0, 16), st(SS_RSTD, 0, 16),
                                                    AF.Sqrt, bias=CONSTS[:, 0:1]), [(vE, sv)], aE)

    _s2ops = [
        lambda e: e.reciprocal(st(SS_RSTD, 0, 16), st(SS_AA, 0, 16)),
        lambda e: e.tensor_tensor(st(SS_AA, 0, 16), st(SS_MAX, 0, 16), st(SS_M, 0, 16), ALU.subtract),
        lambda e: e.scalar_tensor_tensor(st(SS_BBM, 0, 16), st(SS_MIN, 0, 16), -1.0, st(SS_M, 0, 16),
                                         ALU.mult, ALU.add),
        lambda e: e.tensor_tensor(st(SS_AA, 0, 16), st(SS_AA, 0, 16), st(SS_BBM, 0, 16), ALU.max),
        lambda e: e.tensor_tensor(st(SS_AA, 0, 16), st(SS_AA, 0, 16), st(SS_RSTD, 0, 16), ALU.mult),
        lambda e: e.tensor_scalar(st(SS_AA, 0, 16), st(SS_AA, 0, 16), 1e-5, None, ALU.max),
        lambda e: e.reciprocal(st(SS_S2Q, 0, 16), st(SS_AA, 0, 16)),
        lambda e: e.tensor_scalar(st(SS_S2Q, 0, 16), st(SS_S2Q, 0, 16), 127.0, None, ALU.mult),
        lambda e: e.tensor_tensor(st(SS_AA, 0, 16), st(SS_RSTD, 0, 16), st(SS_S2Q, 0, 16), ALU.mult),
        lambda e: e.scalar_tensor_tensor(st(SS_BBM, 0, 16), st(SS_M, 0, 16), -1.0, st(SS_AA, 0, 16),
                                         ALU.mult, ALU.mult),
        lambda e: e.reciprocal(st(SS_IG, 0, 16), st(SS_S2Q, 0, 16)),
        lambda e: e.tensor_scalar(st(SS_IU, 0, 16), st(SS_IG, 0, 16), SMA[:, 17:18], -1.0, ALU.mult, ALU.mult),
    ]
    for ii, op_ in enumerate(_s2ops):
        p.on("vector", op_, [(aE, sqa)] if ii == 0 else [])
    st2 = p.inc_on("vector", lambda e: e.tensor_scalar(st(SS_IG, 0, 16), st(SS_IG, 0, 16),
                                                       SMA[:, 16:17], -1.0, ALU.mult, ALU.mult),
                   [], vE)

    ra_l = []
    rv_l = []
    xi_st = [0] * NTO
    x1all_sx = dSx.n
    for t in range(NTO):
        slot = t % 2
        xb = XTI[:, slot * 1024:(slot + 1) * 1024]
        zb = WVB[:, slot * 1024:(slot + 1) * 1024]
        w = [(vE, st2), (dSx, x1all_sx)]
        if t >= 2:
            w.append((aE, ra_l[t - 2]))
        xv = p.dma("sync", dX[slot], xb, x1_d.ap()[t * 128:(t + 1) * 128, :], waits=w)
        ra = p.inc_on("scalar", (lambda e, xb=xb, t=t: e.activation(
            EVAC[:], xb, AF.Identity, bias=st(SS_BBM, t), scale=st(SS_AA, t))),
            [(dX[slot], xv)] + ([(vE, rv_l[t - 1])] if t >= 1 else []), aE)
        ra_l.append(ra)
        rv = p.inc_on("vector", (lambda e, zb=zb: e.tensor_scalar(
            zb, EVAC[:], MAGIC, -MAGIC, ALU.add, ALU.add)),
            [(aE, ra)] + ([(dSi, xi_st[t - 2])] if t >= 2 else []), vE)
        rv_l.append(rv)
        xi_st[t] = p.dma("gpsimd", dSi, xint_d.ap()[t * 128:(t + 1) * 128, :], zb,
                         waits=[(vE, rv)] + ([(dSi, xi_st[t - 1])] if t >= 1 else []))
    for f in range(8):
        p.dma("sync", dT2, VT[:, f * 2048:(f + 1) * 2048],
              xint_d.ap()[:, f * 128:(f + 1) * 128], transpose=True,
              waits=[(dSi, xi_st[-1])] if f == 0 else [])
    xiT_all = [(dT2, dT2.n)]

    zi_st = [0] * NTO
    s2q_v = [0] * NTO
    gaL, uaL, zvL = [], [], []
    for t in range(NTO):
        gu_done = []
        for n in range(8):
            bka = PB[0 + (n % 2)]
            bkb = PB[2 + (n % 2)]
            gsl = WQS[:, (n % 2) * 1024:(n % 2) * 1024 + 512]
            usl = WQS[:, (n % 2) * 1024 + 512:(n % 2) * 1024 + 1024]
            gi = t * 8 + n
            for kk in range(8):
                ww = []
                if kk == 0:
                    if gi == 0:
                        ww = list(xiT_all) + [(dWG, wg_ld), (dWU, wu_ld), (dSi, xi_st[-1])]
                    if gi >= 2:
                        ww.append((aE, gaL[gi - 2]))
                fng = (lambda e, bka=bka, kk=kk, t=t, n=n: MM(e, 
                    bka[:], VT[:, kk * 2048 + t * 128: kk * 2048 + t * 128 + 128],
                    XT[:, kk * 4096 + n * 512: kk * 4096 + (n + 1) * 512],
                    start=(kk == 0), stop=(kk == 7)))
                if kk == 7:
                    p.inc_on("tensor", fng, ww, tE)
                else:
                    p.on("tensor", fng, ww)
            for kk in range(8):
                srcW = QT if kk < 4 else KT
                ww = []
                if kk == 0 and gi >= 2:
                    ww.append((aE, uaL[gi - 2]))
                fn = (lambda e, bkb=bkb, kk=kk, t=t, n=n, srcW=srcW: MM(e, 
                    bkb[:], VT[:, kk * 2048 + t * 128: kk * 2048 + t * 128 + 128],
                    srcW[:, (kk % 4) * 4096 + n * 512: (kk % 4) * 4096 + (n + 1) * 512],
                    start=(kk == 0), stop=(kk == 7)))
                if kk == 7:
                    up_t = p.inc_on("tensor", fn, ww, tE)
                else:
                    p.on("tensor", fn, ww)
            ga = p.inc_on("scalar", (lambda e, bka=bka, gsl=gsl, t=t: e.activation(
                gsl, bka[:], AF.Silu, scale=st(SS_IG, t))),
                [(tE, up_t)] + ([(vE, zvL[gi - 2])] if gi >= 2 else []), aE)
            ua = p.inc_on("scalar", (lambda e, bkb=bkb, usl=usl, t=t: e.activation(
                usl, bkb[:], AF.Copy, scale=st(SS_IU, t))), [], aE)
            zv = p.inc_on("vector", (lambda e, gsl=gsl, usl=usl, n=n: e.tensor_tensor(
                WVB[:, n * 512:(n + 1) * 512], gsl, usl, ALU.mult)),
                [(aE, ua)] + ([(dZ, zi_st[t - 1])] if (t >= 1 and n == 0) else []), vE)
            gu_done.append(zv)
            zvL.append(zv)
            gaL.append(ga)
            uaL.append(ua)
        p.on("vector", (lambda e: e.tensor_reduce(
            SMB[:, 0:64], WVB[:].rearrange("p (a b) -> p a b", b=64), axis=AX.X,
            op=ALU.max, apply_absolute_value=True)), [(vE, gu_done[-1])])
        p.on("vector", lambda e: e.reduce_max(SMB[:, 32:33], SMB[:, 0:64], axis=AX.X,
                                              apply_absolute_value=True), [])
        p.on("vector", lambda e: e.tensor_scalar(SMB[:, 32:33], SMB[:, 32:33], 1e-5, None, ALU.max), [])
        p.on("vector", lambda e: e.reciprocal(SMB[:, 33:34], SMB[:, 32:33]), [])
        s2q_v[t] = p.inc_on("vector", (lambda e, t=t: e.tensor_scalar(
            st(SS_S2Q, t), SMB[:, 33:34], 127.0, None, ALU.mult)), [], vE)
        qrv = 0
        for q4 in range(4):
            ra = p.inc_on("scalar", (lambda e, q4=q4, t=t: e.activation(
                EVAC[:], WVB[:, q4 * 1024:(q4 + 1) * 1024], AF.Identity, bias=CONSTS[:, 1:2],
                scale=st(SS_S2Q, t))),
                [(vE, s2q_v[t])] + ([(vE, qrv)] if q4 > 0 else []), aE)
            qrv = p.inc_on("vector", (lambda e, q4=q4: e.tensor_scalar(
                WVB[:, q4 * 1024:(q4 + 1) * 1024], EVAC[:], -MAGIC, None, ALU.add)),
                [(aE, ra)], vE)
        zq_v = vE.n
        zi_st[t] = p.dma("gpsimd", dZ, zint_d.ap()[t * 128:(t + 1) * 128, :], WVB[:],
                         waits=[(vE, zq_v)] + ([(dZ, zi_st[t - 1])] if t >= 1 else []))
    p.on("vector", lambda e: e.reciprocal(st(SS_IG, 0, 16), st(SS_S2Q, 0, 16)), [(vE, s2q_v[-1])])
    id_v = p.inc_on("vector", lambda e: e.tensor_scalar(st(SS_ID, 0, 16), st(SS_IG, 0, 16),
                                                        SMA[:, 18:19], -1.0, ALU.mult, ALU.mult),
                    [], vE)
    if debug:
        p.dma("gpsimd", dDB, dbg["dbg_stat"].ap(), STAT[:], waits=[(vE, id_v)])
    p.emit(nc)
    if stop_after <= 4:
        for gd in guards.values():
            gd.__exit__(None, None, None)
        nc.finalize()
        return nc

    p = Prog({"scalar": aE, "vector": vE, "gpsimd": gE})
    for f in range(32):
        if f < 16:
            dst = XT[:, f * 2048:(f + 1) * 2048]
        elif f < 24:
            dst = QT[:, (f - 16) * 2048:(f - 15) * 2048]
        else:
            dst = KT[:, (f - 24) * 2048:(f - 23) * 2048]
        p.dma("sync", dT3 if f < 16 else dT3b, dst,
              zint_d.ap()[:, f * 128:(f + 1) * 128], transpose=True,
              waits=[(dZ, dZ.n)] if f == 0 else [])
    zT_half1 = [(dT3, dT3.n)]
    zT_half2 = [(dT3b, dT3b.n)]

    def zT_chunk(kk):
        if kk < 16:
            return XT[:, kk * 2048:(kk + 1) * 2048]
        if kk < 24:
            return QT[:, (kk - 16) * 2048:(kk - 15) * 2048]
        return KT[:, (kk - 24) * 2048:(kk - 23) * 2048]

    dOs = [sems["dO"], sems["dM"]]
    ost_prev = [0, 0]
    ev_prev = [0] * 4
    x1all_sx2 = dSx.n
    for half in range(2):
        wd_ld = 0
        for kk in range(32):
            wd_ld = p.dma("gpsimd", dV2, VT[:, kk * 512:(kk + 1) * 512],
                          agd_o.ap()[kk * 128:(kk + 1) * 128, half * 512:(half + 1) * 512],
                          waits=([(tE, tE.n)] if (half == 1 and kk == 0) else
                                 ([(cw, 4)] if (half == 0 and kk == 0) else [])))
        for t in range(NTO):
            bank = PB[t % 4]
            oslot = t % 2
            xb = XTI[:, (t % 4) * 1024:(t % 4) * 1024 + 512]
            xv = p.dma("sync" if t % 2 == 0 else "scalar", dX[t % 4], xb,
                       x1_d.ap()[t * 128:(t + 1) * 128, half * 512:(half + 1) * 512],
                       waits=([(vE, ev_prev[t % 4])] if ev_prev[t % 4] else []) +
                             ([(dSx, x1all_sx2)] if (half == 0 and t < 4) else []))
            for kk in range(32):
                ww = []
                if kk == 0:
                    ww = list(zT_half1) + [(dV2, wd_ld)]
                    if t >= 4:
                        ww.append((vE, ev_prev[t % 4]))
                elif kk == 16:
                    ww = list(zT_half2)
                fn = (lambda e, bank=bank, kk=kk, t=t: MM(e, 
                    bank[:], zT_chunk(kk)[:, t * 128:(t + 1) * 128],
                    VT[:, kk * 512:(kk + 1) * 512],
                    start=(kk == 0), stop=(kk == 31)))
                if kk == 31:
                    mm_v = p.inc_on("tensor", fn, ww, tE)
                else:
                    p.on("tensor", fn, ww)
            ev = p.inc_on("vector", (lambda e, bank=bank, xb=xb, t=t, oslot=oslot: e.scalar_tensor_tensor(
                EVAC[:, oslot * 512:(oslot + 1) * 512], bank[:], st(SS_ID, t), xb,
                ALU.mult, ALU.add)),
                [(tE, mm_v), (dX[t % 4], xv)] + ([(dOs[oslot], ost_prev[oslot])] if ost_prev[oslot] else []), vE)
            ev_prev[t % 4] = ev
            ost_prev[oslot] = p.dma("gpsimd", dOs[oslot],
                                    out_d.ap()[t * 128:(t + 1) * 128, half * 512:(half + 1) * 512],
                                    EVAC[:, oslot * 512:(oslot + 1) * 512], waits=[(vE, ev)])
    f0, f1 = dOs[0].n, dOs[1].n
    p.on("sync", lambda e: (e.wait_ge(dOs[0].sem, f0), e.wait_ge(dOs[1].sem, f1))[-1], [])
    if debug:
        p.dma("gpsimd", dDB, dbg["dbg_ctxme"].ap(), ctxme_d.ap(), waits=[(dOs[0], f0), (dOs[1], f1)])
        p.dma("gpsimd", dDB, dbg["dbg_x1"].ap(), x1_d.ap())
        p.dma("gpsimd", dDB, dbg["dbg_xint"].ap(), xint_d.ap())
        p.dma("gpsimd", dDB, dbg["dbg_zint"].ap(), zint_d.ap())
        p.dma("gpsimd", dDB, dbg["dbg_wgT"].ap(), agg_o.ap())
        fdb = dDB.n
        p.on("gpsimd", lambda e: e.wait_ge(dDB.sem, fdb), [])
    p.emit(nc)

    for gd in guards.values():
        gd.__exit__(None, None, None)
    nc.finalize()
    return nc


_NC_CACHE = {}


def _get_nc(debug=False):
    key = bool(debug)
    if key not in _NC_CACHE:
        _NC_CACHE[key] = build_nc(debug=debug)
    return _NC_CACHE[key]


def _build_conv_patterns(conv_w, s):
    w = np.asarray(conv_w, np.float32).reshape(H, KSZ)
    out = np.zeros((H8 * 3 * 128, 128), np.float32)
    i_idx = np.arange(128)[:, None]
    t_idx = np.arange(128)[None, :]
    for hh in range(H8):
        hcol = s * H8 + hh
        for cj in range(3):
            kidx = 128 * (cj - 1) + i_idx - t_idx + 32
            valid = (kidx >= 0) & (kidx < KSZ)
            blk = np.zeros((128, 128), np.float32)
            blk[valid] = w[hcol][kidx[valid]]
            out[(hh * 3 + cj) * 128:(hh * 3 + cj + 1) * 128, :] = blk
    return out


def kernel(L0_attn_hidden_states, L0_attn_Wq, L0_attn_bq, L0_attn_Wk, L0_attn_bk,
           L0_attn_Wv, L0_attn_bv, L0_attn_conv_w, L0_ffn_w_gate, L0_ffn_w_up,
           L0_ffn_w_down, _debug=False):
    x = np.ascontiguousarray(np.asarray(L0_attn_hidden_states, np.float32))
    Wq = np.asarray(L0_attn_Wq, np.float32)
    Wk = np.asarray(L0_attn_Wk, np.float32)
    Wv = np.asarray(L0_attn_Wv, np.float32)
    bq = np.asarray(L0_attn_bq, np.float32)
    bk = np.asarray(L0_attn_bk, np.float32)
    bv = np.asarray(L0_attn_bv, np.float32)
    wg = np.asarray(L0_ffn_w_gate, np.float32)
    wu = np.asarray(L0_ffn_w_up, np.float32)
    wd = np.asarray(L0_ffn_w_down, np.float32)
    eye2 = np.vstack([np.eye(64, dtype=np.float32)] * 2)

    in_maps = []
    for c in range(8):
        b, s = c // 2, c % 2
        cols = slice(s * HD, (s + 1) * HD)
        offs = np.zeros((1, 8), np.int32)
        offs[0, 0] = s * TOK
        offs[0, 1] = (1 - s) * S + s * TOK
        offs[0, 2] = s * 512
        offs[0, 3] = (1 - s) * 512
        in_maps.append({
            "x_in": x[b],
            "x_own": np.ascontiguousarray(x[b][s * TOK:(s + 1) * TOK]),
            "wq_in": np.ascontiguousarray(Wq[:, cols]),
            "wk_in": np.ascontiguousarray(Wk[:, cols]),
            "wv_in": np.ascontiguousarray(Wv[:, cols]),
            "bq_in": np.ascontiguousarray(bq[cols].reshape(4, 128).T),
            "bk_in": np.ascontiguousarray(bk[cols].reshape(4, 128).T),
            "bv_in": np.ascontiguousarray(bv[cols].reshape(1, HD)),
            "convp_in": _build_conv_patterns(L0_attn_conv_w, s),
            "wg_sl": np.ascontiguousarray(wg[:, c * 128:(c + 1) * 128].T),
            "wu_sl": np.ascontiguousarray(wu[:, c * 128:(c + 1) * 128].T),
            "wd_sl": np.ascontiguousarray(wd[:, c * 512:(c + 1) * 512].T),
            "eye2_in": eye2,
            "offs_in": offs,
        })

    nc = _get_nc(debug=_debug)
    res = run_bass_kernel_spmd(nc, in_maps, core_ids=list(range(8)))

    out = np.empty((B, S, DM), np.float32)
    for c in range(8):
        b, s = c // 2, c % 2
        out[b, s * TOK:(s + 1) * TOK, :] = res.results[c]["out_d"]
    if _debug:
        kernel._dbg = res.results
    return out
